# version 17
# baseline (speedup 1.0000x reference)
import numpy as np
import ml_dtypes
import concourse.bass as bass
import concourse.mybir as mybir
from concourse.bass_utils import run_bass_kernel_spmd

F32 = mybir.dt.float32
BF16 = mybir.dt.bfloat16
I32 = mybir.dt.int32
AF = mybir.ActivationFunctionType
ALU = mybir.AluOpType
AXX = mybir.AxisListType.X

P = 128
B = 4
S = 4096
D = 2048
DT = D // P
NH = 16
HD = 128
JQKV = 18
HID = 5632
HT = HID // P
JT = D // P
NSLOT = 16
RT = S // P
SCALE = float(HD) ** -0.5
EPS = 1e-5

KE2 = [256 * (i + 1) for i in range(NSLOT)]

ENGS = ("tensor", "scalar", "vector", "sync", "gpsimd")
E_PRELOAD = False
E_INCR = True


class Sched:
    def __init__(self):
        self.streams = {e: [] for e in ENGS}
        self.cnt = {}
        self.sem_names = []

    def sem(self, name):
        if name not in self.cnt:
            self.cnt[name] = 0
            self.sem_names.append(name)

    def wait(self, eng, sem, val):
        val = min(val, self.cnt[sem])
        if val <= 0:
            return
        self.streams[eng].append(("w", sem, val))

    def inc(self, sem, amt):
        self.cnt[sem] += amt
        return self.cnt[sem]

    def op(self, eng, fn, inc_sem=None, inc_amt=1):
        if inc_sem is not None:
            self.inc(inc_sem, inc_amt)
        self.streams[eng].append(("i", fn, inc_sem, inc_amt))


def build_program():
    nc = bass.Bass("TRN2", target_bir_lowering=False)

    x_att = nc.dram_tensor("x_att", [S, D], F32, kind="ExternalInput").ap()
    xq = nc.dram_tensor("xq", [NSLOT, P, D], F32, kind="ExternalInput").ap()
    wqkv = nc.dram_tensor("wqkv", [D, 2304], F32, kind="ExternalInput").ap()
    bqkv = nc.dram_tensor("bqkv", [P, JQKV], F32, kind="ExternalInput").ap()
    wp = nc.dram_tensor("wp", [D, D], F32, kind="ExternalInput").ap()
    bproj = nc.dram_tensor("bproj", [P, D], F32, kind="ExternalInput").ap()
    wg = nc.dram_tensor("wg", [HID, D], I32, kind="ExternalInput").ap()
    wu = nc.dram_tensor("wu", [HID, D], I32, kind="ExternalInput").ap()
    wd = nc.dram_tensor("wd", [D, HID], I32, kind="ExternalInput").ap()
    sg = nc.dram_tensor("sg", [P, HT], F32, kind="ExternalInput").ap()
    su = nc.dram_tensor("su", [P, HT], F32, kind="ExternalInput").ap()
    sd = nc.dram_tensor("sd", [P, JT], F32, kind="ExternalInput").ap()
    cmask_h = nc.dram_tensor("cmask", [P, 256], BF16, kind="ExternalInput").ap()
    out_h = nc.dram_tensor("out", [NSLOT, P, D], F32, kind="ExternalOutput").ap()

    qt_hbm = nc.dram_tensor("qt_hbm", [NH, P, 2048], BF16, kind="Internal").ap()
    x1_hbm = nc.dram_tensor("x1_hbm", [NSLOT, P, D], F32, kind="Internal").ap()
    act_hbm = nc.dram_tensor("act_hbm", [HT, P, 2048], BF16, kind="Internal").ap()
    wdt_hbm = nc.dram_tensor("wdt_hbm", [JT, P, HT * P], BF16, kind="Internal").ap()

    from contextlib import ExitStack
    ctx = ExitStack()
    e = ctx.enter_context
    bqkv_sb = e(nc.sbuf_tensor("bqkv_sb", [P, JQKV], F32))
    bproj_sb = e(nc.sbuf_tensor("bproj_sb", [P, D], F32))
    sg_sb = e(nc.sbuf_tensor("sg_sb", [P, HT], F32))
    su_sb = e(nc.sbuf_tensor("su_sb", [P, HT], F32))
    sd_sb = e(nc.sbuf_tensor("sd_sb", [P, JT], F32))
    sgb_sb = e(nc.sbuf_tensor("sgb_sb", [P, HT], F32))
    sub_sb = e(nc.sbuf_tensor("sub_sb", [P, HT], F32))
    sdb_sb = e(nc.sbuf_tensor("sdb_sb", [P, JT], F32))
    bnb = [e(nc.sbuf_tensor(f"bnb{i}", [P, 24], F32)) for i in range(4)]
    mv = [e(nc.sbuf_tensor(f"mv{i}", [P, 2], F32)) for i in range(4)]
    vpe = [e(nc.sbuf_tensor(f"vpe{i}", [P, 1], F32)) for i in range(4)]
    lnv = [e(nc.sbuf_tensor(f"lnv{i}", [P, 1], F32)) for i in range(4)]
    rstd = [e(nc.sbuf_tensor(f"rstd{i}", [P, 1], F32)) for i in range(4)]
    nmr = [e(nc.sbuf_tensor(f"nmr{i}", [P, 1], F32)) for i in range(4)]
    smc = [e(nc.sbuf_tensor(f"smc{i}", [P, 8], F32)) for i in range(3)]
    zv = [e(nc.sbuf_tensor(f"zv{i}", [P, 1], F32)) for i in range(3)]
    rcv = [e(nc.sbuf_tensor(f"rcv{i}", [P, 1], F32)) for i in range(3)]

    SEG = {
        "wqkv_sb": ("A", [P, DT, 2304], BF16),
        "xt0": ("AC", [P, D], F32), "xt1": ("AC", [P, D], F32),
        "xt2": ("AC", [P, D], F32),
        "lnt0": ("AC", [P, D], BF16), "lnt1": ("AC", [P, D], BF16),
        "lnt2": ("AC", [P, D], BF16),
        "xlk0": ("A", [P, DT, 512], BF16), "xlk1": ("A", [P, DT, 512], BF16),
        "kt_sb": ("AB", [P, S], BF16),
        "vt_sb": ("A", [P, S], BF16),
        "v_sb": ("AB", [P, RT, P], BF16),
        "qev0": ("A", [P, 512], BF16), "qev1": ("A", [P, 512], BF16),
        "qev2": ("A", [P, 512], BF16), "qev3": ("A", [P, 512], BF16),
        "cmask_sb": ("AB", [P, 256], BF16),
        "qth0": ("B", [P, 2048], BF16), "qth1": ("B", [P, 2048], BF16),
        "prb0": ("B", [P, 4096], BF16), "prb1": ("B", [P, 4096], BF16),
        "prb2": ("B", [P, 4096], BF16),
        "pT0": ("B", [P, 32, 256], BF16), "pT1": ("B", [P, 32, 256], BF16),
        "pT2": ("B", [P, 32, 256], BF16),
        "attnT": ("BC", [P, NH, 2048], BF16),
        "wpb0": ("C", [P, D], BF16), "wpb1": ("C", [P, D], BF16),
        "x1t0": ("C", [P, D], F32), "x1t1": ("C", [P, D], F32),
        "x1t2": ("C", [P, D], F32),
        "xl2T": ("CD", [P, DT, 2048], BF16),
        "wgr0": ("D", [P, D], BF16), "wgr1": ("D", [P, D], BF16),
        "wur0": ("D", [P, D], BF16), "wur1": ("D", [P, D], BF16),
        "wgd0": ("D", [P, D], BF16), "wgd1": ("D", [P, D], BF16),
        "wud0": ("D", [P, D], BF16), "wud1": ("D", [P, D], BF16),
        "wgT0": ("D", [P, DT, P], BF16), "wgT1": ("D", [P, DT, P], BF16),
        "wuT0": ("D", [P, DT, P], BF16), "wuT1": ("D", [P, DT, P], BF16),
        "sil0": ("D", [P, 512], BF16), "sil1": ("D", [P, 512], BF16),
        "sil2": ("D", [P, 512], BF16), "sil3": ("D", [P, 512], BF16),
        "actt0": ("D", [P, 2048], BF16), "actt1": ("D", [P, 2048], BF16),
        "wdr": ("D", [P, HID], BF16),
        "wdd": ("D", [P, HID], BF16),
        "wdTt": ("D", [P, HT, P], BF16),
        "actH": ("E", [P, HT, 1024], BF16),
        "out_sb": ("E", [P, 8, D], F32),
        "wdTb0": ("E", [P, HT, P], BF16), "wdTb1": ("E", [P, HT, P], BF16),
        "dTt0": ("E", [P, 1024], BF16), "dTt1": ("E", [P, 1024], BF16),
        "dnt0": ("E", [P, 8, P], BF16), "dnt1": ("E", [P, 8, P], BF16),
    }

    def seg_bytes(shape, dtype):
        n = 1
        for s_ in shape[1:]:
            n *= s_
        return n * mybir.dt.size(dtype)

    order = sorted(SEG.items(), key=lambda kv: -seg_bytes(kv[1][1], kv[1][2]))
    placed = []
    offsets = {}
    for name, (phases, shape, dtype) in order:
        sz = (seg_bytes(shape, dtype) + 63) // 64 * 64
        cands = sorted(set([0] + [pe for (pp, po, pe) in placed if set(pp) & set(phases)]))
        for off in cands:
            end = off + sz
            if all((not (set(pp) & set(phases))) or end <= po or off >= pe
                   for (pp, po, pe) in placed):
                placed.append((phases, off, end))
                offsets[name] = off
                break
        else:
            raise RuntimeError("no fit " + name)
    arena_sz = max(pe for (_, _, pe) in placed)
    assert arena_sz <= 200 * 1024, f"arena {arena_sz}"
    arena = e(nc.sbuf_tensor("arena", [P, arena_sz], mybir.dt.uint8))
    arena_addr = nc.lookup_mloc(arena).addr

    T = {}
    for name, (phases, shape, dtype) in SEG.items():
        h = nc.virtual_sbuf_tensor(name, shape, dtype)
        nc._tensor(h.name, h.shape, h.dtype, type="SB")
        mloc = nc.lookup_mloc(h)
        mloc.allocated = True
        mloc.addr = arena_addr + offsets[name]
        T[name] = h.ap()

    psum = e(nc.psum_tensor("psum", [P, 4096], F32))

    sch = Sched()
    for s_ in ("ld_cst", "ld_x", "ld_qh", "ld_wp0", "ld_wp1", "ld_wg", "ld_wu",
               "ld_wd", "ld_wdt",
               "ld_act", "ld_out", "st_q", "st_x1", "st_act", "st_wdt", "st_out",
               "v_stat", "v_nmr", "v_dg", "v_mx", "v_nm", "v_act", "v_pj", "v_oadd",
               "s_rstd", "s_aff", "s_ev", "s_xp", "s_pe", "s_sil", "s_dq", "s_dqd",
               "s_de", "s_kv", "s_qe",
               "t_mm", "t_qkv", "t_pv", "t_pj", "t_pjd", "t_gu", "t_dn",
               "y_tp", "ph"):
        sch.sem(s_)

    def dma(eng, dst, src, sem, ser=True, **kw):
        if ser and sch.cnt[sem] > 0:
            sch.wait(eng, sem, sch.cnt[sem])
        def fn(E, dst=dst, src=src, kw=kw):
            return E.dma_start(dst, src, **kw)
        sch.op(eng, fn, sem, 16)

    def tpose(dst, src):
        if sch.cnt["y_tp"] > 0:
            sch.wait("sync", "y_tp", sch.cnt["y_tp"])
        def fn(E, dst=dst, src=src):
            return E.dma_start(dst, src, transpose=True)
        sch.op("sync", fn, "y_tp", 16)
        return sch.cnt["y_tp"]

    def act(eng, out, in_, func, sem=None, **kw):
        def fn(E, out=out, in_=in_, func=func, kw=kw):
            return E.activation(out, in_, func, **kw)
        sch.op(eng, fn, sem, 1)

    def mm(out, lhsT, rhs, start, stop, sem=None):
        def fn(E, out=out, lhsT=lhsT, rhs=rhs, start=start, stop=stop):
            return E.matmul(out, lhsT, rhs, start=start, stop=stop)
        sch.op("tensor", fn, sem, 1)

    def vop(meth, *args, sem=None, **kw):
        def fn(E, meth=meth, args=args, kw=kw):
            return getattr(E, meth)(*args, **kw)
        sch.op("vector", fn, sem, 1)

    def phase_barrier():
        for eng in ENGS:
            def fn2(E):
                return E.nop()
            sch.op(eng, fn2, "ph", 1)
        tgt = sch.cnt["ph"]
        for eng in ENGS:
            sch.wait(eng, "ph", tgt)

    dma("gpsimd", bqkv_sb[:], bqkv[:], "ld_cst")
    dma("gpsimd", bproj_sb[:], bproj[:], "ld_cst")
    dma("gpsimd", sg_sb[:], sg[:], "ld_cst")
    dma("gpsimd", su_sb[:], su[:], "ld_cst")
    dma("gpsimd", sd_sb[:], sd[:], "ld_cst")
    dma("gpsimd", T["cmask_sb"][:], cmask_h[:], "ld_cst")
    for t in range(DT):
        dma("gpsimd", T["wqkv_sb"][:, t, :], wqkv[t * P:(t + 1) * P, :], "ld_cst")
    n_cst = sch.cnt["ld_cst"]
    sch.wait("vector", "ld_cst", n_cst)
    vop("tensor_scalar_mul", sgb_sb[:], sg_sb[:], -128.0, sem="v_stat")
    vop("tensor_scalar_mul", sub_sb[:], su_sb[:], -128.0, sem="v_stat")
    vop("tensor_scalar_mul", sdb_sb[:], sd_sb[:], -128.0, sem="v_stat")

    aff_h, rstd_h, nmr_h, stat_h, lntp_h = {}, {}, {}, {}, {}

    def ln_front(r, src_ap, gate):
        pa = r % 4
        for sem_, val_ in gate:
            sch.wait("vector", sem_, val_)
        if r - 4 in aff_h:
            sch.wait("vector", "s_aff", aff_h[r - 4])
        for i in range(4):
            vop("bn_stats", bnb[pa][:, 6 * i:6 * (i + 1)],
                src_ap[:, 512 * i:512 * (i + 1)])
        vop("drain")
        vop("bn_aggr", mv[pa][:], bnb[pa][:])
        vop("drain")
        if r - 4 in rstd_h:
            sch.wait("vector", "s_rstd", rstd_h[r - 4])
        vop("tensor_scalar_add", vpe[pa][:], mv[pa][:, 1:2], EPS, sem="v_stat")
        stat_h[r] = sch.cnt["v_stat"]
        sch.wait("scalar", "v_stat", stat_h[r])
        if r - 4 in nmr_h:
            sch.wait("scalar", "v_nmr", nmr_h[r - 4])
        act("scalar", lnv[pa][:], vpe[pa][:], AF.Ln)
        act("scalar", rstd[pa][:], lnv[pa][:], AF.Exp, sem="s_rstd", scale=-0.5)
        rstd_h[r] = sch.cnt["s_rstd"]
        sch.wait("vector", "s_rstd", rstd_h[r])
        def f_nmr(E, o=nmr[pa][:], i0=mv[pa][:, 0:1], s1=rstd[pa][:]):
            return E.tensor_scalar(o, i0, s1, -1.0, ALU.mult, ALU.mult)
        sch.op("vector", f_nmr, "v_nmr", 1)
        nmr_h[r] = sch.cnt["v_nmr"]

    def ln_back(r, src_ap, dst_fn):
        pa = r % 4
        lnt = T[f"lnt{r % 3}"]
        sch.wait("scalar", "v_nmr", nmr_h[r])
        if r - 3 in lntp_h:
            sch.wait("scalar", "y_tp", lntp_h[r - 3])
        act("scalar", lnt[:], src_ap[:], AF.Identity, sem="s_aff",
            scale=rstd[pa][:], bias=nmr[pa][:])
        aff_h[r] = sch.cnt["s_aff"]
        sch.wait("sync", "s_aff", aff_h[r])
        lntp_h[r] = dst_fn(lnt)

    kv_ev = 0
    q_ev = 0
    qkv_chunk_h = {}

    def emit_chunk(gch):
        nonlocal kv_ev, q_ev
        is_q = gch >= 8
        xlk = T[f"xlk{gch % 2}"]
        sch.wait("tensor", "y_tp", lntp_h[4 * gch + 3])
        sch.wait("tensor", "ld_cst", n_cst)
        if not is_q:
            c = gch
            for ji, jt in enumerate((16, 17)):
                off = (c % 2) * 1024 + ji * 512
                bank = psum[:, off:off + 512]
                if kv_ev + 1 > 4:
                    sch.wait("tensor", "s_kv", kv_ev - 3)
                for d in range(DT):
                    mm(bank, T["wqkv_sb"][:, d, jt * P:(jt + 1) * P], xlk[:, d, :],
                       d == 0, d == DT - 1, sem="t_qkv" if d == DT - 1 else None)
                dst = (T["kt_sb"] if jt == 16 else T["vt_sb"])[:, c * 512:(c + 1) * 512]
                sch.wait("scalar", "t_qkv", sch.cnt["t_qkv"])
                act("scalar", dst, bank, AF.Identity, sem="s_kv",
                    bias=bqkv_sb[:, jt:jt + 1])
                kv_ev += 1
        else:
            qc = gch - 8
            for jt in range(16):
                n = qc * 16 + jt
                off = 2048 + (n % 4) * 512
                bank = psum[:, off:off + 512]
                if n >= 4:
                    sch.wait("tensor", "s_qe", n - 3)
                for d in range(DT):
                    mm(bank, T["wqkv_sb"][:, d, jt * P:(jt + 1) * P], xlk[:, d, :],
                       d == 0, d == DT - 1, sem="t_qkv" if d == DT - 1 else None)
                qb = T[f"qev{n % 4}"]
                sch.wait("scalar", "t_qkv", sch.cnt["t_qkv"])
                if n >= 4:
                    sch.wait("scalar", "st_q", 16 * (n - 3))
                act("scalar", qb[:], bank, AF.Identity, sem="s_qe",
                    bias=bqkv_sb[:, jt:jt + 1])
                q_ev += 1
                sch.wait("gpsimd", "s_qe", q_ev)
                dma("gpsimd", qt_hbm[jt, :, qc * 512:(qc + 1) * 512], qb[:], "st_q")
        qkv_chunk_h[gch] = sch.cnt["t_qkv"]

    def a_back(rb):
        xtb = T[f"xt{rb % 3}"]

        def dst_fn(lnt, cbuf=(rb // 4) % 2, tt=rb % 4, gch=rb // 4):
            if gch - 2 in qkv_chunk_h:
                sch.wait("sync", "t_qkv", qkv_chunk_h[gch - 2])
            return tpose(T[f"xlk{cbuf}"][:, :, tt * P:(tt + 1) * P], lnt[:])

        ln_back(rb, xtb, dst_fn)
        if rb % 4 == 3:
            emit_chunk(rb // 4)

    for r in range(RT + NSLOT):
        is_q = r >= RT
        xt = T[f"xt{r % 3}"]
        if r - 3 in aff_h:
            sch.wait("gpsimd", "s_aff", aff_h[r - 3])
            sch.wait("gpsimd", "v_stat", stat_h[r - 3])
        src = x_att[(r * P):(r + 1) * P, :] if not is_q else xq[r - RT, :, :]
        dma("gpsimd", xt[:], src, "ld_x")
        ln_front(r, xt, [("ld_x", sch.cnt["ld_x"])])
        if r >= 2:
            a_back(r - 2)
    a_back(RT + NSLOT - 2)
    a_back(RT + NSLOT - 1)
    sch.wait("sync", "s_kv", kv_ev)
    v_tp_h = tpose(T["v_sb"][:], T["vt_sb"][:])
    sch.wait("gpsimd", "st_q", 16 * 64)
    phase_barrier()

    NSB = 6
    tenant = [0] * NSB
    nb = 0
    tp_h = {}
    xp_h = {}
    nm_h = {}
    mm_after = {}
    head_mm = {}
    pv_cnt = {}
    pe_cnt = {}
    slot_plan = {}
    PVB = [3072, 3584]

    def plan_slot(u):
        nonlocal nb
        i_loc = 2 * ((u // 2) % 8) + (u & 1)
        E = KE2[i_loc]
        fds = [512] * (E // 512) + ([256] if E % 512 else [])
        segs = []
        koff = 0
        k = 0
        while k < len(fds):
            L = min(4, NSB - nb, len(fds) - k)
            segs.append((nb, fds[k:k + L], koff))
            koff += sum(fds[k:k + L])
            nb = (nb + L) % NSB
            k += L
        sb, sfds, _ = segs[-1]
        dbo = sb * 512 + sum(sfds) - 256
        return [segs, dbo, E, 0]

    def b_slot(u, h):
        segs, dbo, E, _ = slot_plan[u]
        pa = u % 3
        prb = T[f"prb{pa}"]
        i_loc = 2 * ((u // 2) % 8) + (u & 1)
        if u - 3 in tp_h:
            sch.wait("scalar", "y_tp", tp_h[u - 3])
            sch.wait("scalar", "v_nm", nm_h[u - 3])
        nseg = 0
        for si, (b0, sfds, koff) in enumerate(segs):
            last = si == len(segs) - 1
            for ci, fd in enumerate(sfds):
                b = b0 + ci
                if tenant[b] > 0:
                    sch.wait("tensor", "s_xp", tenant[b])
                k0 = koff + sum(sfds[:ci])
                mm(psum[:, b * 512:b * 512 + fd],
                   T[f"qth{h % 2}"][:, i_loc * P:(i_loc + 1) * P],
                   T["kt_sb"][:, k0:k0 + fd], True, True, sem="t_mm")
            sch.wait("scalar", "t_mm", sch.cnt["t_mm"])
            fd = sum(sfds) - (256 if last else 0)
            if fd > 0:
                act("scalar", prb[:, koff:koff + fd],
                    psum[:, b0 * 512:b0 * 512 + fd],
                    AF.Exp, sem="s_xp", scale=SCALE,
                    accum_out=smc[pa][:, nseg:nseg + 1])
                nseg += 1
            if last:
                act("scalar", prb[:, E - 256:E], psum[:, dbo:dbo + 256],
                    AF.Exp, sem="s_xp", scale=SCALE)
            for ci in range(len(sfds)):
                tenant[b0 + ci] = sch.cnt["s_xp"]
        xp_h[u] = sch.cnt["s_xp"]
        mm_after[u] = sch.cnt["t_mm"]
        slot_plan[u][3] = nseg

    def b_post(u):
        segs, dbo, E, nseg = slot_plan[u]
        pa = u % 3
        prb = T[f"prb{pa}"]
        sch.wait("vector", "s_xp", xp_h[u])

        def f_stt(Eg, o=prb[:, E - 256:E], i1=T["cmask_sb"][:],
                  acc=smc[pa][:, nseg:nseg + 1]):
            return Eg.scalar_tensor_tensor(out=o, in0=o, scalar=1.0, in1=i1,
                                           op0=ALU.mult, op1=ALU.mult,
                                           accum_out=acc)
        sch.op("vector", f_stt, "v_dg", 1)
        vop("drain")
        vop("reduce_sum", zv[pa][:], smc[pa][:, 0:nseg + 1], AXX)
        vop("drain")
        vop("reciprocal", rcv[pa][:], zv[pa][:])
        vop("drain")
        if u % 2 == 0:
            vop("tensor_scalar_mul", prb[:, 0:E], prb[:, 0:E], rcv[pa][:])
            vop("memset", prb[:, E:E + 256], 0.0, sem="v_nm")
        else:
            vop("tensor_scalar_mul", prb[:, 0:E], prb[:, 0:E], rcv[pa][:],
                sem="v_nm")
        nm_h[u] = sch.cnt["v_nm"]

    def b_tpose(u):
        pa = u % 3
        j = u // 2
        E = slot_plan[u][2]
        Ep = E + 256 if u % 2 == 0 else E
        sch.wait("sync", "v_nm", nm_h[u])
        if j - 3 in pv_cnt:
            sch.wait("sync", "t_pv", pv_cnt[j - 3])
        tp_h[u] = tpose(T[f"pT{j % 3}"][:, 0:Ep // 128, (u & 1) * P:((u & 1) + 1) * P],
                        T[f"prb{pa}"][:, 0:Ep])

    def emit_pv(j):
        pr_ = j % 8
        nkt = 4 * (pr_ + 1)
        sch.wait("tensor", "y_tp", tp_h[2 * j + 1])
        if j - 2 in pe_cnt:
            sch.wait("tensor", "s_pe", pe_cnt[j - 2])
        pvb = psum[:, PVB[j % 2]:PVB[j % 2] + 256]
        for kt in range(nkt):
            mm(pvb, T["v_sb"][:, kt, :], T[f"pT{j % 3}"][:, kt, :],
               kt == 0, kt == nkt - 1, sem="t_pv" if kt == nkt - 1 else None)
        pv_cnt[j] = sch.cnt["t_pv"]

    def emit_pv_evac(j):
        h_, pr_ = j // 8, j % 8
        sch.wait("scalar", "t_pv", pv_cnt[j])
        pvb = psum[:, PVB[j % 2]:PVB[j % 2] + 256]
        act("scalar", T["attnT"][:, h_, pr_ * 256:(pr_ + 1) * 256], pvb,
            AF.Identity, sem="s_pe")
        pe_cnt[j] = sch.cnt["s_pe"]

    for idx in range(128):
        h, pr = idx // 8, idx % 8
        if pr == 0:
            if h >= 2:
                sch.wait("gpsimd", "t_mm", head_mm[h - 2])
            dma("gpsimd", T[f"qth{h % 2}"][:], qt_hbm[h, :, :], "ld_qh")
        if idx >= 2:
            emit_pv(idx - 2)
        if idx >= 3:
            emit_pv_evac(idx - 3)
        for s01 in range(2):
            u = 2 * idx + s01
            slot_plan[u] = plan_slot(u)
            sch.wait("tensor", "ld_qh", 16 * (h + 1))
            b_slot(u, h)
            b_post(u)
            b_tpose(u)
        if pr == 7:
            head_mm[h] = sch.cnt["t_mm"]
    emit_pv(126)
    emit_pv_evac(125)
    emit_pv(127)
    emit_pv_evac(126)
    emit_pv_evac(127)
    phase_barrier()

    pjd_h = []
    for sp in range(8):
        slots = (2 * sp, 2 * sp + 1)
        for sl01, sl in enumerate(slots):
            if sp >= 1:
                sch.wait("gpsimd", "v_pj", 2 * sp)
            dma("gpsimd", T[f"xt{sl01}"][:], xq[sl, :, :], "ld_x")
        for dd in range(DT):
            gdd = sp * DT + dd
            if gdd >= 2:
                sch.wait("gpsimd", "t_pjd", pjd_h[gdd - 2])
            dma("gpsimd", T[f"wpb{dd % 2}"][:], wp[dd * P:(dd + 1) * P, :],
                f"ld_wp{dd % 2}")
            sch.wait("tensor", f"ld_wp{dd % 2}", sch.cnt[f"ld_wp{dd % 2}"])
            if sp >= 1 and dd == 0:
                sch.wait("tensor", "v_pj", 2 * sp)
            for sl01 in range(2):
                for jc in range(4):
                    mm(psum[:, sl01 * 2048 + jc * 512: sl01 * 2048 + jc * 512 + 512],
                       T["attnT"][:, dd, slots[sl01] * P:(slots[sl01] + 1) * P],
                       T[f"wpb{dd % 2}"][:, jc * 512:(jc + 1) * 512],
                       dd == 0, dd == DT - 1,
                       sem="t_pjd" if (sl01 == 1 and jc == 3) else None)
            pjd_h.append(sch.cnt["t_pjd"])
        for sl01, sl in enumerate(slots):
            r = RT + NSLOT + sl
            x1t = T[f"x1t{sl % 3}"]
            sch.wait("vector", "t_pjd", DT * (sp + 1))
            sch.wait("vector", "ld_x", sch.cnt["ld_x"])
            if sl - 3 >= 0:
                sch.wait("vector", "st_x1", 16 * (sl - 2))
                if r - 3 in aff_h:
                    sch.wait("vector", "s_aff", aff_h[r - 3])
            vop("tensor_add", x1t[:], psum[:, sl01 * 2048:sl01 * 2048 + 2048],
                T[f"xt{sl01}"][:])
            vop("tensor_add", x1t[:], x1t[:], bproj_sb[:], sem="v_pj")
            sch.wait("gpsimd", "v_pj", sch.cnt["v_pj"])
            dma("gpsimd", x1_hbm[sl, :, :], x1t[:], "st_x1")
            ln_front(r, x1t, [("v_pj", sch.cnt["v_pj"])])
            if sl >= 2:
                rb = r - 2

                def dst2(lnt, sl2=sl - 2):
                    return tpose(T["xl2T"][:, :, sl2 * P:(sl2 + 1) * P], lnt[:])

                ln_back(rb, T[f"x1t{(sl - 2) % 3}"], dst2)
    for sl2 in (NSLOT - 2, NSLOT - 1):
        rb = RT + NSLOT + sl2

        def dst3(lnt, sl2=sl2):
            return tpose(T["xl2T"][:, :, sl2 * P:(sl2 + 1) * P], lnt[:])

        ln_back(rb, T[f"x1t{sl2 % 3}"], dst3)
    sch.wait("gpsimd", "st_x1", 16 * 16)
    phase_barrier()

    gu_grp = 0
    gutp_h = {}
    wdtp_h = {}
    dq_h = {}

    def d_load(ht):
        pa = ht % 2
        if ht >= 2:
            sch.wait("gpsimd", "s_dq", dq_h.get(ht - 2, 0))
        dma("gpsimd", T[f"wgr{pa}"][:], wg[ht * P:(ht + 1) * P, :], "ld_wg")
        dma("gpsimd", T[f"wur{pa}"][:], wu[ht * P:(ht + 1) * P, :], "ld_wu")

    def d_prep(ht):
        pa = ht % 2
        sch.wait("scalar", "ld_wg", 16 * (ht + 1))
        sch.wait("scalar", "ld_wu", 16 * (ht + 1))
        sch.wait("scalar", "v_stat", 3)
        if ht - 2 in gutp_h:
            sch.wait("scalar", "y_tp", gutp_h[ht - 2])
        act("scalar", T[f"wgd{pa}"][:], T[f"wgr{pa}"][:], AF.Identity, sem="s_dq",
            scale=sg_sb[:, ht:ht + 1], bias=sgb_sb[:, ht:ht + 1])
        act("scalar", T[f"wud{pa}"][:], T[f"wur{pa}"][:], AF.Identity, sem="s_dq",
            scale=su_sb[:, ht:ht + 1], bias=sub_sb[:, ht:ht + 1])
        dq_h[ht] = sch.cnt["s_dq"]
        sch.wait("sync", "s_dq", dq_h[ht])
        if ht >= 2:
            sch.wait("sync", "t_gu", 4 * (ht - 1))
        tpose(T[f"wgT{pa}"][:], T[f"wgd{pa}"][:])
        gutp_h[ht] = tpose(T[f"wuT{pa}"][:], T[f"wud{pa}"][:])

    d_load(0)
    d_prep(0)
    for ht in range(HT):
        pa = ht % 2
        if ht + 1 < HT:
            d_load(ht + 1)
            d_prep(ht + 1)
        for hh in range(2):
            goff = (gu_grp % 2) * 2048
            sch.wait("tensor", "y_tp", gutp_h[ht])
            if gu_grp >= 2:
                sch.wait("tensor", "s_sil", 2 * (gu_grp - 1))
                sch.wait("tensor", "v_act", 2 * (gu_grp - 1))
            for d in range(DT):
                for ci, cc in enumerate((2 * hh, 2 * hh + 1)):
                    mm(psum[:, goff + ci * 512:goff + ci * 512 + 512],
                       T[f"wgT{pa}"][:, d, :], T["xl2T"][:, d, cc * 512:(cc + 1) * 512],
                       d == 0, d == DT - 1)
                for ci, cc in enumerate((2 * hh, 2 * hh + 1)):
                    mm(psum[:, goff + 1024 + ci * 512:goff + 1024 + ci * 512 + 512],
                       T[f"wuT{pa}"][:, d, :], T["xl2T"][:, d, cc * 512:(cc + 1) * 512],
                       d == 0, d == DT - 1,
                       sem="t_gu" if (d == DT - 1 and ci == 1) else None)
            for ci, cc in enumerate((2 * hh, 2 * hh + 1)):
                sch.wait("scalar", "t_gu", sch.cnt["t_gu"])
                if gu_grp >= 2:
                    sch.wait("scalar", "v_act", 2 * (gu_grp - 1))
                act("scalar", T[f"sil{cc}"][:],
                    psum[:, goff + ci * 512:goff + ci * 512 + 512],
                    AF.Silu, sem="s_sil")
                sch.wait("vector", "s_sil", sch.cnt["s_sil"])
                if ht >= 2 and hh == 0 and ci == 0:
                    sch.wait("vector", "st_act", 16 * (ht - 1))
                def f_mul(E, o=T[f"actt{pa}"][:, cc * 512:(cc + 1) * 512],
                          i0=T[f"sil{cc}"][:],
                          i1=psum[:, goff + 1024 + ci * 512:goff + 1024 + ci * 512 + 512]):
                    return E.tensor_mul(o, i0, i1)
                sch.op("vector", f_mul, "v_act", 1)
            gu_grp += 1
        sch.wait("gpsimd", "v_act", sch.cnt["v_act"])
        dma("gpsimd", act_hbm[ht, :, :], T[f"actt{pa}"][:], "st_act")
        if ht >= 4 and ht % 2 == 0 and (ht - 4) // 2 < JT:
            jt = (ht - 4) // 2
            if jt >= 1:
                sch.wait("gpsimd", "s_dqd", jt)
            dma("gpsimd", T["wdr"][:], wd[jt * P:(jt + 1) * P, :], "ld_wd",
                max_dma_last_dim=8192)
            sch.wait("scalar", "ld_wd", 16 * (jt + 1))
            if jt - 1 in wdtp_h:
                sch.wait("scalar", "y_tp", wdtp_h[jt - 1])
            act("scalar", T["wdd"][:], T["wdr"][:], AF.Identity, sem="s_dqd",
                scale=sd_sb[:, jt:jt + 1], bias=sdb_sb[:, jt:jt + 1])
            sch.wait("sync", "s_dqd", sch.cnt["s_dqd"])
            if jt >= 1:
                sch.wait("sync", "st_wdt", 16 * jt)
            wdtp_h[jt] = tpose(T["wdTt"][:], T["wdd"][:])
            sch.wait("gpsimd", "y_tp", wdtp_h[jt])
            dma("gpsimd", wdt_hbm[jt, :, :], T["wdTt"][:], "st_wdt")
    sch.wait("gpsimd", "st_act", 16 * HT)
    sch.wait("gpsimd", "st_wdt", 16 * JT)
    phase_barrier()

    detp_h = {}
    for hh in range(2):
        if hh == 1:
            sch.wait("gpsimd", "st_out", 16 * 8)
            sch.wait("gpsimd", "v_oadd", sch.cnt["v_oadd"])
        if E_PRELOAD:
            dma("gpsimd", T["wdTb0"][:], wdt_hbm[0, :, :], "ld_wdt")
        for ht in range(HT):
            dma("gpsimd", T["actH"][:, ht, :],
                act_hbm[ht, :, hh * 1024:(hh + 1) * 1024], "ld_act")
        for sl in range(8):
            dma("gpsimd", T["out_sb"][:, sl, :], x1_hbm[hh * 8 + sl, :, :], "ld_out")
        for jt in range(JT):
            g = hh * JT + jt
            pa = g % 2
            if jt > 0 or not E_PRELOAD:
                if g >= 2:
                    sch.wait("gpsimd", "t_dn", sch.cnt["t_dn"])
                dma("gpsimd", T[f"wdTb{pa}"][:], wdt_hbm[jt, :, :], "ld_wdt")
            sch.wait("tensor", "ld_wdt", sch.cnt["ld_wdt"])
            if jt > 0 or not E_INCR:
                sch.wait("tensor", "ld_act", 16 * (hh + 1) * HT)
            if g >= 2:
                sch.wait("tensor", "s_de", g - 1)
            for ht in range(HT):
                if jt == 0 and E_INCR:
                    sch.wait("tensor", "ld_act", 16 * (hh * HT + ht + 1))
                for ci in range(2):
                    mm(psum[:, pa * 1024 + ci * 512:pa * 1024 + ci * 512 + 512],
                       T[f"wdTb{pa}"][:, ht, :],
                       T["actH"][:, ht, ci * 512:(ci + 1) * 512],
                       ht == 0, ht == HT - 1,
                       sem="t_dn" if (ht == HT - 1 and ci == 1) else None)
            sch.wait("scalar", "t_dn", sch.cnt["t_dn"])
            if g - 2 in detp_h:
                sch.wait("scalar", "y_tp", detp_h[g - 2])
            act("scalar", T[f"dTt{pa}"][:], psum[:, pa * 1024:pa * 1024 + 1024],
                AF.Identity, sem="s_de")
            sch.wait("sync", "s_de", sch.cnt["s_de"])
            if g >= 2:
                sch.wait("sync", "v_oadd", 8 * (g - 1))
            detp_h[g] = tpose(T[f"dnt{pa}"][:], T[f"dTt{pa}"][:])
            sch.wait("vector", "y_tp", detp_h[g])
            sch.wait("vector", "ld_out", 16 * (hh + 1) * 8)
            for nt in range(8):
                o = T["out_sb"][:, nt, jt * P:jt * P + P]
                def f_add(E, o=o, i1=T[f"dnt{pa}"][:, nt, :]):
                    return E.tensor_add(o, o, i1)
                sch.op("vector", f_add, "v_oadd", 1)
        sch.wait("gpsimd", "v_oadd", sch.cnt["v_oadd"])
        for sl in range(8):
            dma("gpsimd", out_h[hh * 8 + sl, :, :], T["out_sb"][:, sl, :], "st_out")
    sch.wait("gpsimd", "st_out", 16 * 16)

    global _LAST_SCHED
    _LAST_SCHED = sch
    sems = {}
    for name in sch.sem_names:
        sems[name] = e(nc.semaphore(name))
    block = e(nc.Block())

    def emit(E, stream):
        for item in stream:
            if item[0] == "w":
                E.wait_ge(sems[item[1]], item[2])
            else:
                _, fn, inc_sem, inc_amt = item
                inst = fn(E)
                if inc_sem is not None and inst is not None:
                    inst.then_inc(sems[inc_sem], inc_amt)

    @block.tensor
    def _(t):
        emit(t, sch.streams["tensor"])

    @block.scalar
    def _(sc):
        emit(sc, sch.streams["scalar"])

    @block.vector
    def _(v):
        emit(v, sch.streams["vector"])

    @block.sync
    def _(sy):
        emit(sy, sch.streams["sync"])

    @block.gpsimd
    def _(g):
        emit(g, sch.streams["gpsimd"])

    ctx.close()
    return nc


_LAST_SCHED = None
_NC_CACHE = None


def _get_nc():
    global _NC_CACHE
    if _NC_CACHE is None:
        _NC_CACHE = build_program()
    return _NC_CACHE


def kernel(L0_attn_x, L0_attn_W_qkv, L0_attn_b_qkv, L0_attn_W_proj, L0_attn_b_proj,
           L0_ffn_wq_gate, L0_ffn_s_gate, L0_ffn_wq_up, L0_ffn_s_up,
           L0_ffn_wq_down, L0_ffn_s_down):
    x = np.ascontiguousarray(np.asarray(L0_attn_x, dtype=np.float32))
    wqkv = np.ascontiguousarray(np.asarray(L0_attn_W_qkv, dtype=np.float32))
    bq = np.asarray(L0_attn_b_qkv, dtype=np.float32)
    wpn = np.ascontiguousarray(np.asarray(L0_attn_W_proj, dtype=np.float32))
    bp = np.asarray(L0_attn_b_proj, dtype=np.float32)
    wgn = np.ascontiguousarray(np.asarray(L0_ffn_wq_gate, dtype=np.int32))
    sgn = np.asarray(L0_ffn_s_gate, dtype=np.float32)
    wun = np.ascontiguousarray(np.asarray(L0_ffn_wq_up, dtype=np.int32))
    sun = np.asarray(L0_ffn_s_up, dtype=np.float32)
    wdn = np.ascontiguousarray(np.asarray(L0_ffn_wq_down, dtype=np.int32))
    sdn = np.asarray(L0_ffn_s_down, dtype=np.float32)

    bqkv_t = np.ascontiguousarray(bq.reshape(JQKV, P).T)
    bproj_t = np.ascontiguousarray(np.tile(bp[None, :], (P, 1)))
    sg_t = np.ascontiguousarray(sgn.reshape(HT, P).T)
    su_t = np.ascontiguousarray(sun.reshape(HT, P).T)
    sd_t = np.ascontiguousarray(sdn.reshape(JT, P).T)

    nc = _get_nc()
    in_maps = []
    metas = []
    for c in range(8):
        b = c // 2
        par = c % 2
        gl = [2 * i + par for i in range(NSLOT)]
        xb = x[b]
        xq_c = np.ascontiguousarray(xb.reshape(RT, P, D)[gl])
        tri = (np.arange(128)[None, :] <= np.arange(P)[:, None]).astype(np.float32)
        cm = np.zeros((P, 256), dtype=np.float32)
        if par == 0:
            cm[:, 0:128] = tri
        else:
            cm[:, 0:128] = 1.0
            cm[:, 128:256] = tri
        cmb = cm.astype(ml_dtypes.bfloat16)
        in_maps.append({
            "x_att": xb, "xq": xq_c, "wqkv": wqkv, "bqkv": bqkv_t,
            "wp": wpn, "bproj": bproj_t, "wg": wgn, "wu": wun, "wd": wdn,
            "sg": sg_t, "su": su_t, "sd": sd_t, "cmask": cmb,
        })
        metas.append((b, gl))

    res = run_bass_kernel_spmd(nc, in_maps, core_ids=list(range(8)))

    out = np.empty((B, S, D), dtype=np.float32)
    for c in range(8):
        b, gl = metas[c]
        oc = np.asarray(res.results[c]["out"])
        for i, g in enumerate(gl):
            out[b, g * P:(g + 1) * P, :] = oc[i]
    return out


# version 19
# speedup vs baseline: 1.0706x; 1.0706x over previous
import numpy as np
import ml_dtypes
import concourse.bass as bass
import concourse.mybir as mybir
from concourse.bass_utils import run_bass_kernel_spmd

F32 = mybir.dt.float32
BF16 = mybir.dt.bfloat16
I32 = mybir.dt.int32
AF = mybir.ActivationFunctionType
ALU = mybir.AluOpType
AXX = mybir.AxisListType.X

P = 128
B = 4
S = 4096
D = 2048
DT = D // P
NH = 16
HD = 128
JQKV = 18
HID = 5632
HT = HID // P
JT = D // P
NSLOT = 16
RT = S // P
SCALE = float(HD) ** -0.5
EPS = 1e-5

KE2 = [256 * (i + 1) for i in range(NSLOT)]

ENGS = ("tensor", "scalar", "vector", "sync", "gpsimd")
E_PRELOAD = False
E_INCR = True


class Sched:
    def __init__(self):
        self.streams = {e: [] for e in ENGS}
        self.cnt = {}
        self.sem_names = []

    def sem(self, name):
        if name not in self.cnt:
            self.cnt[name] = 0
            self.sem_names.append(name)

    def wait(self, eng, sem, val):
        val = min(val, self.cnt[sem])
        if val <= 0:
            return
        self.streams[eng].append(("w", sem, val))

    def inc(self, sem, amt):
        self.cnt[sem] += amt
        return self.cnt[sem]

    def op(self, eng, fn, inc_sem=None, inc_amt=1):
        if inc_sem is not None:
            self.inc(inc_sem, inc_amt)
        self.streams[eng].append(("i", fn, inc_sem, inc_amt))


def build_program():
    nc = bass.Bass("TRN2", target_bir_lowering=False)

    x_att = nc.dram_tensor("x_att", [S, D], F32, kind="ExternalInput").ap()
    xq = nc.dram_tensor("xq", [NSLOT, P, D], F32, kind="ExternalInput").ap()
    wqkv = nc.dram_tensor("wqkv", [D, 2304], F32, kind="ExternalInput").ap()
    bqkv = nc.dram_tensor("bqkv", [P, JQKV], F32, kind="ExternalInput").ap()
    wp = nc.dram_tensor("wp", [D, D], F32, kind="ExternalInput").ap()
    bproj = nc.dram_tensor("bproj", [P, D], F32, kind="ExternalInput").ap()
    wg = nc.dram_tensor("wg", [HID, D], I32, kind="ExternalInput").ap()
    wu = nc.dram_tensor("wu", [HID, D], I32, kind="ExternalInput").ap()
    wd = nc.dram_tensor("wd", [D, HID], I32, kind="ExternalInput").ap()
    sg = nc.dram_tensor("sg", [P, HT], F32, kind="ExternalInput").ap()
    su = nc.dram_tensor("su", [P, HT], F32, kind="ExternalInput").ap()
    sd = nc.dram_tensor("sd", [P, JT], F32, kind="ExternalInput").ap()
    cmask_h = nc.dram_tensor("cmask", [P, 256], BF16, kind="ExternalInput").ap()
    out_h = nc.dram_tensor("out", [NSLOT, P, D], F32, kind="ExternalOutput").ap()

    qt_hbm = nc.dram_tensor("qt_hbm", [NH, P, 2048], BF16, kind="Internal").ap()
    x1_hbm = nc.dram_tensor("x1_hbm", [NSLOT, P, D], F32, kind="Internal").ap()
    act_hbm = nc.dram_tensor("act_hbm", [HT, P, 2048], BF16, kind="Internal").ap()
    wdt_hbm = nc.dram_tensor("wdt_hbm", [JT, P, HT * P], BF16, kind="Internal").ap()

    from contextlib import ExitStack
    ctx = ExitStack()
    e = ctx.enter_context
    bqkv_sb = e(nc.sbuf_tensor("bqkv_sb", [P, JQKV], F32))
    bproj_sb = e(nc.sbuf_tensor("bproj_sb", [P, D], F32))
    sg_sb = e(nc.sbuf_tensor("sg_sb", [P, HT], F32))
    su_sb = e(nc.sbuf_tensor("su_sb", [P, HT], F32))
    sd_sb = e(nc.sbuf_tensor("sd_sb", [P, JT], F32))
    sgb_sb = e(nc.sbuf_tensor("sgb_sb", [P, HT], F32))
    sub_sb = e(nc.sbuf_tensor("sub_sb", [P, HT], F32))
    sdb_sb = e(nc.sbuf_tensor("sdb_sb", [P, JT], F32))
    bnb = [e(nc.sbuf_tensor(f"bnb{i}", [P, 24], F32)) for i in range(4)]
    mv = [e(nc.sbuf_tensor(f"mv{i}", [P, 2], F32)) for i in range(4)]
    vpe = [e(nc.sbuf_tensor(f"vpe{i}", [P, 1], F32)) for i in range(4)]
    lnv = [e(nc.sbuf_tensor(f"lnv{i}", [P, 1], F32)) for i in range(4)]
    rstd = [e(nc.sbuf_tensor(f"rstd{i}", [P, 1], F32)) for i in range(4)]
    nmr = [e(nc.sbuf_tensor(f"nmr{i}", [P, 1], F32)) for i in range(4)]
    smc = [e(nc.sbuf_tensor(f"smc{i}", [P, 8], F32)) for i in range(3)]
    zv = [e(nc.sbuf_tensor(f"zv{i}", [P, 1], F32)) for i in range(3)]
    rcv = [e(nc.sbuf_tensor(f"rcv{i}", [P, 1], F32)) for i in range(3)]

    SEG = {
        "wqkv_sb": ("A", [P, DT, 2304], BF16),
        "xt0": ("AC", [P, D], F32), "xt1": ("AC", [P, D], F32),
        "xt2": ("A", [P, D], F32),
        "lnt0": ("AC", [P, D], BF16), "lnt1": ("AC", [P, D], BF16),
        "lnt2": ("AC", [P, D], BF16),
        "xlk0": ("A", [P, DT, 512], BF16), "xlk1": ("A", [P, DT, 512], BF16),
        "kt_sb": ("AB", [P, S], BF16),
        "vt_sb": ("A", [P, S], BF16),
        "v_sb": ("AB", [P, RT, P], BF16),
        "qev0": ("A", [P, 512], BF16), "qev1": ("A", [P, 512], BF16),
        "qev2": ("A", [P, 512], BF16), "qev3": ("A", [P, 512], BF16),
        "cmask_sb": ("AB", [P, 256], BF16),
        "qth0": ("B", [P, 2048], BF16), "qth1": ("B", [P, 2048], BF16),
        "prb0": ("B", [P, 4096], BF16), "prb1": ("B", [P, 4096], BF16),
        "prb2": ("B", [P, 4096], BF16),
        "pT0": ("B", [P, 32, 256], BF16), "pT1": ("B", [P, 32, 256], BF16),
        "pT2": ("B", [P, 32, 256], BF16),
        "attnT": ("BC", [P, NH, 2048], BF16),
        "wpb0": ("C", [P, D], BF16), "wpb1": ("C", [P, D], BF16),
        "wpb2": ("C", [P, D], BF16),
        "x1t0": ("C", [P, D], F32), "x1t1": ("C", [P, D], F32),
        "x1t2": ("C", [P, D], F32),
        "xl2T": ("CD", [P, DT, 2048], BF16),
        "wgr0": ("D", [P, D], BF16), "wgr1": ("D", [P, D], BF16),
        "wur0": ("D", [P, D], BF16), "wur1": ("D", [P, D], BF16),
        "wgd0": ("D", [P, D], BF16), "wgd1": ("D", [P, D], BF16),
        "wud0": ("D", [P, D], BF16), "wud1": ("D", [P, D], BF16),
        "wgT0": ("D", [P, DT, P], BF16), "wgT1": ("D", [P, DT, P], BF16),
        "wuT0": ("D", [P, DT, P], BF16), "wuT1": ("D", [P, DT, P], BF16),
        "sil0": ("D", [P, 512], BF16), "sil1": ("D", [P, 512], BF16),
        "sil2": ("D", [P, 512], BF16), "sil3": ("D", [P, 512], BF16),
        "actt0": ("D", [P, 2048], BF16), "actt1": ("D", [P, 2048], BF16),
        "wdr": ("D", [P, HID], BF16),
        "wdd": ("D", [P, HID], BF16),
        "wdTt": ("D", [P, HT, P], BF16),
        "actH": ("E", [P, HT, 1024], BF16),
        "out_sb": ("E", [P, 8, D], F32),
        "wdTb0": ("E", [P, HT, P], BF16), "wdTb1": ("E", [P, HT, P], BF16),
        "dTt0": ("E", [P, 1024], BF16), "dTt1": ("E", [P, 1024], BF16),
        "dnt0": ("E", [P, 8, P], BF16), "dnt1": ("E", [P, 8, P], BF16),
    }

    def seg_bytes(shape, dtype):
        n = 1
        for s_ in shape[1:]:
            n *= s_
        return n * mybir.dt.size(dtype)

    order = sorted(SEG.items(), key=lambda kv: -seg_bytes(kv[1][1], kv[1][2]))
    placed = []
    offsets = {}
    for name, (phases, shape, dtype) in order:
        sz = (seg_bytes(shape, dtype) + 63) // 64 * 64
        cands = sorted(set([0] + [pe for (pp, po, pe) in placed if set(pp) & set(phases)]))
        for off in cands:
            end = off + sz
            if all((not (set(pp) & set(phases))) or end <= po or off >= pe
                   for (pp, po, pe) in placed):
                placed.append((phases, off, end))
                offsets[name] = off
                break
        else:
            raise RuntimeError("no fit " + name)
    arena_sz = max(pe for (_, _, pe) in placed)
    assert arena_sz <= 202364, f"arena {arena_sz}"
    arena = e(nc.sbuf_tensor("arena", [P, arena_sz], mybir.dt.uint8))
    arena_addr = nc.lookup_mloc(arena).addr

    T = {}
    for name, (phases, shape, dtype) in SEG.items():
        h = nc.virtual_sbuf_tensor(name, shape, dtype)
        nc._tensor(h.name, h.shape, h.dtype, type="SB")
        mloc = nc.lookup_mloc(h)
        mloc.allocated = True
        mloc.addr = arena_addr + offsets[name]
        T[name] = h.ap()

    psum = e(nc.psum_tensor("psum", [P, 4096], F32))

    sch = Sched()
    for s_ in ("ld_cst", "ld_x", "ld_qh", "ld_wp0", "ld_wp1", "ld_wp2",
               "ld_wp3", "ld_wg", "ld_wu", "ld_wd", "ld_wdt0", "ld_wdt1",
               "ld_act", "ld_out", "st_q", "st_x1", "st_act", "st_wdt", "st_out",
               "v_stat", "v_nmr", "v_dg", "v_mx", "v_nm", "v_act", "v_pj", "v_oadd",
               "s_rstd", "s_aff", "s_ev", "s_xp", "s_pe", "s_sil", "s_dq", "s_dqd",
               "s_de", "s_kv", "s_qe",
               "t_mm", "t_qkv", "t_pv", "t_pj", "t_pjd", "t_gu", "t_dn",
               "y_tp", "ph"):
        sch.sem(s_)

    def dma(eng, dst, src, sem, ser=True, **kw):
        if ser and sch.cnt[sem] > 0:
            sch.wait(eng, sem, sch.cnt[sem])
        def fn(E, dst=dst, src=src, kw=kw):
            return E.dma_start(dst, src, **kw)
        sch.op(eng, fn, sem, 16)

    def tpose(dst, src):
        if sch.cnt["y_tp"] > 0:
            sch.wait("sync", "y_tp", sch.cnt["y_tp"])
        def fn(E, dst=dst, src=src):
            return E.dma_start(dst, src, transpose=True)
        sch.op("sync", fn, "y_tp", 16)
        return sch.cnt["y_tp"]

    def act(eng, out, in_, func, sem=None, **kw):
        def fn(E, out=out, in_=in_, func=func, kw=kw):
            return E.activation(out, in_, func, **kw)
        sch.op(eng, fn, sem, 1)

    def mm(out, lhsT, rhs, start, stop, sem=None):
        def fn(E, out=out, lhsT=lhsT, rhs=rhs, start=start, stop=stop):
            return E.matmul(out, lhsT, rhs, start=start, stop=stop)
        sch.op("tensor", fn, sem, 1)

    def vop(meth, *args, sem=None, **kw):
        def fn(E, meth=meth, args=args, kw=kw):
            return getattr(E, meth)(*args, **kw)
        sch.op("vector", fn, sem, 1)

    def phase_barrier():
        for eng in ENGS:
            def fn2(E):
                return E.nop()
            sch.op(eng, fn2, "ph", 1)
        tgt = sch.cnt["ph"]
        for eng in ENGS:
            sch.wait(eng, "ph", tgt)

    dma("gpsimd", bqkv_sb[:], bqkv[:], "ld_cst")
    dma("gpsimd", bproj_sb[:], bproj[:], "ld_cst")
    dma("gpsimd", sg_sb[:], sg[:], "ld_cst")
    dma("gpsimd", su_sb[:], su[:], "ld_cst")
    dma("gpsimd", sd_sb[:], sd[:], "ld_cst")
    dma("gpsimd", T["cmask_sb"][:], cmask_h[:], "ld_cst")
    for t in range(DT):
        dma("gpsimd", T["wqkv_sb"][:, t, :], wqkv[t * P:(t + 1) * P, :], "ld_cst")
    n_cst = sch.cnt["ld_cst"]
    sch.wait("vector", "ld_cst", n_cst)
    vop("tensor_scalar_mul", sgb_sb[:], sg_sb[:], -128.0, sem="v_stat")
    vop("tensor_scalar_mul", sub_sb[:], su_sb[:], -128.0, sem="v_stat")
    vop("tensor_scalar_mul", sdb_sb[:], sd_sb[:], -128.0, sem="v_stat")

    aff_h, rstd_h, nmr_h, stat_h, lntp_h = {}, {}, {}, {}, {}

    def ln_front(r, src_ap, gate):
        pa = r % 4
        for sem_, val_ in gate:
            sch.wait("vector", sem_, val_)
        if r - 4 in aff_h:
            sch.wait("vector", "s_aff", aff_h[r - 4])
        for i in range(4):
            vop("bn_stats", bnb[pa][:, 6 * i:6 * (i + 1)],
                src_ap[:, 512 * i:512 * (i + 1)])
        vop("drain")
        vop("bn_aggr", mv[pa][:], bnb[pa][:])
        vop("drain")
        if r - 4 in rstd_h:
            sch.wait("vector", "s_rstd", rstd_h[r - 4])
        vop("tensor_scalar_add", vpe[pa][:], mv[pa][:, 1:2], EPS, sem="v_stat")
        stat_h[r] = sch.cnt["v_stat"]
        sch.wait("scalar", "v_stat", stat_h[r])
        if r - 4 in nmr_h:
            sch.wait("scalar", "v_nmr", nmr_h[r - 4])
        act("scalar", lnv[pa][:], vpe[pa][:], AF.Ln)
        act("scalar", rstd[pa][:], lnv[pa][:], AF.Exp, sem="s_rstd", scale=-0.5)
        rstd_h[r] = sch.cnt["s_rstd"]
        sch.wait("vector", "s_rstd", rstd_h[r])
        def f_nmr(E, o=nmr[pa][:], i0=mv[pa][:, 0:1], s1=rstd[pa][:]):
            return E.tensor_scalar(o, i0, s1, -1.0, ALU.mult, ALU.mult)
        sch.op("vector", f_nmr, "v_nmr", 1)
        nmr_h[r] = sch.cnt["v_nmr"]

    def ln_back(r, src_ap, dst_fn):
        pa = r % 4
        lnt = T[f"lnt{r % 3}"]
        sch.wait("scalar", "v_nmr", nmr_h[r])
        if r - 3 in lntp_h:
            sch.wait("scalar", "y_tp", lntp_h[r - 3])
        act("scalar", lnt[:], src_ap[:], AF.Identity, sem="s_aff",
            scale=rstd[pa][:], bias=nmr[pa][:])
        aff_h[r] = sch.cnt["s_aff"]
        sch.wait("sync", "s_aff", aff_h[r])
        lntp_h[r] = dst_fn(lnt)

    kv_ev = 0
    q_ev = 0
    qkv_chunk_h = {}

    def emit_chunk(gch):
        nonlocal kv_ev, q_ev
        is_q = gch >= 8
        xlk = T[f"xlk{gch % 2}"]
        sch.wait("tensor", "y_tp", lntp_h[4 * gch + 3])
        sch.wait("tensor", "ld_cst", n_cst)
        if not is_q:
            c = gch
            for ji, jt in enumerate((16, 17)):
                off = (c % 2) * 1024 + ji * 512
                bank = psum[:, off:off + 512]
                if kv_ev + 1 > 4:
                    sch.wait("tensor", "s_kv", kv_ev - 3)
                for d in range(DT):
                    mm(bank, T["wqkv_sb"][:, d, jt * P:(jt + 1) * P], xlk[:, d, :],
                       d == 0, d == DT - 1, sem="t_qkv" if d == DT - 1 else None)
                dst = (T["kt_sb"] if jt == 16 else T["vt_sb"])[:, c * 512:(c + 1) * 512]
                sch.wait("scalar", "t_qkv", sch.cnt["t_qkv"])
                act("scalar", dst, bank, AF.Identity, sem="s_kv",
                    bias=bqkv_sb[:, jt:jt + 1])
                kv_ev += 1
        else:
            qc = gch - 8
            for jt in range(16):
                n = qc * 16 + jt
                off = 2048 + (n % 4) * 512
                bank = psum[:, off:off + 512]
                if n >= 4:
                    sch.wait("tensor", "s_qe", n - 3)
                for d in range(DT):
                    mm(bank, T["wqkv_sb"][:, d, jt * P:(jt + 1) * P], xlk[:, d, :],
                       d == 0, d == DT - 1, sem="t_qkv" if d == DT - 1 else None)
                qb = T[f"qev{n % 4}"]
                sch.wait("scalar", "t_qkv", sch.cnt["t_qkv"])
                if n >= 4:
                    sch.wait("scalar", "st_q", 16 * (n - 3))
                act("scalar", qb[:], bank, AF.Identity, sem="s_qe",
                    bias=bqkv_sb[:, jt:jt + 1])
                q_ev += 1
                sch.wait("gpsimd", "s_qe", q_ev)
                dma("gpsimd", qt_hbm[jt, :, qc * 512:(qc + 1) * 512], qb[:], "st_q")
        qkv_chunk_h[gch] = sch.cnt["t_qkv"]

    def a_back(rb):
        xtb = T[f"xt{rb % 3}"]

        def dst_fn(lnt, cbuf=(rb // 4) % 2, tt=rb % 4, gch=rb // 4):
            if gch - 2 in qkv_chunk_h:
                sch.wait("sync", "t_qkv", qkv_chunk_h[gch - 2])
            return tpose(T[f"xlk{cbuf}"][:, :, tt * P:(tt + 1) * P], lnt[:])

        ln_back(rb, xtb, dst_fn)
        if rb % 4 == 3:
            emit_chunk(rb // 4)

    for r in range(RT + NSLOT):
        is_q = r >= RT
        xt = T[f"xt{r % 3}"]
        if r - 3 in aff_h:
            sch.wait("gpsimd", "s_aff", aff_h[r - 3])
            sch.wait("gpsimd", "v_stat", stat_h[r - 3])
        src = x_att[(r * P):(r + 1) * P, :] if not is_q else xq[r - RT, :, :]
        dma("gpsimd", xt[:], src, "ld_x")
        ln_front(r, xt, [("ld_x", sch.cnt["ld_x"])])
        if r >= 2:
            a_back(r - 2)
    a_back(RT + NSLOT - 2)
    a_back(RT + NSLOT - 1)
    sch.wait("sync", "s_kv", kv_ev)
    v_tp_h = tpose(T["v_sb"][:], T["vt_sb"][:])
    sch.wait("gpsimd", "st_q", 16 * 64)
    phase_barrier()

    NSB = 6
    tenant = [0] * NSB
    nb = 0
    tp_h = {}
    xp_h = {}
    nm_h = {}
    mm_after = {}
    head_mm = {}
    pv_cnt = {}
    pe_cnt = {}
    slot_plan = {}
    PVB = [3072, 3584]

    def plan_slot(u):
        nonlocal nb
        i_loc = 2 * ((u // 2) % 8) + (u & 1)
        E = KE2[i_loc]
        fds = [512] * (E // 512) + ([256] if E % 512 else [])
        segs = []
        koff = 0
        k = 0
        while k < len(fds):
            L = min(4, NSB - nb, len(fds) - k)
            segs.append((nb, fds[k:k + L], koff))
            koff += sum(fds[k:k + L])
            nb = (nb + L) % NSB
            k += L
        sb, sfds, _ = segs[-1]
        dbo = sb * 512 + sum(sfds) - 256
        return [segs, dbo, E, 0]

    def b_slot(u, h):
        segs, dbo, E, _ = slot_plan[u]
        pa = u % 3
        prb = T[f"prb{pa}"]
        i_loc = 2 * ((u // 2) % 8) + (u & 1)
        if u - 3 in tp_h:
            sch.wait("scalar", "y_tp", tp_h[u - 3])
            sch.wait("scalar", "v_nm", nm_h[u - 3])
        nseg = 0
        for si, (b0, sfds, koff) in enumerate(segs):
            last = si == len(segs) - 1
            for ci, fd in enumerate(sfds):
                b = b0 + ci
                if tenant[b] > 0:
                    sch.wait("tensor", "s_xp", tenant[b])
                k0 = koff + sum(sfds[:ci])
                mm(psum[:, b * 512:b * 512 + fd],
                   T[f"qth{h % 2}"][:, i_loc * P:(i_loc + 1) * P],
                   T["kt_sb"][:, k0:k0 + fd], True, True, sem="t_mm")
            sch.wait("scalar", "t_mm", sch.cnt["t_mm"])
            fd = sum(sfds) - (256 if last else 0)
            if fd > 0:
                act("scalar", prb[:, koff:koff + fd],
                    psum[:, b0 * 512:b0 * 512 + fd],
                    AF.Exp, sem="s_xp", scale=SCALE,
                    accum_out=smc[pa][:, nseg:nseg + 1])
                nseg += 1
            if last:
                act("scalar", prb[:, E - 256:E], psum[:, dbo:dbo + 256],
                    AF.Exp, sem="s_xp", scale=SCALE)
            for ci in range(len(sfds)):
                tenant[b0 + ci] = sch.cnt["s_xp"]
        xp_h[u] = sch.cnt["s_xp"]
        mm_after[u] = sch.cnt["t_mm"]
        slot_plan[u][3] = nseg

    def b_post(u):
        segs, dbo, E, nseg = slot_plan[u]
        pa = u % 3
        prb = T[f"prb{pa}"]
        sch.wait("vector", "s_xp", xp_h[u])

        def f_stt(Eg, o=prb[:, E - 256:E], i1=T["cmask_sb"][:],
                  acc=smc[pa][:, nseg:nseg + 1]):
            return Eg.scalar_tensor_tensor(out=o, in0=o, scalar=1.0, in1=i1,
                                           op0=ALU.mult, op1=ALU.mult,
                                           accum_out=acc)
        sch.op("vector", f_stt, "v_dg", 1)
        vop("drain")
        vop("reduce_sum", zv[pa][:], smc[pa][:, 0:nseg + 1], AXX)
        vop("drain")
        vop("reciprocal", rcv[pa][:], zv[pa][:])
        vop("drain")
        if u % 2 == 0:
            vop("tensor_scalar_mul", prb[:, 0:E], prb[:, 0:E], rcv[pa][:])
            vop("memset", prb[:, E:E + 256], 0.0, sem="v_nm")
        else:
            vop("tensor_scalar_mul", prb[:, 0:E], prb[:, 0:E], rcv[pa][:],
                sem="v_nm")
        nm_h[u] = sch.cnt["v_nm"]

    def b_tpose(u):
        pa = u % 3
        j = u // 2
        E = slot_plan[u][2]
        Ep = E + 256 if u % 2 == 0 else E
        sch.wait("sync", "v_nm", nm_h[u])
        if j - 3 in pv_cnt:
            sch.wait("sync", "t_pv", pv_cnt[j - 3])
        tp_h[u] = tpose(T[f"pT{j % 3}"][:, 0:Ep // 128, (u & 1) * P:((u & 1) + 1) * P],
                        T[f"prb{pa}"][:, 0:Ep])

    def emit_pv(j):
        pr_ = j % 8
        nkt = 4 * (pr_ + 1)
        sch.wait("tensor", "y_tp", tp_h[2 * j + 1])
        if j - 2 in pe_cnt:
            sch.wait("tensor", "s_pe", pe_cnt[j - 2])
        pvb = psum[:, PVB[j % 2]:PVB[j % 2] + 256]
        for kt in range(nkt):
            mm(pvb, T["v_sb"][:, kt, :], T[f"pT{j % 3}"][:, kt, :],
               kt == 0, kt == nkt - 1, sem="t_pv" if kt == nkt - 1 else None)
        pv_cnt[j] = sch.cnt["t_pv"]

    def emit_pv_evac(j):
        h_, pr_ = j // 8, j % 8
        sch.wait("scalar", "t_pv", pv_cnt[j])
        pvb = psum[:, PVB[j % 2]:PVB[j % 2] + 256]
        act("scalar", T["attnT"][:, h_, pr_ * 256:(pr_ + 1) * 256], pvb,
            AF.Identity, sem="s_pe")
        pe_cnt[j] = sch.cnt["s_pe"]

    for idx in range(128):
        h, pr = idx // 8, idx % 8
        if pr == 0:
            if h >= 2:
                sch.wait("gpsimd", "t_mm", head_mm[h - 2])
            dma("gpsimd", T[f"qth{h % 2}"][:], qt_hbm[h, :, :], "ld_qh")
        if idx >= 2:
            emit_pv(idx - 2)
        if idx >= 3:
            emit_pv_evac(idx - 3)
        for s01 in range(2):
            u = 2 * idx + s01
            slot_plan[u] = plan_slot(u)
            sch.wait("tensor", "ld_qh", 16 * (h + 1))
            b_slot(u, h)
            b_post(u)
            b_tpose(u)
        if pr == 7:
            head_mm[h] = sch.cnt["t_mm"]
    emit_pv(126)
    emit_pv_evac(125)
    emit_pv(127)
    emit_pv_evac(126)
    emit_pv_evac(127)
    phase_barrier()

    pjd_h = []
    for sp in range(8):
        slots = (2 * sp, 2 * sp + 1)
        for sl01, sl in enumerate(slots):
            if sp >= 1:
                sch.wait("gpsimd", "v_pj", 2 * sp)
            dma("gpsimd", T[f"xt{sl01}"][:], xq[sl, :, :], "ld_x")
        for dd in range(DT):
            gdd = sp * DT + dd
            if gdd >= 3:
                sch.wait("gpsimd", "t_pjd", pjd_h[gdd - 3])
            dma("gpsimd", T[f"wpb{dd % 3}"][:], wp[dd * P:(dd + 1) * P, :],
                f"ld_wp{dd % 3}")
            sch.wait("tensor", f"ld_wp{dd % 3}", sch.cnt[f"ld_wp{dd % 3}"])
            if sp >= 1 and dd == 0:
                sch.wait("tensor", "v_pj", 2 * sp)
            for sl01 in range(2):
                for jc in range(4):
                    mm(psum[:, sl01 * 2048 + jc * 512: sl01 * 2048 + jc * 512 + 512],
                       T["attnT"][:, dd, slots[sl01] * P:(slots[sl01] + 1) * P],
                       T[f"wpb{dd % 3}"][:, jc * 512:(jc + 1) * 512],
                       dd == 0, dd == DT - 1,
                       sem="t_pjd" if (sl01 == 1 and jc == 3) else None)
            pjd_h.append(sch.cnt["t_pjd"])
        for sl01, sl in enumerate(slots):
            r = RT + NSLOT + sl
            x1t = T[f"x1t{sl % 3}"]
            sch.wait("vector", "t_pjd", DT * (sp + 1))
            sch.wait("vector", "ld_x", sch.cnt["ld_x"])
            if sl - 3 >= 0:
                sch.wait("vector", "st_x1", 16 * (sl - 2))
                if r - 3 in aff_h:
                    sch.wait("vector", "s_aff", aff_h[r - 3])
            vop("tensor_add", x1t[:], psum[:, sl01 * 2048:sl01 * 2048 + 2048],
                T[f"xt{sl01}"][:])
            vop("tensor_add", x1t[:], x1t[:], bproj_sb[:], sem="v_pj")
            sch.wait("gpsimd", "v_pj", sch.cnt["v_pj"])
            dma("gpsimd", x1_hbm[sl, :, :], x1t[:], "st_x1")
            ln_front(r, x1t, [("v_pj", sch.cnt["v_pj"])])
            if sl >= 2:
                rb = r - 2

                def dst2(lnt, sl2=sl - 2):
                    return tpose(T["xl2T"][:, :, sl2 * P:(sl2 + 1) * P], lnt[:])

                ln_back(rb, T[f"x1t{(sl - 2) % 3}"], dst2)
    for sl2 in (NSLOT - 2, NSLOT - 1):
        rb = RT + NSLOT + sl2

        def dst3(lnt, sl2=sl2):
            return tpose(T["xl2T"][:, :, sl2 * P:(sl2 + 1) * P], lnt[:])

        ln_back(rb, T[f"x1t{sl2 % 3}"], dst3)
    sch.wait("gpsimd", "st_x1", 16 * 16)
    phase_barrier()

    gu_grp = 0
    gutp_h = {}
    wdtp_h = {}
    dq_h = {}

    def d_load(ht):
        pa = ht % 2
        if ht >= 2:
            sch.wait("gpsimd", "s_dq", dq_h.get(ht - 2, 0))
        dma("gpsimd", T[f"wgr{pa}"][:], wg[ht * P:(ht + 1) * P, :], "ld_wg")
        dma("gpsimd", T[f"wur{pa}"][:], wu[ht * P:(ht + 1) * P, :], "ld_wu")

    def d_prep(ht):
        pa = ht % 2
        sch.wait("scalar", "ld_wg", 16 * (ht + 1))
        sch.wait("scalar", "ld_wu", 16 * (ht + 1))
        sch.wait("scalar", "v_stat", 3)
        if ht - 2 in gutp_h:
            sch.wait("scalar", "y_tp", gutp_h[ht - 2])
        act("scalar", T[f"wgd{pa}"][:], T[f"wgr{pa}"][:], AF.Identity, sem="s_dq",
            scale=sg_sb[:, ht:ht + 1], bias=sgb_sb[:, ht:ht + 1])
        act("scalar", T[f"wud{pa}"][:], T[f"wur{pa}"][:], AF.Identity, sem="s_dq",
            scale=su_sb[:, ht:ht + 1], bias=sub_sb[:, ht:ht + 1])
        dq_h[ht] = sch.cnt["s_dq"]
        sch.wait("sync", "s_dq", dq_h[ht])
        if ht >= 2:
            sch.wait("sync", "t_gu", 4 * (ht - 1))
        tpose(T[f"wgT{pa}"][:], T[f"wgd{pa}"][:])
        gutp_h[ht] = tpose(T[f"wuT{pa}"][:], T[f"wud{pa}"][:])

    d_load(0)
    d_prep(0)
    for ht in range(HT):
        pa = ht % 2
        if ht + 1 < HT:
            d_load(ht + 1)
            d_prep(ht + 1)
        for hh in range(2):
            goff = (gu_grp % 2) * 2048
            sch.wait("tensor", "y_tp", gutp_h[ht])
            if gu_grp >= 2:
                sch.wait("tensor", "s_sil", 2 * (gu_grp - 1))
                sch.wait("tensor", "v_act", 2 * (gu_grp - 1))
            for d in range(DT):
                for ci, cc in enumerate((2 * hh, 2 * hh + 1)):
                    mm(psum[:, goff + ci * 512:goff + ci * 512 + 512],
                       T[f"wgT{pa}"][:, d, :], T["xl2T"][:, d, cc * 512:(cc + 1) * 512],
                       d == 0, d == DT - 1)
                for ci, cc in enumerate((2 * hh, 2 * hh + 1)):
                    mm(psum[:, goff + 1024 + ci * 512:goff + 1024 + ci * 512 + 512],
                       T[f"wuT{pa}"][:, d, :], T["xl2T"][:, d, cc * 512:(cc + 1) * 512],
                       d == 0, d == DT - 1,
                       sem="t_gu" if (d == DT - 1 and ci == 1) else None)
            for ci, cc in enumerate((2 * hh, 2 * hh + 1)):
                sch.wait("scalar", "t_gu", sch.cnt["t_gu"])
                if gu_grp >= 2:
                    sch.wait("scalar", "v_act", 2 * (gu_grp - 1))
                act("scalar", T[f"sil{cc}"][:],
                    psum[:, goff + ci * 512:goff + ci * 512 + 512],
                    AF.Silu, sem="s_sil")
                sch.wait("vector", "s_sil", sch.cnt["s_sil"])
                if ht >= 2 and hh == 0 and ci == 0:
                    sch.wait("vector", "st_act", 16 * (ht - 1))
                def f_mul(E, o=T[f"actt{pa}"][:, cc * 512:(cc + 1) * 512],
                          i0=T[f"sil{cc}"][:],
                          i1=psum[:, goff + 1024 + ci * 512:goff + 1024 + ci * 512 + 512]):
                    return E.tensor_mul(o, i0, i1)
                sch.op("vector", f_mul, "v_act", 1)
            gu_grp += 1
        sch.wait("gpsimd", "v_act", sch.cnt["v_act"])
        dma("gpsimd", act_hbm[ht, :, :], T[f"actt{pa}"][:], "st_act")
        if ht >= 4 and ht % 2 == 0 and (ht - 4) // 2 < JT:
            jt = (ht - 4) // 2
            if jt >= 1:
                sch.wait("gpsimd", "s_dqd", jt)
            dma("gpsimd", T["wdr"][:], wd[jt * P:(jt + 1) * P, :], "ld_wd",
                max_dma_last_dim=8192)
            sch.wait("scalar", "ld_wd", 16 * (jt + 1))
            if jt - 1 in wdtp_h:
                sch.wait("scalar", "y_tp", wdtp_h[jt - 1])
            act("scalar", T["wdd"][:], T["wdr"][:], AF.Identity, sem="s_dqd",
                scale=sd_sb[:, jt:jt + 1], bias=sdb_sb[:, jt:jt + 1])
            sch.wait("sync", "s_dqd", sch.cnt["s_dqd"])
            if jt >= 1:
                sch.wait("sync", "st_wdt", 16 * jt)
            wdtp_h[jt] = tpose(T["wdTt"][:], T["wdd"][:])
            sch.wait("gpsimd", "y_tp", wdtp_h[jt])
            dma("gpsimd", wdt_hbm[jt, :, :], T["wdTt"][:], "st_wdt")
    sch.wait("gpsimd", "st_act", 16 * HT)
    sch.wait("gpsimd", "st_wdt", 16 * JT)
    phase_barrier()

    detp_h = {}
    dn_after = {}
    wdtb_cnt = {}
    for hh in range(2):
        if hh == 1:
            sch.wait("gpsimd", "st_out", 16 * 8)
            sch.wait("gpsimd", "v_oadd", sch.cnt["v_oadd"])
        for jt0 in range(2):
            g0 = hh * JT + jt0
            if g0 - 2 in dn_after:
                sch.wait("gpsimd", "t_dn", dn_after[g0 - 2])
            dma("gpsimd", T[f"wdTb{g0 % 2}"][:], wdt_hbm[jt0, :, :],
                f"ld_wdt{g0 % 2}")
            wdtb_cnt[g0] = sch.cnt[f"ld_wdt{g0 % 2}"]
        for ht in range(HT):
            dma("gpsimd", T["actH"][:, ht, :],
                act_hbm[ht, :, hh * 1024:(hh + 1) * 1024], "ld_act")
        for sl in range(8):
            dma("gpsimd", T["out_sb"][:, sl, :], x1_hbm[hh * 8 + sl, :, :], "ld_out")
        for jt in range(JT):
            g = hh * JT + jt
            pa = g % 2
            if jt >= 2:
                sch.wait("gpsimd", "t_dn", dn_after[g - 2])
                dma("gpsimd", T[f"wdTb{pa}"][:], wdt_hbm[jt, :, :],
                    f"ld_wdt{pa}")
                wdtb_cnt[g] = sch.cnt[f"ld_wdt{pa}"]
            sch.wait("tensor", f"ld_wdt{pa}", wdtb_cnt[g])
            if jt > 0 or not E_INCR:
                sch.wait("tensor", "ld_act", 16 * (hh + 1) * HT)
            if g >= 2:
                sch.wait("tensor", "s_de", g - 1)
            for ht in range(HT):
                if jt == 0 and E_INCR:
                    sch.wait("tensor", "ld_act", 16 * (hh * HT + ht + 1))
                for ci in range(2):
                    mm(psum[:, pa * 1024 + ci * 512:pa * 1024 + ci * 512 + 512],
                       T[f"wdTb{pa}"][:, ht, :],
                       T["actH"][:, ht, ci * 512:(ci + 1) * 512],
                       ht == 0, ht == HT - 1,
                       sem="t_dn" if (ht == HT - 1 and ci == 1) else None)
            dn_after[g] = sch.cnt["t_dn"]
            sch.wait("scalar", "t_dn", sch.cnt["t_dn"])
            if g - 2 in detp_h:
                sch.wait("scalar", "y_tp", detp_h[g - 2])
            act("scalar", T[f"dTt{pa}"][:], psum[:, pa * 1024:pa * 1024 + 1024],
                AF.Identity, sem="s_de")
            sch.wait("sync", "s_de", sch.cnt["s_de"])
            if g >= 2:
                sch.wait("sync", "v_oadd", 8 * (g - 1))
            detp_h[g] = tpose(T[f"dnt{pa}"][:], T[f"dTt{pa}"][:])
            sch.wait("vector", "y_tp", detp_h[g])
            sch.wait("vector", "ld_out", 16 * (hh + 1) * 8)
            for nt in range(8):
                o = T["out_sb"][:, nt, jt * P:jt * P + P]
                def f_add(E, o=o, i1=T[f"dnt{pa}"][:, nt, :]):
                    return E.tensor_add(o, o, i1)
                sch.op("vector", f_add, "v_oadd", 1)
        sch.wait("gpsimd", "v_oadd", sch.cnt["v_oadd"])
        for sl in range(8):
            dma("gpsimd", out_h[hh * 8 + sl, :, :], T["out_sb"][:, sl, :], "st_out")
    sch.wait("gpsimd", "st_out", 16 * 16)

    global _LAST_SCHED
    _LAST_SCHED = sch
    sems = {}
    for name in sch.sem_names:
        sems[name] = e(nc.semaphore(name))
    block = e(nc.Block())

    def emit(E, stream):
        for item in stream:
            if item[0] == "w":
                E.wait_ge(sems[item[1]], item[2])
            else:
                _, fn, inc_sem, inc_amt = item
                inst = fn(E)
                if inc_sem is not None and inst is not None:
                    inst.then_inc(sems[inc_sem], inc_amt)

    @block.tensor
    def _(t):
        emit(t, sch.streams["tensor"])

    @block.scalar
    def _(sc):
        emit(sc, sch.streams["scalar"])

    @block.vector
    def _(v):
        emit(v, sch.streams["vector"])

    @block.sync
    def _(sy):
        emit(sy, sch.streams["sync"])

    @block.gpsimd
    def _(g):
        emit(g, sch.streams["gpsimd"])

    ctx.close()
    return nc


_LAST_SCHED = None
_NC_CACHE = None


def _get_nc():
    global _NC_CACHE
    if _NC_CACHE is None:
        _NC_CACHE = build_program()
    return _NC_CACHE


def kernel(L0_attn_x, L0_attn_W_qkv, L0_attn_b_qkv, L0_attn_W_proj, L0_attn_b_proj,
           L0_ffn_wq_gate, L0_ffn_s_gate, L0_ffn_wq_up, L0_ffn_s_up,
           L0_ffn_wq_down, L0_ffn_s_down):
    x = np.ascontiguousarray(np.asarray(L0_attn_x, dtype=np.float32))
    wqkv = np.ascontiguousarray(np.asarray(L0_attn_W_qkv, dtype=np.float32))
    bq = np.asarray(L0_attn_b_qkv, dtype=np.float32)
    wpn = np.ascontiguousarray(np.asarray(L0_attn_W_proj, dtype=np.float32))
    bp = np.asarray(L0_attn_b_proj, dtype=np.float32)
    wgn = np.ascontiguousarray(np.asarray(L0_ffn_wq_gate, dtype=np.int32))
    sgn = np.asarray(L0_ffn_s_gate, dtype=np.float32)
    wun = np.ascontiguousarray(np.asarray(L0_ffn_wq_up, dtype=np.int32))
    sun = np.asarray(L0_ffn_s_up, dtype=np.float32)
    wdn = np.ascontiguousarray(np.asarray(L0_ffn_wq_down, dtype=np.int32))
    sdn = np.asarray(L0_ffn_s_down, dtype=np.float32)

    bqkv_t = np.ascontiguousarray(bq.reshape(JQKV, P).T)
    bproj_t = np.ascontiguousarray(np.tile(bp[None, :], (P, 1)))
    sg_t = np.ascontiguousarray(sgn.reshape(HT, P).T)
    su_t = np.ascontiguousarray(sun.reshape(HT, P).T)
    sd_t = np.ascontiguousarray(sdn.reshape(JT, P).T)

    nc = _get_nc()
    in_maps = []
    metas = []
    for c in range(8):
        b = c // 2
        par = c % 2
        gl = [2 * i + par for i in range(NSLOT)]
        xb = x[b]
        xq_c = np.ascontiguousarray(xb.reshape(RT, P, D)[gl])
        tri = (np.arange(128)[None, :] <= np.arange(P)[:, None]).astype(np.float32)
        cm = np.zeros((P, 256), dtype=np.float32)
        if par == 0:
            cm[:, 0:128] = tri
        else:
            cm[:, 0:128] = 1.0
            cm[:, 128:256] = tri
        cmb = cm.astype(ml_dtypes.bfloat16)
        in_maps.append({
            "x_att": xb, "xq": xq_c, "wqkv": wqkv, "bqkv": bqkv_t,
            "wp": wpn, "bproj": bproj_t, "wg": wgn, "wu": wun, "wd": wdn,
            "sg": sg_t, "su": su_t, "sd": sd_t, "cmask": cmb,
        })
        metas.append((b, gl))

    res = run_bass_kernel_spmd(nc, in_maps, core_ids=list(range(8)))

    out = np.empty((B, S, D), dtype=np.float32)
    for c in range(8):
        b, gl = metas[c]
        oc = np.asarray(res.results[c]["out"])
        for i, g in enumerate(gl):
            out[b, g * P:(g + 1) * P, :] = oc[i]
    return out


# version 20
# speedup vs baseline: 1.1158x; 1.0421x over previous
import numpy as np
import ml_dtypes
import concourse.bass as bass
import concourse.mybir as mybir
from concourse.bass_utils import run_bass_kernel_spmd

F32 = mybir.dt.float32
BF16 = mybir.dt.bfloat16
I32 = mybir.dt.int32
AF = mybir.ActivationFunctionType
ALU = mybir.AluOpType
AXX = mybir.AxisListType.X

P = 128
B = 4
S = 4096
D = 2048
DT = D // P
NH = 16
HD = 128
JQKV = 18
HID = 5632
HT = HID // P
JT = D // P
NSLOT = 16
RT = S // P
SCALE = float(HD) ** -0.5
EPS = 1e-5

KE2 = [256 * (i + 1) for i in range(NSLOT)]

ENGS = ("tensor", "scalar", "vector", "sync", "gpsimd")
E_PRELOAD = False
E_INCR = True


class Sched:
    def __init__(self):
        self.streams = {e: [] for e in ENGS}
        self.cnt = {}
        self.sem_names = []

    def sem(self, name):
        if name not in self.cnt:
            self.cnt[name] = 0
            self.sem_names.append(name)

    def wait(self, eng, sem, val):
        val = min(val, self.cnt[sem])
        if val <= 0:
            return
        self.streams[eng].append(("w", sem, val))

    def inc(self, sem, amt):
        self.cnt[sem] += amt
        return self.cnt[sem]

    def op(self, eng, fn, inc_sem=None, inc_amt=1):
        if inc_sem is not None:
            self.inc(inc_sem, inc_amt)
        self.streams[eng].append(("i", fn, inc_sem, inc_amt))


def build_program():
    nc = bass.Bass("TRN2", target_bir_lowering=False)

    x_att = nc.dram_tensor("x_att", [S, D], F32, kind="ExternalInput").ap()
    xq = nc.dram_tensor("xq", [NSLOT, P, D], F32, kind="ExternalInput").ap()
    wqkv = nc.dram_tensor("wqkv", [D, 2304], F32, kind="ExternalInput").ap()
    bqkv = nc.dram_tensor("bqkv", [P, JQKV], F32, kind="ExternalInput").ap()
    wp = nc.dram_tensor("wp", [D, D], F32, kind="ExternalInput").ap()
    bproj = nc.dram_tensor("bproj", [P, D], F32, kind="ExternalInput").ap()
    wg = nc.dram_tensor("wg", [HID, D], I32, kind="ExternalInput").ap()
    wu = nc.dram_tensor("wu", [HID, D], I32, kind="ExternalInput").ap()
    wd = nc.dram_tensor("wd", [D, HID], I32, kind="ExternalInput").ap()
    sg = nc.dram_tensor("sg", [P, HT], F32, kind="ExternalInput").ap()
    su = nc.dram_tensor("su", [P, HT], F32, kind="ExternalInput").ap()
    sd = nc.dram_tensor("sd", [P, JT], F32, kind="ExternalInput").ap()
    cmask_h = nc.dram_tensor("cmask", [P, 256], BF16, kind="ExternalInput").ap()
    out_h = nc.dram_tensor("out", [NSLOT, P, D], F32, kind="ExternalOutput").ap()

    qt_hbm = nc.dram_tensor("qt_hbm", [NH, P, 2048], BF16, kind="Internal").ap()
    x1_hbm = nc.dram_tensor("x1_hbm", [NSLOT, P, D], F32, kind="Internal").ap()
    act_hbm = nc.dram_tensor("act_hbm", [HT, P, 2048], BF16, kind="Internal").ap()
    wdt_hbm = nc.dram_tensor("wdt_hbm", [JT, P, HT * P], BF16, kind="Internal").ap()

    from contextlib import ExitStack
    ctx = ExitStack()
    e = ctx.enter_context
    bqkv_sb = e(nc.sbuf_tensor("bqkv_sb", [P, JQKV], F32))
    bproj_sb = e(nc.sbuf_tensor("bproj_sb", [P, D], F32))
    sg_sb = e(nc.sbuf_tensor("sg_sb", [P, HT], F32))
    su_sb = e(nc.sbuf_tensor("su_sb", [P, HT], F32))
    sd_sb = e(nc.sbuf_tensor("sd_sb", [P, JT], F32))
    sgb_sb = e(nc.sbuf_tensor("sgb_sb", [P, HT], F32))
    sub_sb = e(nc.sbuf_tensor("sub_sb", [P, HT], F32))
    sdb_sb = e(nc.sbuf_tensor("sdb_sb", [P, JT], F32))
    bnb = [e(nc.sbuf_tensor(f"bnb{i}", [P, 24], F32)) for i in range(4)]
    mv = [e(nc.sbuf_tensor(f"mv{i}", [P, 2], F32)) for i in range(4)]
    vpe = [e(nc.sbuf_tensor(f"vpe{i}", [P, 1], F32)) for i in range(4)]
    lnv = [e(nc.sbuf_tensor(f"lnv{i}", [P, 1], F32)) for i in range(4)]
    rstd = [e(nc.sbuf_tensor(f"rstd{i}", [P, 1], F32)) for i in range(4)]
    nmr = [e(nc.sbuf_tensor(f"nmr{i}", [P, 1], F32)) for i in range(4)]
    smc = [e(nc.sbuf_tensor(f"smc{i}", [P, 8], F32)) for i in range(3)]
    zv = [e(nc.sbuf_tensor(f"zv{i}", [P, 1], F32)) for i in range(3)]
    rcv = [e(nc.sbuf_tensor(f"rcv{i}", [P, 1], F32)) for i in range(3)]

    SEG = {
        "wqkv_sb": ("A", [P, DT, 2304], BF16),
        "xt0": ("AC", [P, D], F32), "xt1": ("AC", [P, D], F32),
        "xt2": ("A", [P, D], F32), "xt3": ("A", [P, D], F32),
        "xt4": ("A", [P, D], F32),
        "lnt0": ("AC", [P, D], BF16), "lnt1": ("AC", [P, D], BF16),
        "lnt2": ("AC", [P, D], BF16),
        "xlk0": ("A", [P, DT, 512], BF16), "xlk1": ("A", [P, DT, 512], BF16),
        "kt_sb": ("AB", [P, S], BF16),
        "vt_sb": ("A", [P, S], BF16),
        "v_sb": ("AB", [P, RT, P], BF16),
        "qev0": ("A", [P, 512], BF16), "qev1": ("A", [P, 512], BF16),
        "qev2": ("A", [P, 512], BF16), "qev3": ("A", [P, 512], BF16),
        "cmask_sb": ("AB", [P, 256], BF16),
        "qth0": ("B", [P, 2048], BF16), "qth1": ("B", [P, 2048], BF16),
        "prb0": ("B", [P, 4096], BF16), "prb1": ("B", [P, 4096], BF16),
        "prb2": ("B", [P, 4096], BF16),
        "pT0": ("B", [P, 32, 256], BF16), "pT1": ("B", [P, 32, 256], BF16),
        "pT2": ("B", [P, 32, 256], BF16),
        "attnT": ("BC", [P, NH, 2048], BF16),
        "wpb0": ("C", [P, D], BF16), "wpb1": ("C", [P, D], BF16),
        "wpb2": ("C", [P, D], BF16),
        "x1t0": ("C", [P, D], F32), "x1t1": ("C", [P, D], F32),
        "x1t2": ("C", [P, D], F32),
        "xl2T": ("CD", [P, DT, 2048], BF16),
        "wgr0": ("D", [P, D], BF16), "wgr1": ("D", [P, D], BF16),
        "wur0": ("D", [P, D], BF16), "wur1": ("D", [P, D], BF16),
        "wgd0": ("D", [P, D], BF16), "wgd1": ("D", [P, D], BF16),
        "wud0": ("D", [P, D], BF16), "wud1": ("D", [P, D], BF16),
        "wgT0": ("D", [P, DT, P], BF16), "wgT1": ("D", [P, DT, P], BF16),
        "wuT0": ("D", [P, DT, P], BF16), "wuT1": ("D", [P, DT, P], BF16),
        "sil0": ("D", [P, 512], BF16), "sil1": ("D", [P, 512], BF16),
        "sil2": ("D", [P, 512], BF16), "sil3": ("D", [P, 512], BF16),
        "actt0": ("D", [P, 2048], BF16), "actt1": ("D", [P, 2048], BF16),
        "wdr": ("D", [P, HID], BF16),
        "wdd": ("D", [P, HID], BF16),
        "wdTt": ("D", [P, HT, P], BF16),
        "actH": ("E", [P, HT, 1024], BF16),
        "out_sb": ("E", [P, 8, D], F32),
        "wdTb0": ("E", [P, HT, P], BF16), "wdTb1": ("E", [P, HT, P], BF16),
        "dTt0": ("E", [P, 1024], BF16), "dTt1": ("E", [P, 1024], BF16),
        "dnt0": ("E", [P, 8, P], BF16), "dnt1": ("E", [P, 8, P], BF16),
    }

    def seg_bytes(shape, dtype):
        n = 1
        for s_ in shape[1:]:
            n *= s_
        return n * mybir.dt.size(dtype)

    order = sorted(SEG.items(), key=lambda kv: -seg_bytes(kv[1][1], kv[1][2]))
    placed = []
    offsets = {}
    for name, (phases, shape, dtype) in order:
        sz = (seg_bytes(shape, dtype) + 63) // 64 * 64
        cands = sorted(set([0] + [pe for (pp, po, pe) in placed if set(pp) & set(phases)]))
        for off in cands:
            end = off + sz
            if all((not (set(pp) & set(phases))) or end <= po or off >= pe
                   for (pp, po, pe) in placed):
                placed.append((phases, off, end))
                offsets[name] = off
                break
        else:
            raise RuntimeError("no fit " + name)
    arena_sz = max(pe for (_, _, pe) in placed)
    assert arena_sz <= 202364, f"arena {arena_sz}"
    arena = e(nc.sbuf_tensor("arena", [P, arena_sz], mybir.dt.uint8))
    arena_addr = nc.lookup_mloc(arena).addr

    T = {}
    for name, (phases, shape, dtype) in SEG.items():
        h = nc.virtual_sbuf_tensor(name, shape, dtype)
        nc._tensor(h.name, h.shape, h.dtype, type="SB")
        mloc = nc.lookup_mloc(h)
        mloc.allocated = True
        mloc.addr = arena_addr + offsets[name]
        T[name] = h.ap()

    psum = e(nc.psum_tensor("psum", [P, 4096], F32))

    sch = Sched()
    for s_ in ("ld_cst", "ld_x0", "ld_x1", "ld_qh", "ld_wp0", "ld_wp1", "ld_wp2",
               "ld_wp3", "ld_wg", "ld_wu", "ld_wd", "ld_wdt0", "ld_wdt1",
               "ld_act", "ld_out", "st_q", "st_x1", "st_act", "st_wdt", "st_out",
               "v_stat", "v_nmr", "v_dg", "v_mx", "v_nm", "v_act", "v_pj", "v_oadd",
               "s_rstd", "s_aff", "s_ev", "s_xp", "s_pe", "s_sil", "s_dq", "s_dqd",
               "s_de", "s_kv", "s_qe",
               "t_mm", "t_qkv", "t_pv", "t_pj", "t_pjd", "t_gu", "t_dn",
               "y_tp", "ph"):
        sch.sem(s_)

    def dma(eng, dst, src, sem, ser=True, **kw):
        if ser and sch.cnt[sem] > 0:
            sch.wait(eng, sem, sch.cnt[sem])
        def fn(E, dst=dst, src=src, kw=kw):
            return E.dma_start(dst, src, **kw)
        sch.op(eng, fn, sem, 16)

    def tpose(dst, src):
        if sch.cnt["y_tp"] > 0:
            sch.wait("sync", "y_tp", sch.cnt["y_tp"])
        def fn(E, dst=dst, src=src):
            return E.dma_start(dst, src, transpose=True)
        sch.op("sync", fn, "y_tp", 16)
        return sch.cnt["y_tp"]

    def act(eng, out, in_, func, sem=None, **kw):
        def fn(E, out=out, in_=in_, func=func, kw=kw):
            return E.activation(out, in_, func, **kw)
        sch.op(eng, fn, sem, 1)

    def mm(out, lhsT, rhs, start, stop, sem=None):
        def fn(E, out=out, lhsT=lhsT, rhs=rhs, start=start, stop=stop):
            return E.matmul(out, lhsT, rhs, start=start, stop=stop)
        sch.op("tensor", fn, sem, 1)

    def vop(meth, *args, sem=None, **kw):
        def fn(E, meth=meth, args=args, kw=kw):
            return getattr(E, meth)(*args, **kw)
        sch.op("vector", fn, sem, 1)

    def phase_barrier():
        for eng in ENGS:
            def fn2(E):
                return E.nop()
            sch.op(eng, fn2, "ph", 1)
        tgt = sch.cnt["ph"]
        for eng in ENGS:
            sch.wait(eng, "ph", tgt)

    dma("gpsimd", bqkv_sb[:], bqkv[:], "ld_cst")
    dma("gpsimd", bproj_sb[:], bproj[:], "ld_cst")
    dma("gpsimd", sg_sb[:], sg[:], "ld_cst")
    dma("gpsimd", su_sb[:], su[:], "ld_cst")
    dma("gpsimd", sd_sb[:], sd[:], "ld_cst")
    dma("gpsimd", T["cmask_sb"][:], cmask_h[:], "ld_cst")
    for t in range(DT):
        dma("gpsimd", T["wqkv_sb"][:, t, :], wqkv[t * P:(t + 1) * P, :], "ld_cst")
    n_cst = sch.cnt["ld_cst"]
    sch.wait("vector", "ld_cst", n_cst)
    vop("tensor_scalar_mul", sgb_sb[:], sg_sb[:], -128.0, sem="v_stat")
    vop("tensor_scalar_mul", sub_sb[:], su_sb[:], -128.0, sem="v_stat")
    vop("tensor_scalar_mul", sdb_sb[:], sd_sb[:], -128.0, sem="v_stat")

    aff_h, rstd_h, nmr_h, stat_h, lntp_h = {}, {}, {}, {}, {}

    def ln_front(r, src_ap, gate):
        pa = r % 4
        for sem_, val_ in gate:
            sch.wait("vector", sem_, val_)
        if r - 4 in aff_h:
            sch.wait("vector", "s_aff", aff_h[r - 4])
        for i in range(4):
            vop("bn_stats", bnb[pa][:, 6 * i:6 * (i + 1)],
                src_ap[:, 512 * i:512 * (i + 1)])
        vop("drain")
        vop("bn_aggr", mv[pa][:], bnb[pa][:])
        vop("drain")
        if r - 4 in rstd_h:
            sch.wait("vector", "s_rstd", rstd_h[r - 4])
        vop("tensor_scalar_add", vpe[pa][:], mv[pa][:, 1:2], EPS, sem="v_stat")
        stat_h[r] = sch.cnt["v_stat"]
        sch.wait("scalar", "v_stat", stat_h[r])
        if r - 4 in nmr_h:
            sch.wait("scalar", "v_nmr", nmr_h[r - 4])
        act("scalar", lnv[pa][:], vpe[pa][:], AF.Ln)
        act("scalar", rstd[pa][:], lnv[pa][:], AF.Exp, sem="s_rstd", scale=-0.5)
        rstd_h[r] = sch.cnt["s_rstd"]
        sch.wait("vector", "s_rstd", rstd_h[r])
        def f_nmr(E, o=nmr[pa][:], i0=mv[pa][:, 0:1], s1=rstd[pa][:]):
            return E.tensor_scalar(o, i0, s1, -1.0, ALU.mult, ALU.mult)
        sch.op("vector", f_nmr, "v_nmr", 1)
        nmr_h[r] = sch.cnt["v_nmr"]

    def ln_back(r, src_ap, dst_fn):
        pa = r % 4
        lnt = T[f"lnt{r % 3}"]
        sch.wait("scalar", "v_nmr", nmr_h[r])
        if r - 3 in lntp_h:
            sch.wait("scalar", "y_tp", lntp_h[r - 3])
        act("scalar", lnt[:], src_ap[:], AF.Identity, sem="s_aff",
            scale=rstd[pa][:], bias=nmr[pa][:])
        aff_h[r] = sch.cnt["s_aff"]
        sch.wait("sync", "s_aff", aff_h[r])
        lntp_h[r] = dst_fn(lnt)

    kv_ev = 0
    q_ev = 0
    qkv_chunk_h = {}

    def emit_chunk(gch):
        nonlocal kv_ev, q_ev
        is_q = gch >= 8
        xlk = T[f"xlk{gch % 2}"]
        sch.wait("tensor", "y_tp", lntp_h[4 * gch + 3])
        sch.wait("tensor", "ld_cst", n_cst)
        if not is_q:
            c = gch
            for ji, jt in enumerate((16, 17)):
                off = (c % 2) * 1024 + ji * 512
                bank = psum[:, off:off + 512]
                if kv_ev + 1 > 4:
                    sch.wait("tensor", "s_kv", kv_ev - 3)
                for d in range(DT):
                    mm(bank, T["wqkv_sb"][:, d, jt * P:(jt + 1) * P], xlk[:, d, :],
                       d == 0, d == DT - 1, sem="t_qkv" if d == DT - 1 else None)
                dst = (T["kt_sb"] if jt == 16 else T["vt_sb"])[:, c * 512:(c + 1) * 512]
                sch.wait("scalar", "t_qkv", sch.cnt["t_qkv"])
                act("scalar", dst, bank, AF.Identity, sem="s_kv",
                    bias=bqkv_sb[:, jt:jt + 1])
                kv_ev += 1
        else:
            qc = gch - 8
            for jt in range(16):
                n = qc * 16 + jt
                off = 2048 + (n % 4) * 512
                bank = psum[:, off:off + 512]
                if n >= 4:
                    sch.wait("tensor", "s_qe", n - 3)
                for d in range(DT):
                    mm(bank, T["wqkv_sb"][:, d, jt * P:(jt + 1) * P], xlk[:, d, :],
                       d == 0, d == DT - 1, sem="t_qkv" if d == DT - 1 else None)
                qb = T[f"qev{n % 4}"]
                sch.wait("scalar", "t_qkv", sch.cnt["t_qkv"])
                if n >= 4:
                    sch.wait("scalar", "st_q", 16 * (n - 3))
                act("scalar", qb[:], bank, AF.Identity, sem="s_qe",
                    bias=bqkv_sb[:, jt:jt + 1])
                q_ev += 1
                sch.wait("gpsimd", "s_qe", q_ev)
                dma("gpsimd", qt_hbm[jt, :, qc * 512:(qc + 1) * 512], qb[:], "st_q")
        qkv_chunk_h[gch] = sch.cnt["t_qkv"]

    def a_back(rb):
        xtb = T[f"xt{rb % 5}"]

        def dst_fn(lnt, cbuf=(rb // 4) % 2, tt=rb % 4, gch=rb // 4):
            if gch - 2 in qkv_chunk_h:
                sch.wait("sync", "t_qkv", qkv_chunk_h[gch - 2])
            return tpose(T[f"xlk{cbuf}"][:, :, tt * P:(tt + 1) * P], lnt[:])

        ln_back(rb, xtb, dst_fn)
        if rb % 4 == 3:
            emit_chunk(rb // 4)

    for r in range(RT + NSLOT):
        is_q = r >= RT
        xt = T[f"xt{r % 5}"]
        if r - 5 in aff_h:
            sch.wait("gpsimd", "s_aff", aff_h[r - 5])
            sch.wait("gpsimd", "v_stat", stat_h[r - 5])
        src = x_att[(r * P):(r + 1) * P, :] if not is_q else xq[r - RT, :, :]
        dma("gpsimd", xt[:], src, f"ld_x{r % 2}")
        ln_front(r, xt, [(f"ld_x{r % 2}", sch.cnt[f"ld_x{r % 2}"])])
        if r >= 2:
            a_back(r - 2)
    a_back(RT + NSLOT - 2)
    a_back(RT + NSLOT - 1)
    sch.wait("sync", "s_kv", kv_ev)
    v_tp_h = tpose(T["v_sb"][:], T["vt_sb"][:])
    sch.wait("gpsimd", "st_q", 16 * 64)
    phase_barrier()

    NSB = 6
    tenant = [0] * NSB
    nb = 0
    tp_h = {}
    xp_h = {}
    nm_h = {}
    mm_after = {}
    head_mm = {}
    pv_cnt = {}
    pe_cnt = {}
    slot_plan = {}
    PVB = [3072, 3584]

    def plan_slot(u):
        nonlocal nb
        i_loc = 2 * ((u // 2) % 8) + (u & 1)
        E = KE2[i_loc]
        fds = [512] * (E // 512) + ([256] if E % 512 else [])
        segs = []
        koff = 0
        k = 0
        while k < len(fds):
            L = min(4, NSB - nb, len(fds) - k)
            segs.append((nb, fds[k:k + L], koff))
            koff += sum(fds[k:k + L])
            nb = (nb + L) % NSB
            k += L
        sb, sfds, _ = segs[-1]
        dbo = sb * 512 + sum(sfds) - 256
        return [segs, dbo, E, 0]

    def b_slot(u, h):
        segs, dbo, E, _ = slot_plan[u]
        pa = u % 3
        prb = T[f"prb{pa}"]
        i_loc = 2 * ((u // 2) % 8) + (u & 1)
        if u - 3 in tp_h:
            sch.wait("scalar", "y_tp", tp_h[u - 3])
            sch.wait("scalar", "v_nm", nm_h[u - 3])
        nseg = 0
        for si, (b0, sfds, koff) in enumerate(segs):
            last = si == len(segs) - 1
            for ci, fd in enumerate(sfds):
                b = b0 + ci
                if tenant[b] > 0:
                    sch.wait("tensor", "s_xp", tenant[b])
                k0 = koff + sum(sfds[:ci])
                mm(psum[:, b * 512:b * 512 + fd],
                   T[f"qth{h % 2}"][:, i_loc * P:(i_loc + 1) * P],
                   T["kt_sb"][:, k0:k0 + fd], True, True, sem="t_mm")
            sch.wait("scalar", "t_mm", sch.cnt["t_mm"])
            fd = sum(sfds) - (256 if last else 0)
            if fd > 0:
                act("scalar", prb[:, koff:koff + fd],
                    psum[:, b0 * 512:b0 * 512 + fd],
                    AF.Exp, sem="s_xp", scale=SCALE,
                    accum_out=smc[pa][:, nseg:nseg + 1])
                nseg += 1
            if last:
                act("scalar", prb[:, E - 256:E], psum[:, dbo:dbo + 256],
                    AF.Exp, sem="s_xp", scale=SCALE)
            for ci in range(len(sfds)):
                tenant[b0 + ci] = sch.cnt["s_xp"]
        xp_h[u] = sch.cnt["s_xp"]
        mm_after[u] = sch.cnt["t_mm"]
        slot_plan[u][3] = nseg

    def b_post(u):
        segs, dbo, E, nseg = slot_plan[u]
        pa = u % 3
        prb = T[f"prb{pa}"]
        sch.wait("vector", "s_xp", xp_h[u])

        def f_stt(Eg, o=prb[:, E - 256:E], i1=T["cmask_sb"][:],
                  acc=smc[pa][:, nseg:nseg + 1]):
            return Eg.scalar_tensor_tensor(out=o, in0=o, scalar=1.0, in1=i1,
                                           op0=ALU.mult, op1=ALU.mult,
                                           accum_out=acc)
        sch.op("vector", f_stt, "v_dg", 1)
        vop("drain")
        vop("reduce_sum", zv[pa][:], smc[pa][:, 0:nseg + 1], AXX)
        vop("drain")
        vop("reciprocal", rcv[pa][:], zv[pa][:])
        vop("drain")
        if u % 2 == 0:
            vop("tensor_scalar_mul", prb[:, 0:E], prb[:, 0:E], rcv[pa][:])
            vop("memset", prb[:, E:E + 256], 0.0, sem="v_nm")
        else:
            vop("tensor_scalar_mul", prb[:, 0:E], prb[:, 0:E], rcv[pa][:],
                sem="v_nm")
        nm_h[u] = sch.cnt["v_nm"]

    def b_tpose(u):
        pa = u % 3
        j = u // 2
        E = slot_plan[u][2]
        Ep = E + 256 if u % 2 == 0 else E
        sch.wait("sync", "v_nm", nm_h[u])
        if j - 3 in pv_cnt:
            sch.wait("sync", "t_pv", pv_cnt[j - 3])
        tp_h[u] = tpose(T[f"pT{j % 3}"][:, 0:Ep // 128, (u & 1) * P:((u & 1) + 1) * P],
                        T[f"prb{pa}"][:, 0:Ep])

    def emit_pv(j):
        pr_ = j % 8
        nkt = 4 * (pr_ + 1)
        sch.wait("tensor", "y_tp", tp_h[2 * j + 1])
        if j - 2 in pe_cnt:
            sch.wait("tensor", "s_pe", pe_cnt[j - 2])
        pvb = psum[:, PVB[j % 2]:PVB[j % 2] + 256]
        for kt in range(nkt):
            mm(pvb, T["v_sb"][:, kt, :], T[f"pT{j % 3}"][:, kt, :],
               kt == 0, kt == nkt - 1, sem="t_pv" if kt == nkt - 1 else None)
        pv_cnt[j] = sch.cnt["t_pv"]

    def emit_pv_evac(j):
        h_, pr_ = j // 8, j % 8
        sch.wait("scalar", "t_pv", pv_cnt[j])
        pvb = psum[:, PVB[j % 2]:PVB[j % 2] + 256]
        act("scalar", T["attnT"][:, h_, pr_ * 256:(pr_ + 1) * 256], pvb,
            AF.Identity, sem="s_pe")
        pe_cnt[j] = sch.cnt["s_pe"]

    for idx in range(128):
        h, pr = idx // 8, idx % 8
        if pr == 0:
            if h >= 2:
                sch.wait("gpsimd", "t_mm", head_mm[h - 2])
            dma("gpsimd", T[f"qth{h % 2}"][:], qt_hbm[h, :, :], "ld_qh")
        if idx >= 2:
            emit_pv(idx - 2)
        if idx >= 3:
            emit_pv_evac(idx - 3)
        for s01 in range(2):
            u = 2 * idx + s01
            slot_plan[u] = plan_slot(u)
            sch.wait("tensor", "ld_qh", 16 * (h + 1))
            b_slot(u, h)
            b_post(u)
            b_tpose(u)
        if pr == 7:
            head_mm[h] = sch.cnt["t_mm"]
    emit_pv(126)
    emit_pv_evac(125)
    emit_pv(127)
    emit_pv_evac(126)
    emit_pv_evac(127)
    phase_barrier()

    pjd_h = []
    for sp in range(8):
        slots = (2 * sp, 2 * sp + 1)
        for sl01, sl in enumerate(slots):
            if sp >= 1:
                sch.wait("gpsimd", "v_pj", 2 * sp)
            dma("gpsimd", T[f"xt{sl01}"][:], xq[sl, :, :], f"ld_x{sl01}")
        for dd in range(DT):
            gdd = sp * DT + dd
            if gdd >= 3:
                sch.wait("gpsimd", "t_pjd", pjd_h[gdd - 3])
            dma("gpsimd", T[f"wpb{dd % 3}"][:], wp[dd * P:(dd + 1) * P, :],
                f"ld_wp{dd % 3}")
            sch.wait("tensor", f"ld_wp{dd % 3}", sch.cnt[f"ld_wp{dd % 3}"])
            if sp >= 1 and dd == 0:
                sch.wait("tensor", "v_pj", 2 * sp)
            for sl01 in range(2):
                for jc in range(4):
                    mm(psum[:, sl01 * 2048 + jc * 512: sl01 * 2048 + jc * 512 + 512],
                       T["attnT"][:, dd, slots[sl01] * P:(slots[sl01] + 1) * P],
                       T[f"wpb{dd % 3}"][:, jc * 512:(jc + 1) * 512],
                       dd == 0, dd == DT - 1,
                       sem="t_pjd" if (sl01 == 1 and jc == 3) else None)
            pjd_h.append(sch.cnt["t_pjd"])
        for sl01, sl in enumerate(slots):
            r = RT + NSLOT + sl
            x1t = T[f"x1t{sl % 3}"]
            sch.wait("vector", "t_pjd", DT * (sp + 1))
            sch.wait("vector", f"ld_x{sl01}", sch.cnt[f"ld_x{sl01}"])
            if sl - 3 >= 0:
                sch.wait("vector", "st_x1", 16 * (sl - 2))
                if r - 3 in aff_h:
                    sch.wait("vector", "s_aff", aff_h[r - 3])
            vop("tensor_add", x1t[:], psum[:, sl01 * 2048:sl01 * 2048 + 2048],
                T[f"xt{sl01}"][:])
            vop("tensor_add", x1t[:], x1t[:], bproj_sb[:], sem="v_pj")
            sch.wait("gpsimd", "v_pj", sch.cnt["v_pj"])
            dma("gpsimd", x1_hbm[sl, :, :], x1t[:], "st_x1")
            ln_front(r, x1t, [("v_pj", sch.cnt["v_pj"])])
            if sl >= 2:
                rb = r - 2

                def dst2(lnt, sl2=sl - 2):
                    return tpose(T["xl2T"][:, :, sl2 * P:(sl2 + 1) * P], lnt[:])

                ln_back(rb, T[f"x1t{(sl - 2) % 3}"], dst2)
    for sl2 in (NSLOT - 2, NSLOT - 1):
        rb = RT + NSLOT + sl2

        def dst3(lnt, sl2=sl2):
            return tpose(T["xl2T"][:, :, sl2 * P:(sl2 + 1) * P], lnt[:])

        ln_back(rb, T[f"x1t{sl2 % 3}"], dst3)
    sch.wait("gpsimd", "st_x1", 16 * 16)
    phase_barrier()

    gu_grp = 0
    gutp_h = {}
    wdtp_h = {}
    dq_h = {}

    def d_load(ht):
        pa = ht % 2
        if ht >= 2:
            sch.wait("gpsimd", "s_dq", dq_h.get(ht - 2, 0))
        dma("gpsimd", T[f"wgr{pa}"][:], wg[ht * P:(ht + 1) * P, :], "ld_wg")
        dma("gpsimd", T[f"wur{pa}"][:], wu[ht * P:(ht + 1) * P, :], "ld_wu")

    def d_prep(ht):
        pa = ht % 2
        sch.wait("scalar", "ld_wg", 16 * (ht + 1))
        sch.wait("scalar", "ld_wu", 16 * (ht + 1))
        sch.wait("scalar", "v_stat", 3)
        if ht - 2 in gutp_h:
            sch.wait("scalar", "y_tp", gutp_h[ht - 2])
        act("scalar", T[f"wgd{pa}"][:], T[f"wgr{pa}"][:], AF.Identity, sem="s_dq",
            scale=sg_sb[:, ht:ht + 1], bias=sgb_sb[:, ht:ht + 1])
        act("scalar", T[f"wud{pa}"][:], T[f"wur{pa}"][:], AF.Identity, sem="s_dq",
            scale=su_sb[:, ht:ht + 1], bias=sub_sb[:, ht:ht + 1])
        dq_h[ht] = sch.cnt["s_dq"]
        sch.wait("sync", "s_dq", dq_h[ht])
        if ht >= 2:
            sch.wait("sync", "t_gu", 4 * (ht - 1))
        tpose(T[f"wgT{pa}"][:], T[f"wgd{pa}"][:])
        gutp_h[ht] = tpose(T[f"wuT{pa}"][:], T[f"wud{pa}"][:])

    d_load(0)
    d_prep(0)
    for ht in range(HT):
        pa = ht % 2
        if ht + 1 < HT:
            d_load(ht + 1)
            d_prep(ht + 1)
        for hh in range(2):
            goff = (gu_grp % 2) * 2048
            sch.wait("tensor", "y_tp", gutp_h[ht])
            if gu_grp >= 2:
                sch.wait("tensor", "s_sil", 2 * (gu_grp - 1))
                sch.wait("tensor", "v_act", 2 * (gu_grp - 1))
            for d in range(DT):
                for ci, cc in enumerate((2 * hh, 2 * hh + 1)):
                    mm(psum[:, goff + ci * 512:goff + ci * 512 + 512],
                       T[f"wgT{pa}"][:, d, :], T["xl2T"][:, d, cc * 512:(cc + 1) * 512],
                       d == 0, d == DT - 1)
                for ci, cc in enumerate((2 * hh, 2 * hh + 1)):
                    mm(psum[:, goff + 1024 + ci * 512:goff + 1024 + ci * 512 + 512],
                       T[f"wuT{pa}"][:, d, :], T["xl2T"][:, d, cc * 512:(cc + 1) * 512],
                       d == 0, d == DT - 1,
                       sem="t_gu" if (d == DT - 1 and ci == 1) else None)
            for ci, cc in enumerate((2 * hh, 2 * hh + 1)):
                sch.wait("scalar", "t_gu", sch.cnt["t_gu"])
                if gu_grp >= 2:
                    sch.wait("scalar", "v_act", 2 * (gu_grp - 1))
                act("scalar", T[f"sil{cc}"][:],
                    psum[:, goff + ci * 512:goff + ci * 512 + 512],
                    AF.Silu, sem="s_sil")
                sch.wait("vector", "s_sil", sch.cnt["s_sil"])
                if ht >= 2 and hh == 0 and ci == 0:
                    sch.wait("vector", "st_act", 16 * (ht - 1))
                def f_mul(E, o=T[f"actt{pa}"][:, cc * 512:(cc + 1) * 512],
                          i0=T[f"sil{cc}"][:],
                          i1=psum[:, goff + 1024 + ci * 512:goff + 1024 + ci * 512 + 512]):
                    return E.tensor_mul(o, i0, i1)
                sch.op("vector", f_mul, "v_act", 1)
            gu_grp += 1
        sch.wait("gpsimd", "v_act", sch.cnt["v_act"])
        dma("gpsimd", act_hbm[ht, :, :], T[f"actt{pa}"][:], "st_act")
        if ht >= 4 and ht % 2 == 0 and (ht - 4) // 2 < JT:
            jt = (ht - 4) // 2
            if jt >= 1:
                sch.wait("gpsimd", "s_dqd", jt)
            dma("gpsimd", T["wdr"][:], wd[jt * P:(jt + 1) * P, :], "ld_wd",
                max_dma_last_dim=8192)
            sch.wait("scalar", "ld_wd", 16 * (jt + 1))
            if jt - 1 in wdtp_h:
                sch.wait("scalar", "y_tp", wdtp_h[jt - 1])
            act("scalar", T["wdd"][:], T["wdr"][:], AF.Identity, sem="s_dqd",
                scale=sd_sb[:, jt:jt + 1], bias=sdb_sb[:, jt:jt + 1])
            sch.wait("sync", "s_dqd", sch.cnt["s_dqd"])
            if jt >= 1:
                sch.wait("sync", "st_wdt", 16 * jt)
            wdtp_h[jt] = tpose(T["wdTt"][:], T["wdd"][:])
            sch.wait("gpsimd", "y_tp", wdtp_h[jt])
            dma("gpsimd", wdt_hbm[jt, :, :], T["wdTt"][:], "st_wdt")
    sch.wait("gpsimd", "st_act", 16 * HT)
    sch.wait("gpsimd", "st_wdt", 16 * JT)
    phase_barrier()

    detp_h = {}
    dn_after = {}
    wdtb_cnt = {}
    for hh in range(2):
        if hh == 1:
            sch.wait("gpsimd", "st_out", 16 * 8)
            sch.wait("gpsimd", "v_oadd", sch.cnt["v_oadd"])
        for jt0 in range(2):
            g0 = hh * JT + jt0
            if g0 - 2 in dn_after:
                sch.wait("gpsimd", "t_dn", dn_after[g0 - 2])
            dma("gpsimd", T[f"wdTb{g0 % 2}"][:], wdt_hbm[jt0, :, :],
                f"ld_wdt{g0 % 2}")
            wdtb_cnt[g0] = sch.cnt[f"ld_wdt{g0 % 2}"]
        for ht in range(HT):
            dma("gpsimd", T["actH"][:, ht, :],
                act_hbm[ht, :, hh * 1024:(hh + 1) * 1024], "ld_act")
        for sl in range(8):
            dma("gpsimd", T["out_sb"][:, sl, :], x1_hbm[hh * 8 + sl, :, :], "ld_out")
        for jt in range(JT):
            g = hh * JT + jt
            pa = g % 2
            if jt >= 2:
                sch.wait("gpsimd", "t_dn", dn_after[g - 2])
                dma("gpsimd", T[f"wdTb{pa}"][:], wdt_hbm[jt, :, :],
                    f"ld_wdt{pa}")
                wdtb_cnt[g] = sch.cnt[f"ld_wdt{pa}"]
            sch.wait("tensor", f"ld_wdt{pa}", wdtb_cnt[g])
            if jt > 0 or not E_INCR:
                sch.wait("tensor", "ld_act", 16 * (hh + 1) * HT)
            if g >= 2:
                sch.wait("tensor", "s_de", g - 1)
            for ht in range(HT):
                if jt == 0 and E_INCR:
                    sch.wait("tensor", "ld_act", 16 * (hh * HT + ht + 1))
                for ci in range(2):
                    mm(psum[:, pa * 1024 + ci * 512:pa * 1024 + ci * 512 + 512],
                       T[f"wdTb{pa}"][:, ht, :],
                       T["actH"][:, ht, ci * 512:(ci + 1) * 512],
                       ht == 0, ht == HT - 1,
                       sem="t_dn" if (ht == HT - 1 and ci == 1) else None)
            dn_after[g] = sch.cnt["t_dn"]
            sch.wait("scalar", "t_dn", sch.cnt["t_dn"])
            if g - 2 in detp_h:
                sch.wait("scalar", "y_tp", detp_h[g - 2])
            act("scalar", T[f"dTt{pa}"][:], psum[:, pa * 1024:pa * 1024 + 1024],
                AF.Identity, sem="s_de")
            sch.wait("sync", "s_de", sch.cnt["s_de"])
            if g >= 2:
                sch.wait("sync", "v_oadd", 8 * (g - 1))
            detp_h[g] = tpose(T[f"dnt{pa}"][:], T[f"dTt{pa}"][:])
            sch.wait("vector", "y_tp", detp_h[g])
            sch.wait("vector", "ld_out", 16 * (hh + 1) * 8)
            for nt in range(8):
                o = T["out_sb"][:, nt, jt * P:jt * P + P]
                def f_add(E, o=o, i1=T[f"dnt{pa}"][:, nt, :]):
                    return E.tensor_add(o, o, i1)
                sch.op("vector", f_add, "v_oadd", 1)
        sch.wait("gpsimd", "v_oadd", sch.cnt["v_oadd"])
        for sl in range(8):
            dma("gpsimd", out_h[hh * 8 + sl, :, :], T["out_sb"][:, sl, :], "st_out")
    sch.wait("gpsimd", "st_out", 16 * 16)

    global _LAST_SCHED
    _LAST_SCHED = sch
    sems = {}
    for name in sch.sem_names:
        sems[name] = e(nc.semaphore(name))
    block = e(nc.Block())

    def emit(E, stream):
        for item in stream:
            if item[0] == "w":
                E.wait_ge(sems[item[1]], item[2])
            else:
                _, fn, inc_sem, inc_amt = item
                inst = fn(E)
                if inc_sem is not None and inst is not None:
                    inst.then_inc(sems[inc_sem], inc_amt)

    @block.tensor
    def _(t):
        emit(t, sch.streams["tensor"])

    @block.scalar
    def _(sc):
        emit(sc, sch.streams["scalar"])

    @block.vector
    def _(v):
        emit(v, sch.streams["vector"])

    @block.sync
    def _(sy):
        emit(sy, sch.streams["sync"])

    @block.gpsimd
    def _(g):
        emit(g, sch.streams["gpsimd"])

    ctx.close()
    return nc


_LAST_SCHED = None
_NC_CACHE = None


def _get_nc():
    global _NC_CACHE
    if _NC_CACHE is None:
        _NC_CACHE = build_program()
    return _NC_CACHE


def kernel(L0_attn_x, L0_attn_W_qkv, L0_attn_b_qkv, L0_attn_W_proj, L0_attn_b_proj,
           L0_ffn_wq_gate, L0_ffn_s_gate, L0_ffn_wq_up, L0_ffn_s_up,
           L0_ffn_wq_down, L0_ffn_s_down):
    x = np.ascontiguousarray(np.asarray(L0_attn_x, dtype=np.float32))
    wqkv = np.ascontiguousarray(np.asarray(L0_attn_W_qkv, dtype=np.float32))
    bq = np.asarray(L0_attn_b_qkv, dtype=np.float32)
    wpn = np.ascontiguousarray(np.asarray(L0_attn_W_proj, dtype=np.float32))
    bp = np.asarray(L0_attn_b_proj, dtype=np.float32)
    wgn = np.ascontiguousarray(np.asarray(L0_ffn_wq_gate, dtype=np.int32))
    sgn = np.asarray(L0_ffn_s_gate, dtype=np.float32)
    wun = np.ascontiguousarray(np.asarray(L0_ffn_wq_up, dtype=np.int32))
    sun = np.asarray(L0_ffn_s_up, dtype=np.float32)
    wdn = np.ascontiguousarray(np.asarray(L0_ffn_wq_down, dtype=np.int32))
    sdn = np.asarray(L0_ffn_s_down, dtype=np.float32)

    bqkv_t = np.ascontiguousarray(bq.reshape(JQKV, P).T)
    bproj_t = np.ascontiguousarray(np.tile(bp[None, :], (P, 1)))
    sg_t = np.ascontiguousarray(sgn.reshape(HT, P).T)
    su_t = np.ascontiguousarray(sun.reshape(HT, P).T)
    sd_t = np.ascontiguousarray(sdn.reshape(JT, P).T)

    nc = _get_nc()
    in_maps = []
    metas = []
    for c in range(8):
        b = c // 2
        par = c % 2
        gl = [2 * i + par for i in range(NSLOT)]
        xb = x[b]
        xq_c = np.ascontiguousarray(xb.reshape(RT, P, D)[gl])
        tri = (np.arange(128)[None, :] <= np.arange(P)[:, None]).astype(np.float32)
        cm = np.zeros((P, 256), dtype=np.float32)
        if par == 0:
            cm[:, 0:128] = tri
        else:
            cm[:, 0:128] = 1.0
            cm[:, 128:256] = tri
        cmb = cm.astype(ml_dtypes.bfloat16)
        in_maps.append({
            "x_att": xb, "xq": xq_c, "wqkv": wqkv, "bqkv": bqkv_t,
            "wp": wpn, "bproj": bproj_t, "wg": wgn, "wu": wun, "wd": wdn,
            "sg": sg_t, "su": su_t, "sd": sd_t, "cmask": cmb,
        })
        metas.append((b, gl))

    res = run_bass_kernel_spmd(nc, in_maps, core_ids=list(range(8)))

    out = np.empty((B, S, D), dtype=np.float32)
    for c in range(8):
        b, gl = metas[c]
        oc = np.asarray(res.results[c]["out"])
        for i, g in enumerate(gl):
            out[b, g * P:(g + 1) * P, :] = oc[i]
    return out


# version 21
# speedup vs baseline: 1.1553x; 1.0354x over previous
import numpy as np
import ml_dtypes
import concourse.bass as bass
import concourse.mybir as mybir
from concourse.bass_utils import run_bass_kernel_spmd

F32 = mybir.dt.float32
BF16 = mybir.dt.bfloat16
I32 = mybir.dt.int32
AF = mybir.ActivationFunctionType
ALU = mybir.AluOpType
AXX = mybir.AxisListType.X

P = 128
B = 4
S = 4096
D = 2048
DT = D // P
NH = 16
HD = 128
JQKV = 18
HID = 5632
HT = HID // P
JT = D // P
NSLOT = 16
RT = S // P
SCALE = float(HD) ** -0.5
EPS = 1e-5

KE2 = [256 * (i + 1) for i in range(NSLOT)]

ENGS = ("tensor", "scalar", "vector", "sync", "gpsimd")
E_PRELOAD = False
E_INCR = True


class Sched:
    def __init__(self):
        self.streams = {e: [] for e in ENGS}
        self.cnt = {}
        self.sem_names = []

    def sem(self, name):
        if name not in self.cnt:
            self.cnt[name] = 0
            self.sem_names.append(name)

    def wait(self, eng, sem, val):
        val = min(val, self.cnt[sem])
        if val <= 0:
            return
        self.streams[eng].append(("w", sem, val))

    def inc(self, sem, amt):
        self.cnt[sem] += amt
        return self.cnt[sem]

    def op(self, eng, fn, inc_sem=None, inc_amt=1):
        if inc_sem is not None:
            self.inc(inc_sem, inc_amt)
        self.streams[eng].append(("i", fn, inc_sem, inc_amt))


def build_program():
    nc = bass.Bass("TRN2", target_bir_lowering=False)

    x_att = nc.dram_tensor("x_att", [S, D], F32, kind="ExternalInput").ap()
    xq = nc.dram_tensor("xq", [NSLOT, P, D], F32, kind="ExternalInput").ap()
    wqkv = nc.dram_tensor("wqkv", [D, 2304], F32, kind="ExternalInput").ap()
    bqkv = nc.dram_tensor("bqkv", [P, JQKV], F32, kind="ExternalInput").ap()
    wp = nc.dram_tensor("wp", [D, D], F32, kind="ExternalInput").ap()
    bproj = nc.dram_tensor("bproj", [P, D], F32, kind="ExternalInput").ap()
    wg = nc.dram_tensor("wg", [HID, D], I32, kind="ExternalInput").ap()
    wu = nc.dram_tensor("wu", [HID, D], I32, kind="ExternalInput").ap()
    wd = nc.dram_tensor("wd", [D, HID], I32, kind="ExternalInput").ap()
    sg = nc.dram_tensor("sg", [P, HT], F32, kind="ExternalInput").ap()
    su = nc.dram_tensor("su", [P, HT], F32, kind="ExternalInput").ap()
    sd = nc.dram_tensor("sd", [P, JT], F32, kind="ExternalInput").ap()
    cmask_h = nc.dram_tensor("cmask", [P, 256], BF16, kind="ExternalInput").ap()
    out_h = nc.dram_tensor("out", [NSLOT, P, D], F32, kind="ExternalOutput").ap()

    qt_hbm = nc.dram_tensor("qt_hbm", [NH, P, 2048], BF16, kind="Internal").ap()
    wpbf_hbm = nc.dram_tensor("wpbf_hbm", [DT, P, D], BF16, kind="Internal").ap()
    x1_hbm = nc.dram_tensor("x1_hbm", [NSLOT, P, D], F32, kind="Internal").ap()
    act_hbm = nc.dram_tensor("act_hbm", [HT, P, 2048], BF16, kind="Internal").ap()
    wdt_hbm = nc.dram_tensor("wdt_hbm", [JT, P, HT * P], BF16, kind="Internal").ap()

    from contextlib import ExitStack
    ctx = ExitStack()
    e = ctx.enter_context
    bqkv_sb = e(nc.sbuf_tensor("bqkv_sb", [P, JQKV], F32))
    bproj_sb = e(nc.sbuf_tensor("bproj_sb", [P, D], F32))
    sg_sb = e(nc.sbuf_tensor("sg_sb", [P, HT], F32))
    su_sb = e(nc.sbuf_tensor("su_sb", [P, HT], F32))
    sd_sb = e(nc.sbuf_tensor("sd_sb", [P, JT], F32))
    sgb_sb = e(nc.sbuf_tensor("sgb_sb", [P, HT], F32))
    sub_sb = e(nc.sbuf_tensor("sub_sb", [P, HT], F32))
    sdb_sb = e(nc.sbuf_tensor("sdb_sb", [P, JT], F32))
    bnb = [e(nc.sbuf_tensor(f"bnb{i}", [P, 24], F32)) for i in range(4)]
    mv = [e(nc.sbuf_tensor(f"mv{i}", [P, 2], F32)) for i in range(4)]
    vpe = [e(nc.sbuf_tensor(f"vpe{i}", [P, 1], F32)) for i in range(4)]
    lnv = [e(nc.sbuf_tensor(f"lnv{i}", [P, 1], F32)) for i in range(4)]
    rstd = [e(nc.sbuf_tensor(f"rstd{i}", [P, 1], F32)) for i in range(4)]
    nmr = [e(nc.sbuf_tensor(f"nmr{i}", [P, 1], F32)) for i in range(4)]
    smc = [e(nc.sbuf_tensor(f"smc{i}", [P, 8], F32)) for i in range(3)]
    zv = [e(nc.sbuf_tensor(f"zv{i}", [P, 1], F32)) for i in range(3)]
    rcv = [e(nc.sbuf_tensor(f"rcv{i}", [P, 1], F32)) for i in range(3)]

    SEG = {
        "wqkv_sb": ("A", [P, DT, 2304], BF16),
        "xt0": ("AC", [P, D], F32), "xt1": ("AC", [P, D], F32),
        "xt2": ("A", [P, D], F32), "xt3": ("A", [P, D], F32),
        "xt4": ("A", [P, D], F32),
        "lnt0": ("AC", [P, D], BF16), "lnt1": ("AC", [P, D], BF16),
        "lnt2": ("AC", [P, D], BF16),
        "xlk0": ("A", [P, DT, 512], BF16), "xlk1": ("A", [P, DT, 512], BF16),
        "kt_sb": ("AB", [P, S], BF16),
        "vt_sb": ("A", [P, S], BF16),
        "v_sb": ("AB", [P, RT, P], BF16),
        "qev0": ("A", [P, 512], BF16), "qev1": ("A", [P, 512], BF16),
        "qev2": ("A", [P, 512], BF16), "qev3": ("A", [P, 512], BF16),
        "wpc0": ("A", [P, D], BF16), "wpc1": ("A", [P, D], BF16),
        "cmask_sb": ("AB", [P, 256], BF16),
        "qth0": ("B", [P, 2048], BF16), "qth1": ("B", [P, 2048], BF16),
        "prb0": ("B", [P, 4096], BF16), "prb1": ("B", [P, 4096], BF16),
        "prb2": ("B", [P, 4096], BF16),
        "pT0": ("B", [P, 32, 256], BF16), "pT1": ("B", [P, 32, 256], BF16),
        "pT2": ("B", [P, 32, 256], BF16),
        "attnT": ("BC", [P, NH, 2048], BF16),
        "wpb0": ("C", [P, D], BF16), "wpb1": ("C", [P, D], BF16),
        "wpb2": ("C", [P, D], BF16),
        "x1t0": ("C", [P, D], F32), "x1t1": ("C", [P, D], F32),
        "x1t2": ("C", [P, D], F32),
        "xl2T": ("CD", [P, DT, 2048], BF16),
        "wgr0": ("D", [P, D], BF16), "wgr1": ("D", [P, D], BF16),
        "wur0": ("D", [P, D], BF16), "wur1": ("D", [P, D], BF16),
        "wgd0": ("D", [P, D], BF16), "wgd1": ("D", [P, D], BF16),
        "wud0": ("D", [P, D], BF16), "wud1": ("D", [P, D], BF16),
        "wgT0": ("D", [P, DT, P], BF16), "wgT1": ("D", [P, DT, P], BF16),
        "wuT0": ("D", [P, DT, P], BF16), "wuT1": ("D", [P, DT, P], BF16),
        "sil0": ("D", [P, 512], BF16), "sil1": ("D", [P, 512], BF16),
        "sil2": ("D", [P, 512], BF16), "sil3": ("D", [P, 512], BF16),
        "actt0": ("D", [P, 2048], BF16), "actt1": ("D", [P, 2048], BF16),
        "wdr": ("D", [P, HID], BF16),
        "wdd": ("D", [P, HID], BF16),
        "wdTt": ("D", [P, HT, P], BF16),
        "actH": ("E", [P, HT, 1024], BF16),
        "out_sb": ("E", [P, 8, D], F32),
        "wdTb0": ("E", [P, HT, P], BF16), "wdTb1": ("E", [P, HT, P], BF16),
        "wdTb2": ("E", [P, HT, P], BF16),
        "dTt0": ("E", [P, 1024], BF16), "dTt1": ("E", [P, 1024], BF16),
        "dnt0": ("E", [P, 8, P], BF16), "dnt1": ("E", [P, 8, P], BF16),
    }

    def seg_bytes(shape, dtype):
        n = 1
        for s_ in shape[1:]:
            n *= s_
        return n * mybir.dt.size(dtype)

    order = sorted(SEG.items(), key=lambda kv: -seg_bytes(kv[1][1], kv[1][2]))
    placed = []
    offsets = {}
    for name, (phases, shape, dtype) in order:
        sz = (seg_bytes(shape, dtype) + 63) // 64 * 64
        cands = sorted(set([0] + [pe for (pp, po, pe) in placed if set(pp) & set(phases)]))
        for off in cands:
            end = off + sz
            if all((not (set(pp) & set(phases))) or end <= po or off >= pe
                   for (pp, po, pe) in placed):
                placed.append((phases, off, end))
                offsets[name] = off
                break
        else:
            raise RuntimeError("no fit " + name)
    arena_sz = max(pe for (_, _, pe) in placed)
    assert arena_sz <= 202364, f"arena {arena_sz}"
    arena = e(nc.sbuf_tensor("arena", [P, arena_sz], mybir.dt.uint8))
    arena_addr = nc.lookup_mloc(arena).addr

    T = {}
    for name, (phases, shape, dtype) in SEG.items():
        h = nc.virtual_sbuf_tensor(name, shape, dtype)
        nc._tensor(h.name, h.shape, h.dtype, type="SB")
        mloc = nc.lookup_mloc(h)
        mloc.allocated = True
        mloc.addr = arena_addr + offsets[name]
        T[name] = h.ap()

    psum = e(nc.psum_tensor("psum", [P, 4096], F32))

    sch = Sched()
    for s_ in ("ld_cst", "ld_x0", "ld_x1", "ld_qh", "ld_wp0", "ld_wp1", "ld_wp2",
               "ld_wp3", "ld_wg", "ld_wu", "ld_wd", "ld_wdt0", "ld_wdt1",
               "ld_act", "ld_out", "ld_wpc", "st_wpc", "ld_wdt2",
               "st_q", "st_x1", "st_act", "st_wdt", "st_out",
               "v_stat", "v_nmr", "v_dg", "v_mx", "v_nm", "v_act", "v_pj", "v_oadd",
               "s_rstd", "s_aff", "s_ev", "s_xp", "s_pe", "s_sil", "s_dq", "s_dqd",
               "s_de", "s_kv", "s_qe",
               "t_mm", "t_qkv", "t_pv", "t_pj", "t_pjd", "t_gu", "t_dn",
               "y_tp", "ph"):
        sch.sem(s_)

    def dma(eng, dst, src, sem, ser=True, **kw):
        if ser and sch.cnt[sem] > 0:
            sch.wait(eng, sem, sch.cnt[sem])
        def fn(E, dst=dst, src=src, kw=kw):
            return E.dma_start(dst, src, **kw)
        sch.op(eng, fn, sem, 16)

    def tpose(dst, src):
        if sch.cnt["y_tp"] > 0:
            sch.wait("sync", "y_tp", sch.cnt["y_tp"])
        def fn(E, dst=dst, src=src):
            return E.dma_start(dst, src, transpose=True)
        sch.op("sync", fn, "y_tp", 16)
        return sch.cnt["y_tp"]

    def act(eng, out, in_, func, sem=None, **kw):
        def fn(E, out=out, in_=in_, func=func, kw=kw):
            return E.activation(out, in_, func, **kw)
        sch.op(eng, fn, sem, 1)

    def mm(out, lhsT, rhs, start, stop, sem=None):
        def fn(E, out=out, lhsT=lhsT, rhs=rhs, start=start, stop=stop):
            return E.matmul(out, lhsT, rhs, start=start, stop=stop)
        sch.op("tensor", fn, sem, 1)

    def vop(meth, *args, sem=None, **kw):
        def fn(E, meth=meth, args=args, kw=kw):
            return getattr(E, meth)(*args, **kw)
        sch.op("vector", fn, sem, 1)

    def phase_barrier():
        for eng in ENGS:
            def fn2(E):
                return E.nop()
            sch.op(eng, fn2, "ph", 1)
        tgt = sch.cnt["ph"]
        for eng in ENGS:
            sch.wait(eng, "ph", tgt)

    dma("gpsimd", bqkv_sb[:], bqkv[:], "ld_cst")
    dma("gpsimd", bproj_sb[:], bproj[:], "ld_cst")
    dma("gpsimd", sg_sb[:], sg[:], "ld_cst")
    dma("gpsimd", su_sb[:], su[:], "ld_cst")
    dma("gpsimd", sd_sb[:], sd[:], "ld_cst")
    dma("gpsimd", T["cmask_sb"][:], cmask_h[:], "ld_cst")
    for t in range(DT):
        dma("gpsimd", T["wqkv_sb"][:, t, :], wqkv[t * P:(t + 1) * P, :], "ld_cst")
    n_cst = sch.cnt["ld_cst"]
    sch.wait("vector", "ld_cst", n_cst)
    vop("tensor_scalar_mul", sgb_sb[:], sg_sb[:], -128.0, sem="v_stat")
    vop("tensor_scalar_mul", sub_sb[:], su_sb[:], -128.0, sem="v_stat")
    vop("tensor_scalar_mul", sdb_sb[:], sd_sb[:], -128.0, sem="v_stat")

    aff_h, rstd_h, nmr_h, stat_h, lntp_h = {}, {}, {}, {}, {}

    def ln_front(r, src_ap, gate):
        pa = r % 4
        for sem_, val_ in gate:
            sch.wait("vector", sem_, val_)
        if r - 4 in aff_h:
            sch.wait("vector", "s_aff", aff_h[r - 4])
        for i in range(4):
            vop("bn_stats", bnb[pa][:, 6 * i:6 * (i + 1)],
                src_ap[:, 512 * i:512 * (i + 1)])
        vop("drain")
        vop("bn_aggr", mv[pa][:], bnb[pa][:])
        vop("drain")
        if r - 4 in rstd_h:
            sch.wait("vector", "s_rstd", rstd_h[r - 4])
        vop("tensor_scalar_add", vpe[pa][:], mv[pa][:, 1:2], EPS, sem="v_stat")
        stat_h[r] = sch.cnt["v_stat"]
        sch.wait("scalar", "v_stat", stat_h[r])
        if r - 4 in nmr_h:
            sch.wait("scalar", "v_nmr", nmr_h[r - 4])
        act("scalar", lnv[pa][:], vpe[pa][:], AF.Ln)
        act("scalar", rstd[pa][:], lnv[pa][:], AF.Exp, sem="s_rstd", scale=-0.5)
        rstd_h[r] = sch.cnt["s_rstd"]
        sch.wait("vector", "s_rstd", rstd_h[r])
        def f_nmr(E, o=nmr[pa][:], i0=mv[pa][:, 0:1], s1=rstd[pa][:]):
            return E.tensor_scalar(o, i0, s1, -1.0, ALU.mult, ALU.mult)
        sch.op("vector", f_nmr, "v_nmr", 1)
        nmr_h[r] = sch.cnt["v_nmr"]

    def ln_back(r, src_ap, dst_fn):
        pa = r % 4
        lnt = T[f"lnt{r % 3}"]
        sch.wait("scalar", "v_nmr", nmr_h[r])
        if r - 3 in lntp_h:
            sch.wait("scalar", "y_tp", lntp_h[r - 3])
        act("scalar", lnt[:], src_ap[:], AF.Identity, sem="s_aff",
            scale=rstd[pa][:], bias=nmr[pa][:])
        aff_h[r] = sch.cnt["s_aff"]
        sch.wait("sync", "s_aff", aff_h[r])
        lntp_h[r] = dst_fn(lnt)

    kv_ev = 0
    q_ev = 0
    qkv_chunk_h = {}

    def emit_chunk(gch):
        nonlocal kv_ev, q_ev
        is_q = gch >= 8
        xlk = T[f"xlk{gch % 2}"]
        sch.wait("tensor", "y_tp", lntp_h[4 * gch + 3])
        sch.wait("tensor", "ld_cst", n_cst)
        if not is_q:
            c = gch
            for ji, jt in enumerate((16, 17)):
                off = (c % 2) * 1024 + ji * 512
                bank = psum[:, off:off + 512]
                if kv_ev + 1 > 4:
                    sch.wait("tensor", "s_kv", kv_ev - 3)
                for d in range(DT):
                    mm(bank, T["wqkv_sb"][:, d, jt * P:(jt + 1) * P], xlk[:, d, :],
                       d == 0, d == DT - 1, sem="t_qkv" if d == DT - 1 else None)
                dst = (T["kt_sb"] if jt == 16 else T["vt_sb"])[:, c * 512:(c + 1) * 512]
                sch.wait("scalar", "t_qkv", sch.cnt["t_qkv"])
                act("scalar", dst, bank, AF.Identity, sem="s_kv",
                    bias=bqkv_sb[:, jt:jt + 1])
                kv_ev += 1
        else:
            qc = gch - 8
            for jt in range(16):
                n = qc * 16 + jt
                off = 2048 + (n % 4) * 512
                bank = psum[:, off:off + 512]
                if n >= 4:
                    sch.wait("tensor", "s_qe", n - 3)
                for d in range(DT):
                    mm(bank, T["wqkv_sb"][:, d, jt * P:(jt + 1) * P], xlk[:, d, :],
                       d == 0, d == DT - 1, sem="t_qkv" if d == DT - 1 else None)
                qb = T[f"qev{n % 4}"]
                sch.wait("scalar", "t_qkv", sch.cnt["t_qkv"])
                if n >= 4:
                    sch.wait("scalar", "st_q", 16 * (n - 3))
                act("scalar", qb[:], bank, AF.Identity, sem="s_qe",
                    bias=bqkv_sb[:, jt:jt + 1])
                q_ev += 1
                sch.wait("gpsimd", "s_qe", q_ev)
                dma("gpsimd", qt_hbm[jt, :, qc * 512:(qc + 1) * 512], qb[:], "st_q")
        qkv_chunk_h[gch] = sch.cnt["t_qkv"]

    def a_back(rb):
        xtb = T[f"xt{rb % 5}"]

        def dst_fn(lnt, cbuf=(rb // 4) % 2, tt=rb % 4, gch=rb // 4):
            if gch - 2 in qkv_chunk_h:
                sch.wait("sync", "t_qkv", qkv_chunk_h[gch - 2])
            return tpose(T[f"xlk{cbuf}"][:, :, tt * P:(tt + 1) * P], lnt[:])

        ln_back(rb, xtb, dst_fn)
        if rb % 4 == 3:
            emit_chunk(rb // 4)

    for r in range(RT + NSLOT):
        is_q = r >= RT
        xt = T[f"xt{r % 5}"]
        if r - 5 in aff_h:
            sch.wait("gpsimd", "s_aff", aff_h[r - 5])
            sch.wait("gpsimd", "v_stat", stat_h[r - 5])
        src = x_att[(r * P):(r + 1) * P, :] if not is_q else xq[r - RT, :, :]
        dma("gpsimd", xt[:], src, f"ld_x{r % 2}")
        if r < DT:
            if r >= 2:
                sch.wait("gpsimd", "st_wpc", 16 * (r - 1))
            dma("gpsimd", T[f"wpc{r % 2}"][:], wp[r * P:(r + 1) * P, :], "ld_wpc")
            if r >= 1:
                sch.wait("gpsimd", "ld_wpc", 16 * r)
                dma("gpsimd", wpbf_hbm[r - 1, :, :], T[f"wpc{(r - 1) % 2}"][:],
                    "st_wpc")
        elif r == DT:
            sch.wait("gpsimd", "ld_wpc", 16 * DT)
            dma("gpsimd", wpbf_hbm[DT - 1, :, :], T[f"wpc{(DT - 1) % 2}"][:],
                "st_wpc")
        ln_front(r, xt, [(f"ld_x{r % 2}", sch.cnt[f"ld_x{r % 2}"])])
        if r >= 2:
            a_back(r - 2)
    a_back(RT + NSLOT - 2)
    a_back(RT + NSLOT - 1)
    sch.wait("sync", "s_kv", kv_ev)
    v_tp_h = tpose(T["v_sb"][:], T["vt_sb"][:])
    sch.wait("gpsimd", "st_q", 16 * 64)
    sch.wait("gpsimd", "st_wpc", 16 * DT)
    phase_barrier()

    NSB = 6
    tenant = [0] * NSB
    nb = 0
    tp_h = {}
    xp_h = {}
    nm_h = {}
    mm_after = {}
    head_mm = {}
    pv_cnt = {}
    pe_cnt = {}
    slot_plan = {}
    PVB = [3072, 3584]

    def plan_slot(u):
        nonlocal nb
        i_loc = 2 * ((u // 2) % 8) + (u & 1)
        E = KE2[i_loc]
        fds = [512] * (E // 512) + ([256] if E % 512 else [])
        segs = []
        koff = 0
        k = 0
        while k < len(fds):
            L = min(4, NSB - nb, len(fds) - k)
            segs.append((nb, fds[k:k + L], koff))
            koff += sum(fds[k:k + L])
            nb = (nb + L) % NSB
            k += L
        sb, sfds, _ = segs[-1]
        dbo = sb * 512 + sum(sfds) - 256
        return [segs, dbo, E, 0]

    def b_slot(u, h):
        segs, dbo, E, _ = slot_plan[u]
        pa = u % 3
        prb = T[f"prb{pa}"]
        i_loc = 2 * ((u // 2) % 8) + (u & 1)
        if u - 3 in tp_h:
            sch.wait("scalar", "y_tp", tp_h[u - 3])
            sch.wait("scalar", "v_nm", nm_h[u - 3])
        nseg = 0
        for si, (b0, sfds, koff) in enumerate(segs):
            last = si == len(segs) - 1
            for ci, fd in enumerate(sfds):
                b = b0 + ci
                if tenant[b] > 0:
                    sch.wait("tensor", "s_xp", tenant[b])
                k0 = koff + sum(sfds[:ci])
                mm(psum[:, b * 512:b * 512 + fd],
                   T[f"qth{h % 2}"][:, i_loc * P:(i_loc + 1) * P],
                   T["kt_sb"][:, k0:k0 + fd], True, True, sem="t_mm")
            sch.wait("scalar", "t_mm", sch.cnt["t_mm"])
            fd = sum(sfds) - (256 if last else 0)
            if fd > 0:
                act("scalar", prb[:, koff:koff + fd],
                    psum[:, b0 * 512:b0 * 512 + fd],
                    AF.Exp, sem="s_xp", scale=SCALE,
                    accum_out=smc[pa][:, nseg:nseg + 1])
                nseg += 1
            if last:
                act("scalar", prb[:, E - 256:E], psum[:, dbo:dbo + 256],
                    AF.Exp, sem="s_xp", scale=SCALE)
            for ci in range(len(sfds)):
                tenant[b0 + ci] = sch.cnt["s_xp"]
        xp_h[u] = sch.cnt["s_xp"]
        mm_after[u] = sch.cnt["t_mm"]
        slot_plan[u][3] = nseg

    def b_post(u):
        segs, dbo, E, nseg = slot_plan[u]
        pa = u % 3
        prb = T[f"prb{pa}"]
        sch.wait("vector", "s_xp", xp_h[u])

        def f_stt(Eg, o=prb[:, E - 256:E], i1=T["cmask_sb"][:],
                  acc=smc[pa][:, nseg:nseg + 1]):
            return Eg.scalar_tensor_tensor(out=o, in0=o, scalar=1.0, in1=i1,
                                           op0=ALU.mult, op1=ALU.mult,
                                           accum_out=acc)
        sch.op("vector", f_stt, "v_dg", 1)
        vop("drain")
        vop("reduce_sum", zv[pa][:], smc[pa][:, 0:nseg + 1], AXX)
        vop("drain")
        vop("reciprocal", rcv[pa][:], zv[pa][:])
        vop("drain")
        if u % 2 == 0:
            vop("tensor_scalar_mul", prb[:, 0:E], prb[:, 0:E], rcv[pa][:])
            vop("memset", prb[:, E:E + 256], 0.0, sem="v_nm")
        else:
            vop("tensor_scalar_mul", prb[:, 0:E], prb[:, 0:E], rcv[pa][:],
                sem="v_nm")
        nm_h[u] = sch.cnt["v_nm"]

    def b_tpose(u):
        pa = u % 3
        j = u // 2
        E = slot_plan[u][2]
        Ep = E + 256 if u % 2 == 0 else E
        sch.wait("sync", "v_nm", nm_h[u])
        if j - 3 in pv_cnt:
            sch.wait("sync", "t_pv", pv_cnt[j - 3])
        tp_h[u] = tpose(T[f"pT{j % 3}"][:, 0:Ep // 128, (u & 1) * P:((u & 1) + 1) * P],
                        T[f"prb{pa}"][:, 0:Ep])

    def emit_pv(j):
        pr_ = j % 8
        nkt = 4 * (pr_ + 1)
        sch.wait("tensor", "y_tp", tp_h[2 * j + 1])
        if j - 2 in pe_cnt:
            sch.wait("tensor", "s_pe", pe_cnt[j - 2])
        pvb = psum[:, PVB[j % 2]:PVB[j % 2] + 256]
        for kt in range(nkt):
            mm(pvb, T["v_sb"][:, kt, :], T[f"pT{j % 3}"][:, kt, :],
               kt == 0, kt == nkt - 1, sem="t_pv" if kt == nkt - 1 else None)
        pv_cnt[j] = sch.cnt["t_pv"]

    def emit_pv_evac(j):
        h_, pr_ = j // 8, j % 8
        sch.wait("scalar", "t_pv", pv_cnt[j])
        pvb = psum[:, PVB[j % 2]:PVB[j % 2] + 256]
        act("scalar", T["attnT"][:, h_, pr_ * 256:(pr_ + 1) * 256], pvb,
            AF.Identity, sem="s_pe")
        pe_cnt[j] = sch.cnt["s_pe"]

    for idx in range(128):
        h, pr = idx // 8, idx % 8
        if pr == 0:
            if h >= 2:
                sch.wait("gpsimd", "t_mm", head_mm[h - 2])
            dma("gpsimd", T[f"qth{h % 2}"][:], qt_hbm[h, :, :], "ld_qh")
        if idx >= 2:
            emit_pv(idx - 2)
        if idx >= 3:
            emit_pv_evac(idx - 3)
        for s01 in range(2):
            u = 2 * idx + s01
            slot_plan[u] = plan_slot(u)
            sch.wait("tensor", "ld_qh", 16 * (h + 1))
            b_slot(u, h)
            b_post(u)
            b_tpose(u)
        if pr == 7:
            head_mm[h] = sch.cnt["t_mm"]
    emit_pv(126)
    emit_pv_evac(125)
    emit_pv(127)
    emit_pv_evac(126)
    emit_pv_evac(127)
    phase_barrier()

    pjd_h = []
    for sp in range(8):
        slots = (2 * sp, 2 * sp + 1)
        for sl01, sl in enumerate(slots):
            if sp >= 1:
                sch.wait("gpsimd", "v_pj", 2 * sp)
            dma("gpsimd", T[f"xt{sl01}"][:], xq[sl, :, :], f"ld_x{sl01}")
        for dd in range(DT):
            gdd = sp * DT + dd
            if gdd >= 3:
                sch.wait("gpsimd", "t_pjd", pjd_h[gdd - 3])
            dma("gpsimd", T[f"wpb{dd % 3}"][:], wpbf_hbm[dd, :, :],
                f"ld_wp{dd % 3}")
            sch.wait("tensor", f"ld_wp{dd % 3}", sch.cnt[f"ld_wp{dd % 3}"])
            if sp >= 1 and dd == 0:
                sch.wait("tensor", "v_pj", 2 * sp)
            for sl01 in range(2):
                for jc in range(4):
                    mm(psum[:, sl01 * 2048 + jc * 512: sl01 * 2048 + jc * 512 + 512],
                       T["attnT"][:, dd, slots[sl01] * P:(slots[sl01] + 1) * P],
                       T[f"wpb{dd % 3}"][:, jc * 512:(jc + 1) * 512],
                       dd == 0, dd == DT - 1,
                       sem="t_pjd" if (sl01 == 1 and jc == 3) else None)
            pjd_h.append(sch.cnt["t_pjd"])
        for sl01, sl in enumerate(slots):
            r = RT + NSLOT + sl
            x1t = T[f"x1t{sl % 3}"]
            sch.wait("vector", "t_pjd", DT * (sp + 1))
            sch.wait("vector", f"ld_x{sl01}", sch.cnt[f"ld_x{sl01}"])
            if sl - 3 >= 0:
                sch.wait("vector", "st_x1", 16 * (sl - 2))
                if r - 3 in aff_h:
                    sch.wait("vector", "s_aff", aff_h[r - 3])
            vop("tensor_add", x1t[:], psum[:, sl01 * 2048:sl01 * 2048 + 2048],
                T[f"xt{sl01}"][:])
            vop("tensor_add", x1t[:], x1t[:], bproj_sb[:], sem="v_pj")
            sch.wait("gpsimd", "v_pj", sch.cnt["v_pj"])
            dma("gpsimd", x1_hbm[sl, :, :], x1t[:], "st_x1")
            ln_front(r, x1t, [("v_pj", sch.cnt["v_pj"])])
            if sl >= 2:
                rb = r - 2

                def dst2(lnt, sl2=sl - 2):
                    return tpose(T["xl2T"][:, :, sl2 * P:(sl2 + 1) * P], lnt[:])

                ln_back(rb, T[f"x1t{(sl - 2) % 3}"], dst2)
    for sl2 in (NSLOT - 2, NSLOT - 1):
        rb = RT + NSLOT + sl2

        def dst3(lnt, sl2=sl2):
            return tpose(T["xl2T"][:, :, sl2 * P:(sl2 + 1) * P], lnt[:])

        ln_back(rb, T[f"x1t{sl2 % 3}"], dst3)
    sch.wait("gpsimd", "st_x1", 16 * 16)
    phase_barrier()

    gu_grp = 0
    gutp_h = {}
    wdtp_h = {}
    dq_h = {}

    def d_load(ht):
        pa = ht % 2
        if ht >= 2:
            sch.wait("gpsimd", "s_dq", dq_h.get(ht - 2, 0))
        dma("gpsimd", T[f"wgr{pa}"][:], wg[ht * P:(ht + 1) * P, :], "ld_wg")
        dma("gpsimd", T[f"wur{pa}"][:], wu[ht * P:(ht + 1) * P, :], "ld_wu")

    def d_prep(ht):
        pa = ht % 2
        sch.wait("scalar", "ld_wg", 16 * (ht + 1))
        sch.wait("scalar", "ld_wu", 16 * (ht + 1))
        sch.wait("scalar", "v_stat", 3)
        if ht - 2 in gutp_h:
            sch.wait("scalar", "y_tp", gutp_h[ht - 2])
        act("scalar", T[f"wgd{pa}"][:], T[f"wgr{pa}"][:], AF.Identity, sem="s_dq",
            scale=sg_sb[:, ht:ht + 1], bias=sgb_sb[:, ht:ht + 1])
        act("scalar", T[f"wud{pa}"][:], T[f"wur{pa}"][:], AF.Identity, sem="s_dq",
            scale=su_sb[:, ht:ht + 1], bias=sub_sb[:, ht:ht + 1])
        dq_h[ht] = sch.cnt["s_dq"]
        sch.wait("sync", "s_dq", dq_h[ht])
        if ht >= 2:
            sch.wait("sync", "t_gu", 4 * (ht - 1))
        tpose(T[f"wgT{pa}"][:], T[f"wgd{pa}"][:])
        gutp_h[ht] = tpose(T[f"wuT{pa}"][:], T[f"wud{pa}"][:])

    d_load(0)
    d_prep(0)
    for ht in range(HT):
        pa = ht % 2
        if ht + 1 < HT:
            d_load(ht + 1)
            d_prep(ht + 1)
        for hh in range(2):
            goff = (gu_grp % 2) * 2048
            sch.wait("tensor", "y_tp", gutp_h[ht])
            if gu_grp >= 2:
                sch.wait("tensor", "s_sil", 2 * (gu_grp - 1))
                sch.wait("tensor", "v_act", 2 * (gu_grp - 1))
            for d in range(DT):
                for ci, cc in enumerate((2 * hh, 2 * hh + 1)):
                    mm(psum[:, goff + ci * 512:goff + ci * 512 + 512],
                       T[f"wgT{pa}"][:, d, :], T["xl2T"][:, d, cc * 512:(cc + 1) * 512],
                       d == 0, d == DT - 1)
                for ci, cc in enumerate((2 * hh, 2 * hh + 1)):
                    mm(psum[:, goff + 1024 + ci * 512:goff + 1024 + ci * 512 + 512],
                       T[f"wuT{pa}"][:, d, :], T["xl2T"][:, d, cc * 512:(cc + 1) * 512],
                       d == 0, d == DT - 1,
                       sem="t_gu" if (d == DT - 1 and ci == 1) else None)
            for ci, cc in enumerate((2 * hh, 2 * hh + 1)):
                sch.wait("scalar", "t_gu", sch.cnt["t_gu"])
                if gu_grp >= 2:
                    sch.wait("scalar", "v_act", 2 * (gu_grp - 1))
                act("scalar", T[f"sil{cc}"][:],
                    psum[:, goff + ci * 512:goff + ci * 512 + 512],
                    AF.Silu, sem="s_sil")
                sch.wait("vector", "s_sil", sch.cnt["s_sil"])
                if ht >= 2 and hh == 0 and ci == 0:
                    sch.wait("vector", "st_act", 16 * (ht - 1))
                def f_mul(E, o=T[f"actt{pa}"][:, cc * 512:(cc + 1) * 512],
                          i0=T[f"sil{cc}"][:],
                          i1=psum[:, goff + 1024 + ci * 512:goff + 1024 + ci * 512 + 512]):
                    return E.tensor_mul(o, i0, i1)
                sch.op("vector", f_mul, "v_act", 1)
            gu_grp += 1
        sch.wait("gpsimd", "v_act", sch.cnt["v_act"])
        dma("gpsimd", act_hbm[ht, :, :], T[f"actt{pa}"][:], "st_act")
        if ht >= 4 and ht % 2 == 0 and (ht - 4) // 2 < JT:
            jt = (ht - 4) // 2
            if jt >= 1:
                sch.wait("gpsimd", "s_dqd", jt)
            dma("gpsimd", T["wdr"][:], wd[jt * P:(jt + 1) * P, :], "ld_wd",
                max_dma_last_dim=8192)
            sch.wait("scalar", "ld_wd", 16 * (jt + 1))
            if jt - 1 in wdtp_h:
                sch.wait("scalar", "y_tp", wdtp_h[jt - 1])
            act("scalar", T["wdd"][:], T["wdr"][:], AF.Identity, sem="s_dqd",
                scale=sd_sb[:, jt:jt + 1], bias=sdb_sb[:, jt:jt + 1])
            sch.wait("sync", "s_dqd", sch.cnt["s_dqd"])
            if jt >= 1:
                sch.wait("sync", "st_wdt", 16 * jt)
            wdtp_h[jt] = tpose(T["wdTt"][:], T["wdd"][:])
            sch.wait("gpsimd", "y_tp", wdtp_h[jt])
            dma("gpsimd", wdt_hbm[jt, :, :], T["wdTt"][:], "st_wdt")
    sch.wait("gpsimd", "st_act", 16 * HT)
    sch.wait("gpsimd", "st_wdt", 16 * JT)
    phase_barrier()

    detp_h = {}
    dn_after = {}
    wdtb_cnt = {}
    for hh in range(2):
        for jt0 in range(3):
            g0 = hh * JT + jt0
            if g0 - 3 in dn_after:
                sch.wait("gpsimd", "t_dn", dn_after[g0 - 3])
            dma("gpsimd", T[f"wdTb{g0 % 3}"][:], wdt_hbm[jt0, :, :],
                f"ld_wdt{g0 % 3}")
            wdtb_cnt[g0] = sch.cnt[f"ld_wdt{g0 % 3}"]
        if hh == 1:
            sch.wait("gpsimd", "s_de", sch.cnt["s_de"])
        for ht in range(HT):
            dma("gpsimd", T["actH"][:, ht, :],
                act_hbm[ht, :, hh * 1024:(hh + 1) * 1024], "ld_act")
        if hh == 1:
            sch.wait("gpsimd", "st_out", 16 * 8)
            sch.wait("gpsimd", "v_oadd", sch.cnt["v_oadd"])
        for sl in range(8):
            dma("gpsimd", T["out_sb"][:, sl, :], x1_hbm[hh * 8 + sl, :, :], "ld_out")
        for jt in range(JT):
            g = hh * JT + jt
            pa = g % 2
            pw = g % 3
            if jt >= 3:
                sch.wait("gpsimd", "t_dn", dn_after[g - 3])
                dma("gpsimd", T[f"wdTb{pw}"][:], wdt_hbm[jt, :, :],
                    f"ld_wdt{pw}")
                wdtb_cnt[g] = sch.cnt[f"ld_wdt{pw}"]
            sch.wait("tensor", f"ld_wdt{pw}", wdtb_cnt[g])
            if jt > 0 or not E_INCR:
                sch.wait("tensor", "ld_act", 16 * (hh + 1) * HT)
            if g >= 2:
                sch.wait("tensor", "s_de", g - 1)
            for ht in range(HT):
                if jt == 0 and E_INCR:
                    sch.wait("tensor", "ld_act", 16 * (hh * HT + ht + 1))
                for ci in range(2):
                    mm(psum[:, pa * 1024 + ci * 512:pa * 1024 + ci * 512 + 512],
                       T[f"wdTb{pw}"][:, ht, :],
                       T["actH"][:, ht, ci * 512:(ci + 1) * 512],
                       ht == 0, ht == HT - 1,
                       sem="t_dn" if (ht == HT - 1 and ci == 1) else None)
            dn_after[g] = sch.cnt["t_dn"]
            sch.wait("scalar", "t_dn", sch.cnt["t_dn"])
            if g - 2 in detp_h:
                sch.wait("scalar", "y_tp", detp_h[g - 2])
            act("scalar", T[f"dTt{pa}"][:], psum[:, pa * 1024:pa * 1024 + 1024],
                AF.Identity, sem="s_de")
            sch.wait("sync", "s_de", sch.cnt["s_de"])
            if g >= 2:
                sch.wait("sync", "v_oadd", 8 * (g - 1))
            detp_h[g] = tpose(T[f"dnt{pa}"][:], T[f"dTt{pa}"][:])
            sch.wait("vector", "y_tp", detp_h[g])
            sch.wait("vector", "ld_out", 16 * (hh + 1) * 8)
            for nt in range(8):
                o = T["out_sb"][:, nt, jt * P:jt * P + P]
                def f_add(E, o=o, i1=T[f"dnt{pa}"][:, nt, :]):
                    return E.tensor_add(o, o, i1)
                sch.op("vector", f_add, "v_oadd", 1)
        sch.wait("gpsimd", "v_oadd", sch.cnt["v_oadd"])
        for sl in range(8):
            dma("gpsimd", out_h[hh * 8 + sl, :, :], T["out_sb"][:, sl, :], "st_out")
    sch.wait("gpsimd", "st_out", 16 * 16)

    global _LAST_SCHED
    _LAST_SCHED = sch
    sems = {}
    for name in sch.sem_names:
        sems[name] = e(nc.semaphore(name))
    block = e(nc.Block())

    def emit(E, stream):
        for item in stream:
            if item[0] == "w":
                E.wait_ge(sems[item[1]], item[2])
            else:
                _, fn, inc_sem, inc_amt = item
                inst = fn(E)
                if inc_sem is not None and inst is not None:
                    inst.then_inc(sems[inc_sem], inc_amt)

    @block.tensor
    def _(t):
        emit(t, sch.streams["tensor"])

    @block.scalar
    def _(sc):
        emit(sc, sch.streams["scalar"])

    @block.vector
    def _(v):
        emit(v, sch.streams["vector"])

    @block.sync
    def _(sy):
        emit(sy, sch.streams["sync"])

    @block.gpsimd
    def _(g):
        emit(g, sch.streams["gpsimd"])

    ctx.close()
    return nc


_LAST_SCHED = None
_NC_CACHE = None


def _get_nc():
    global _NC_CACHE
    if _NC_CACHE is None:
        _NC_CACHE = build_program()
    return _NC_CACHE


def kernel(L0_attn_x, L0_attn_W_qkv, L0_attn_b_qkv, L0_attn_W_proj, L0_attn_b_proj,
           L0_ffn_wq_gate, L0_ffn_s_gate, L0_ffn_wq_up, L0_ffn_s_up,
           L0_ffn_wq_down, L0_ffn_s_down):
    x = np.ascontiguousarray(np.asarray(L0_attn_x, dtype=np.float32))
    wqkv = np.ascontiguousarray(np.asarray(L0_attn_W_qkv, dtype=np.float32))
    bq = np.asarray(L0_attn_b_qkv, dtype=np.float32)
    wpn = np.ascontiguousarray(np.asarray(L0_attn_W_proj, dtype=np.float32))
    bp = np.asarray(L0_attn_b_proj, dtype=np.float32)
    wgn = np.ascontiguousarray(np.asarray(L0_ffn_wq_gate, dtype=np.int32))
    sgn = np.asarray(L0_ffn_s_gate, dtype=np.float32)
    wun = np.ascontiguousarray(np.asarray(L0_ffn_wq_up, dtype=np.int32))
    sun = np.asarray(L0_ffn_s_up, dtype=np.float32)
    wdn = np.ascontiguousarray(np.asarray(L0_ffn_wq_down, dtype=np.int32))
    sdn = np.asarray(L0_ffn_s_down, dtype=np.float32)

    bqkv_t = np.ascontiguousarray(bq.reshape(JQKV, P).T)
    bproj_t = np.ascontiguousarray(np.tile(bp[None, :], (P, 1)))
    sg_t = np.ascontiguousarray(sgn.reshape(HT, P).T)
    su_t = np.ascontiguousarray(sun.reshape(HT, P).T)
    sd_t = np.ascontiguousarray(sdn.reshape(JT, P).T)

    nc = _get_nc()
    in_maps = []
    metas = []
    for c in range(8):
        b = c // 2
        par = c % 2
        gl = [2 * i + par for i in range(NSLOT)]
        xb = x[b]
        xq_c = np.ascontiguousarray(xb.reshape(RT, P, D)[gl])
        tri = (np.arange(128)[None, :] <= np.arange(P)[:, None]).astype(np.float32)
        cm = np.zeros((P, 256), dtype=np.float32)
        if par == 0:
            cm[:, 0:128] = tri
        else:
            cm[:, 0:128] = 1.0
            cm[:, 128:256] = tri
        cmb = cm.astype(ml_dtypes.bfloat16)
        in_maps.append({
            "x_att": xb, "xq": xq_c, "wqkv": wqkv, "bqkv": bqkv_t,
            "wp": wpn, "bproj": bproj_t, "wg": wgn, "wu": wun, "wd": wdn,
            "sg": sg_t, "su": su_t, "sd": sd_t, "cmask": cmb,
        })
        metas.append((b, gl))

    res = run_bass_kernel_spmd(nc, in_maps, core_ids=list(range(8)))

    out = np.empty((B, S, D), dtype=np.float32)
    for c in range(8):
        b, gl = metas[c]
        oc = np.asarray(res.results[c]["out"])
        for i, g in enumerate(gl):
            out[b, g * P:(g + 1) * P, :] = oc[i]
    return out


# version 22
# speedup vs baseline: 1.2088x; 1.0463x over previous
import numpy as np
import ml_dtypes
import concourse.bass as bass
import concourse.mybir as mybir
from concourse.bass_utils import run_bass_kernel_spmd

F32 = mybir.dt.float32
BF16 = mybir.dt.bfloat16
I32 = mybir.dt.int32
AF = mybir.ActivationFunctionType
ALU = mybir.AluOpType
AXX = mybir.AxisListType.X

P = 128
B = 4
S = 4096
D = 2048
DT = D // P
NH = 16
HD = 128
JQKV = 18
HID = 5632
HT = HID // P
JT = D // P
NSLOT = 16
RT = S // P
SCALE = float(HD) ** -0.5
EPS = 1e-5

KE2 = [256 * (i + 1) for i in range(NSLOT)]

ENGS = ("tensor", "scalar", "vector", "sync", "gpsimd")
E_PRELOAD = False
E_INCR = True


class Sched:
    def __init__(self):
        self.streams = {e: [] for e in ENGS}
        self.cnt = {}
        self.sem_names = []

    def sem(self, name):
        if name not in self.cnt:
            self.cnt[name] = 0
            self.sem_names.append(name)

    def wait(self, eng, sem, val):
        val = min(val, self.cnt[sem])
        if val <= 0:
            return
        self.streams[eng].append(("w", sem, val))

    def inc(self, sem, amt):
        self.cnt[sem] += amt
        return self.cnt[sem]

    def op(self, eng, fn, inc_sem=None, inc_amt=1):
        if inc_sem is not None:
            self.inc(inc_sem, inc_amt)
        self.streams[eng].append(("i", fn, inc_sem, inc_amt))


def build_program():
    nc = bass.Bass("TRN2", target_bir_lowering=False)

    x_att = nc.dram_tensor("x_att", [S, D], F32, kind="ExternalInput").ap()
    xq = nc.dram_tensor("xq", [NSLOT, P, D], F32, kind="ExternalInput").ap()
    wqkv = nc.dram_tensor("wqkv", [D, 2304], F32, kind="ExternalInput").ap()
    bqkv = nc.dram_tensor("bqkv", [P, JQKV], F32, kind="ExternalInput").ap()
    wp = nc.dram_tensor("wp", [D, D], F32, kind="ExternalInput").ap()
    bproj = nc.dram_tensor("bproj", [P, D], F32, kind="ExternalInput").ap()
    wg = nc.dram_tensor("wg", [HID, D], I32, kind="ExternalInput").ap()
    wu = nc.dram_tensor("wu", [HID, D], I32, kind="ExternalInput").ap()
    wd = nc.dram_tensor("wd", [D, HID], I32, kind="ExternalInput").ap()
    sg = nc.dram_tensor("sg", [P, HT], F32, kind="ExternalInput").ap()
    su = nc.dram_tensor("su", [P, HT], F32, kind="ExternalInput").ap()
    sd = nc.dram_tensor("sd", [P, JT], F32, kind="ExternalInput").ap()
    cmask_h = nc.dram_tensor("cmask", [P, 256], BF16, kind="ExternalInput").ap()
    out_h = nc.dram_tensor("out", [NSLOT, P, D], F32, kind="ExternalOutput").ap()

    qt_hbm = nc.dram_tensor("qt_hbm", [NH, P, 2048], BF16, kind="Internal").ap()
    wpbf_hbm = nc.dram_tensor("wpbf_hbm", [DT, P, D], BF16, kind="Internal").ap()
    x1_hbm = nc.dram_tensor("x1_hbm", [NSLOT, P, D], F32, kind="Internal").ap()
    act_hbm = nc.dram_tensor("act_hbm", [HT, P, 2048], BF16, kind="Internal").ap()
    wdt_hbm = nc.dram_tensor("wdt_hbm", [JT, P, HT * P], BF16, kind="Internal").ap()

    from contextlib import ExitStack
    ctx = ExitStack()
    e = ctx.enter_context
    bqkv_sb = e(nc.sbuf_tensor("bqkv_sb", [P, JQKV], F32))
    bproj_sb = e(nc.sbuf_tensor("bproj_sb", [P, D], F32))
    sg_sb = e(nc.sbuf_tensor("sg_sb", [P, HT], F32))
    su_sb = e(nc.sbuf_tensor("su_sb", [P, HT], F32))
    sd_sb = e(nc.sbuf_tensor("sd_sb", [P, JT], F32))
    sgb_sb = e(nc.sbuf_tensor("sgb_sb", [P, HT], F32))
    sub_sb = e(nc.sbuf_tensor("sub_sb", [P, HT], F32))
    sdb_sb = e(nc.sbuf_tensor("sdb_sb", [P, JT], F32))
    bnb = [e(nc.sbuf_tensor(f"bnb{i}", [P, 24], F32)) for i in range(4)]
    mv = [e(nc.sbuf_tensor(f"mv{i}", [P, 2], F32)) for i in range(4)]
    vpe = [e(nc.sbuf_tensor(f"vpe{i}", [P, 1], F32)) for i in range(4)]
    lnv = [e(nc.sbuf_tensor(f"lnv{i}", [P, 1], F32)) for i in range(4)]
    rstd = [e(nc.sbuf_tensor(f"rstd{i}", [P, 1], F32)) for i in range(4)]
    nmr = [e(nc.sbuf_tensor(f"nmr{i}", [P, 1], F32)) for i in range(4)]
    smc = [e(nc.sbuf_tensor(f"smc{i}", [P, 8], F32)) for i in range(4)]
    zv = [e(nc.sbuf_tensor(f"zv{i}", [P, 1], F32)) for i in range(4)]
    rcv = [e(nc.sbuf_tensor(f"rcv{i}", [P, 1], F32)) for i in range(4)]

    SEG = {
        "wqkv_sb": ("A", [P, DT, 2304], BF16),
        "xt0": ("AC", [P, D], F32), "xt1": ("AC", [P, D], F32),
        "xt2": ("A", [P, D], F32), "xt3": ("A", [P, D], F32),
        "xt4": ("A", [P, D], F32),
        "lnt0": ("AC", [P, D], BF16), "lnt1": ("AC", [P, D], BF16),
        "lnt2": ("AC", [P, D], BF16),
        "xlk0": ("A", [P, DT, 512], BF16), "xlk1": ("A", [P, DT, 512], BF16),
        "kt_sb": ("AB", [P, S], BF16),
        "vt_sb": ("A", [P, S], BF16),
        "v_sb": ("AB", [P, RT, P], BF16),
        "qev0": ("A", [P, 512], BF16), "qev1": ("A", [P, 512], BF16),
        "qev2": ("A", [P, 512], BF16), "qev3": ("A", [P, 512], BF16),
        "wpc0": ("A", [P, D], BF16), "wpc1": ("A", [P, D], BF16),
        "cmask_sb": ("AB", [P, 256], BF16),
        "qth0": ("B", [P, 2048], BF16), "qth1": ("B", [P, 2048], BF16),
        "prb0": ("B", [P, 4096], BF16), "prb1": ("B", [P, 4096], BF16),
        "prb2": ("B", [P, 4096], BF16), "prb3": ("B", [P, 4096], BF16),
        "pT0": ("B", [P, 32, 256], BF16), "pT1": ("B", [P, 32, 256], BF16),
        "pT2": ("B", [P, 32, 256], BF16),
        "attnT": ("BC", [P, NH, 2048], BF16),
        "wpb0": ("C", [P, D], BF16), "wpb1": ("C", [P, D], BF16),
        "wpb2": ("C", [P, D], BF16),
        "x1t0": ("C", [P, D], F32), "x1t1": ("C", [P, D], F32),
        "x1t2": ("C", [P, D], F32),
        "xl2T": ("CD", [P, DT, 2048], BF16),
        "wgr0": ("D", [P, D], BF16), "wgr1": ("D", [P, D], BF16),
        "wur0": ("D", [P, D], BF16), "wur1": ("D", [P, D], BF16),
        "wgd0": ("D", [P, D], BF16), "wgd1": ("D", [P, D], BF16),
        "wud0": ("D", [P, D], BF16), "wud1": ("D", [P, D], BF16),
        "wgT0": ("D", [P, DT, P], BF16), "wgT1": ("D", [P, DT, P], BF16),
        "wuT0": ("D", [P, DT, P], BF16), "wuT1": ("D", [P, DT, P], BF16),
        "sil0": ("D", [P, 512], BF16), "sil1": ("D", [P, 512], BF16),
        "sil2": ("D", [P, 512], BF16), "sil3": ("D", [P, 512], BF16),
        "actt0": ("D", [P, 2048], BF16), "actt1": ("D", [P, 2048], BF16),
        "wdr": ("D", [P, HID], BF16),
        "wdd": ("D", [P, HID], BF16),
        "wdTt": ("D", [P, HT, P], BF16),
        "actH": ("E", [P, HT, 1024], BF16),
        "out_sb": ("E", [P, 8, D], F32),
        "wdTb0": ("E", [P, HT, P], BF16), "wdTb1": ("E", [P, HT, P], BF16),
        "wdTb2": ("E", [P, HT, P], BF16),
        "dTt0": ("E", [P, 1024], BF16), "dTt1": ("E", [P, 1024], BF16),
        "dnt0": ("E", [P, 8, P], BF16), "dnt1": ("E", [P, 8, P], BF16),
    }

    def seg_bytes(shape, dtype):
        n = 1
        for s_ in shape[1:]:
            n *= s_
        return n * mybir.dt.size(dtype)

    order = sorted(SEG.items(), key=lambda kv: -seg_bytes(kv[1][1], kv[1][2]))
    placed = []
    offsets = {}
    for name, (phases, shape, dtype) in order:
        sz = (seg_bytes(shape, dtype) + 63) // 64 * 64
        cands = sorted(set([0] + [pe for (pp, po, pe) in placed if set(pp) & set(phases)]))
        for off in cands:
            end = off + sz
            if all((not (set(pp) & set(phases))) or end <= po or off >= pe
                   for (pp, po, pe) in placed):
                placed.append((phases, off, end))
                offsets[name] = off
                break
        else:
            raise RuntimeError("no fit " + name)
    arena_sz = max(pe for (_, _, pe) in placed)
    assert arena_sz <= 202364, f"arena {arena_sz}"
    arena = e(nc.sbuf_tensor("arena", [P, arena_sz], mybir.dt.uint8))
    arena_addr = nc.lookup_mloc(arena).addr

    T = {}
    for name, (phases, shape, dtype) in SEG.items():
        h = nc.virtual_sbuf_tensor(name, shape, dtype)
        nc._tensor(h.name, h.shape, h.dtype, type="SB")
        mloc = nc.lookup_mloc(h)
        mloc.allocated = True
        mloc.addr = arena_addr + offsets[name]
        T[name] = h.ap()

    psum = e(nc.psum_tensor("psum", [P, 4096], F32))

    sch = Sched()
    for s_ in ("ld_cst", "ld_x0", "ld_x1", "ld_qh", "ld_wp0", "ld_wp1", "ld_wp2",
               "ld_wp3", "ld_wg", "ld_wu", "ld_wd", "ld_wdt0", "ld_wdt1",
               "ld_act0", "ld_act1", "ld_act2", "ld_act3", "ld_out0",
               "ld_out1", "ld_wpc", "st_wpc", "ld_wdt2",
               "st_q", "st_x1", "st_act", "st_wdt", "st_out",
               "v_stat", "v_nmr", "v_dg", "v_mx", "v_nm", "v_act", "v_pj", "v_oadd",
               "s_rstd", "s_aff", "s_ev", "s_xp", "s_pe", "s_sil", "s_dq", "s_dqd",
               "s_de", "s_kv", "s_qe",
               "t_mm", "t_qkv", "t_pv", "t_pj", "t_pjd", "t_gu", "t_dn",
               "y_tp", "ph"):
        sch.sem(s_)

    def dma(eng, dst, src, sem, ser=True, **kw):
        if ser and sch.cnt[sem] > 0:
            sch.wait(eng, sem, sch.cnt[sem])
        def fn(E, dst=dst, src=src, kw=kw):
            return E.dma_start(dst, src, **kw)
        sch.op(eng, fn, sem, 16)

    def tpose(dst, src):
        if sch.cnt["y_tp"] > 0:
            sch.wait("sync", "y_tp", sch.cnt["y_tp"])
        def fn(E, dst=dst, src=src):
            return E.dma_start(dst, src, transpose=True)
        sch.op("sync", fn, "y_tp", 16)
        return sch.cnt["y_tp"]

    def act(eng, out, in_, func, sem=None, **kw):
        def fn(E, out=out, in_=in_, func=func, kw=kw):
            return E.activation(out, in_, func, **kw)
        sch.op(eng, fn, sem, 1)

    def mm(out, lhsT, rhs, start, stop, sem=None):
        def fn(E, out=out, lhsT=lhsT, rhs=rhs, start=start, stop=stop):
            return E.matmul(out, lhsT, rhs, start=start, stop=stop)
        sch.op("tensor", fn, sem, 1)

    def vop(meth, *args, sem=None, **kw):
        def fn(E, meth=meth, args=args, kw=kw):
            return getattr(E, meth)(*args, **kw)
        sch.op("vector", fn, sem, 1)

    def phase_barrier():
        for eng in ENGS:
            def fn2(E):
                return E.nop()
            sch.op(eng, fn2, "ph", 1)
        tgt = sch.cnt["ph"]
        for eng in ENGS:
            sch.wait(eng, "ph", tgt)

    dma("gpsimd", bqkv_sb[:], bqkv[:], "ld_cst")
    dma("gpsimd", bproj_sb[:], bproj[:], "ld_cst")
    dma("gpsimd", sg_sb[:], sg[:], "ld_cst")
    dma("gpsimd", su_sb[:], su[:], "ld_cst")
    dma("gpsimd", sd_sb[:], sd[:], "ld_cst")
    dma("gpsimd", T["cmask_sb"][:], cmask_h[:], "ld_cst")
    for t in range(DT):
        dma("gpsimd", T["wqkv_sb"][:, t, :], wqkv[t * P:(t + 1) * P, :], "ld_cst")
    n_cst = sch.cnt["ld_cst"]
    sch.wait("vector", "ld_cst", n_cst)
    vop("tensor_scalar_mul", sgb_sb[:], sg_sb[:], -128.0, sem="v_stat")
    vop("tensor_scalar_mul", sub_sb[:], su_sb[:], -128.0, sem="v_stat")
    vop("tensor_scalar_mul", sdb_sb[:], sd_sb[:], -128.0, sem="v_stat")

    aff_h, rstd_h, nmr_h, stat_h, lntp_h = {}, {}, {}, {}, {}

    def ln_front(r, src_ap, gate):
        pa = r % 4
        for sem_, val_ in gate:
            sch.wait("vector", sem_, val_)
        if r - 4 in aff_h:
            sch.wait("vector", "s_aff", aff_h[r - 4])
        for i in range(4):
            vop("bn_stats", bnb[pa][:, 6 * i:6 * (i + 1)],
                src_ap[:, 512 * i:512 * (i + 1)])
        vop("drain")
        vop("bn_aggr", mv[pa][:], bnb[pa][:])
        vop("drain")
        if r - 4 in rstd_h:
            sch.wait("vector", "s_rstd", rstd_h[r - 4])
        vop("tensor_scalar_add", vpe[pa][:], mv[pa][:, 1:2], EPS, sem="v_stat")
        stat_h[r] = sch.cnt["v_stat"]
        sch.wait("scalar", "v_stat", stat_h[r])
        if r - 4 in nmr_h:
            sch.wait("scalar", "v_nmr", nmr_h[r - 4])
        act("scalar", lnv[pa][:], vpe[pa][:], AF.Ln)
        act("scalar", rstd[pa][:], lnv[pa][:], AF.Exp, sem="s_rstd", scale=-0.5)
        rstd_h[r] = sch.cnt["s_rstd"]
        sch.wait("vector", "s_rstd", rstd_h[r])
        def f_nmr(E, o=nmr[pa][:], i0=mv[pa][:, 0:1], s1=rstd[pa][:]):
            return E.tensor_scalar(o, i0, s1, -1.0, ALU.mult, ALU.mult)
        sch.op("vector", f_nmr, "v_nmr", 1)
        nmr_h[r] = sch.cnt["v_nmr"]

    def ln_back(r, src_ap, dst_fn):
        pa = r % 4
        lnt = T[f"lnt{r % 3}"]
        sch.wait("scalar", "v_nmr", nmr_h[r])
        if r - 3 in lntp_h:
            sch.wait("scalar", "y_tp", lntp_h[r - 3])
        act("scalar", lnt[:], src_ap[:], AF.Identity, sem="s_aff",
            scale=rstd[pa][:], bias=nmr[pa][:])
        aff_h[r] = sch.cnt["s_aff"]
        sch.wait("sync", "s_aff", aff_h[r])
        lntp_h[r] = dst_fn(lnt)

    kv_ev = 0
    q_ev = 0
    qkv_chunk_h = {}

    def emit_chunk(gch):
        nonlocal kv_ev, q_ev
        is_q = gch >= 8
        xlk = T[f"xlk{gch % 2}"]
        sch.wait("tensor", "y_tp", lntp_h[4 * gch + 3])
        sch.wait("tensor", "ld_cst", n_cst)
        if not is_q:
            c = gch
            for ji, jt in enumerate((16, 17)):
                off = (c % 2) * 1024 + ji * 512
                bank = psum[:, off:off + 512]
                if kv_ev + 1 > 4:
                    sch.wait("tensor", "s_kv", kv_ev - 3)
                for d in range(DT):
                    mm(bank, T["wqkv_sb"][:, d, jt * P:(jt + 1) * P], xlk[:, d, :],
                       d == 0, d == DT - 1, sem="t_qkv" if d == DT - 1 else None)
                dst = (T["kt_sb"] if jt == 16 else T["vt_sb"])[:, c * 512:(c + 1) * 512]
                sch.wait("scalar", "t_qkv", sch.cnt["t_qkv"])
                act("scalar", dst, bank, AF.Identity, sem="s_kv",
                    bias=bqkv_sb[:, jt:jt + 1])
                kv_ev += 1
        else:
            qc = gch - 8
            for jt in range(16):
                n = qc * 16 + jt
                off = 2048 + (n % 4) * 512
                bank = psum[:, off:off + 512]
                if n >= 4:
                    sch.wait("tensor", "s_qe", n - 3)
                for d in range(DT):
                    mm(bank, T["wqkv_sb"][:, d, jt * P:(jt + 1) * P], xlk[:, d, :],
                       d == 0, d == DT - 1, sem="t_qkv" if d == DT - 1 else None)
                qb = T[f"qev{n % 4}"]
                sch.wait("scalar", "t_qkv", sch.cnt["t_qkv"])
                if n >= 4:
                    sch.wait("scalar", "st_q", 16 * (n - 3))
                act("scalar", qb[:], bank, AF.Identity, sem="s_qe",
                    bias=bqkv_sb[:, jt:jt + 1])
                q_ev += 1
                sch.wait("gpsimd", "s_qe", q_ev)
                dma("gpsimd", qt_hbm[jt, :, qc * 512:(qc + 1) * 512], qb[:], "st_q")
        qkv_chunk_h[gch] = sch.cnt["t_qkv"]

    def a_back(rb):
        xtb = T[f"xt{rb % 5}"]

        def dst_fn(lnt, cbuf=(rb // 4) % 2, tt=rb % 4, gch=rb // 4):
            if gch - 2 in qkv_chunk_h:
                sch.wait("sync", "t_qkv", qkv_chunk_h[gch - 2])
            return tpose(T[f"xlk{cbuf}"][:, :, tt * P:(tt + 1) * P], lnt[:])

        ln_back(rb, xtb, dst_fn)
        if rb % 4 == 3:
            emit_chunk(rb // 4)

    for r in range(RT + NSLOT):
        is_q = r >= RT
        xt = T[f"xt{r % 5}"]
        if r - 5 in aff_h:
            sch.wait("gpsimd", "s_aff", aff_h[r - 5])
            sch.wait("gpsimd", "v_stat", stat_h[r - 5])
        src = x_att[(r * P):(r + 1) * P, :] if not is_q else xq[r - RT, :, :]
        dma("gpsimd", xt[:], src, f"ld_x{r % 2}")
        if r < DT:
            if r >= 2:
                sch.wait("gpsimd", "st_wpc", 16 * (r - 1))
            dma("gpsimd", T[f"wpc{r % 2}"][:], wp[r * P:(r + 1) * P, :], "ld_wpc")
            if r >= 1:
                sch.wait("gpsimd", "ld_wpc", 16 * r)
                dma("gpsimd", wpbf_hbm[r - 1, :, :], T[f"wpc{(r - 1) % 2}"][:],
                    "st_wpc")
        elif r == DT:
            sch.wait("gpsimd", "ld_wpc", 16 * DT)
            dma("gpsimd", wpbf_hbm[DT - 1, :, :], T[f"wpc{(DT - 1) % 2}"][:],
                "st_wpc")
        ln_front(r, xt, [(f"ld_x{r % 2}", sch.cnt[f"ld_x{r % 2}"])])
        if r >= 2:
            a_back(r - 2)
    a_back(RT + NSLOT - 2)
    a_back(RT + NSLOT - 1)
    sch.wait("sync", "s_kv", kv_ev)
    v_tp_h = tpose(T["v_sb"][:], T["vt_sb"][:])
    sch.wait("gpsimd", "st_q", 16 * 64)
    sch.wait("gpsimd", "st_wpc", 16 * DT)
    phase_barrier()

    NSB = 6
    tenant = [0] * NSB
    nb = 0
    tp_h = {}
    xp_h = {}
    nm_h = {}
    mm_after = {}
    head_mm = {}
    pv_cnt = {}
    pe_cnt = {}
    slot_plan = {}
    PVB = [3072, 3584]

    def plan_slot(u):
        nonlocal nb
        i_loc = 2 * ((u // 2) % 8) + (u & 1)
        E = KE2[i_loc]
        fds = [512] * (E // 512) + ([256] if E % 512 else [])
        segs = []
        koff = 0
        k = 0
        while k < len(fds):
            L = min(4, NSB - nb, len(fds) - k)
            segs.append((nb, fds[k:k + L], koff))
            koff += sum(fds[k:k + L])
            nb = (nb + L) % NSB
            k += L
        sb, sfds, _ = segs[-1]
        dbo = sb * 512 + sum(sfds) - 256
        return [segs, dbo, E, 0]

    def b_slot(u, h):
        segs, dbo, E, _ = slot_plan[u]
        pa = u % 4
        prb = T[f"prb{pa}"]
        i_loc = 2 * ((u // 2) % 8) + (u & 1)
        if u - 4 in tp_h:
            sch.wait("scalar", "y_tp", tp_h[u - 4])
            sch.wait("scalar", "v_nm", nm_h[u - 4])
        nseg = 0
        for si, (b0, sfds, koff) in enumerate(segs):
            last = si == len(segs) - 1
            for ci, fd in enumerate(sfds):
                b = b0 + ci
                if tenant[b] > 0:
                    sch.wait("tensor", "s_xp", tenant[b])
                k0 = koff + sum(sfds[:ci])
                mm(psum[:, b * 512:b * 512 + fd],
                   T[f"qth{h % 2}"][:, i_loc * P:(i_loc + 1) * P],
                   T["kt_sb"][:, k0:k0 + fd], True, True, sem="t_mm")
            sch.wait("scalar", "t_mm", sch.cnt["t_mm"])
            fd = sum(sfds) - (256 if last else 0)
            if fd > 0:
                act("scalar", prb[:, koff:koff + fd],
                    psum[:, b0 * 512:b0 * 512 + fd],
                    AF.Exp, sem="s_xp", scale=SCALE,
                    accum_out=smc[pa][:, nseg:nseg + 1])
                nseg += 1
            if last:
                act("scalar", prb[:, E - 256:E], psum[:, dbo:dbo + 256],
                    AF.Exp, sem="s_xp", scale=SCALE)
            for ci in range(len(sfds)):
                tenant[b0 + ci] = sch.cnt["s_xp"]
        xp_h[u] = sch.cnt["s_xp"]
        mm_after[u] = sch.cnt["t_mm"]
        slot_plan[u][3] = nseg

    def b_post(u):
        segs, dbo, E, nseg = slot_plan[u]
        pa = u % 4
        prb = T[f"prb{pa}"]
        sch.wait("vector", "s_xp", xp_h[u])

        def f_stt(Eg, o=prb[:, E - 256:E], i1=T["cmask_sb"][:],
                  acc=smc[pa][:, nseg:nseg + 1]):
            return Eg.scalar_tensor_tensor(out=o, in0=o, scalar=1.0, in1=i1,
                                           op0=ALU.mult, op1=ALU.mult,
                                           accum_out=acc)
        sch.op("vector", f_stt, "v_dg", 1)
        vop("drain")
        vop("reduce_sum", zv[pa][:], smc[pa][:, 0:nseg + 1], AXX)
        vop("drain")
        vop("reciprocal", rcv[pa][:], zv[pa][:])
        vop("drain")
        if u % 2 == 0:
            vop("tensor_scalar_mul", prb[:, 0:E], prb[:, 0:E], rcv[pa][:])
            vop("memset", prb[:, E:E + 256], 0.0, sem="v_nm")
        else:
            vop("tensor_scalar_mul", prb[:, 0:E], prb[:, 0:E], rcv[pa][:],
                sem="v_nm")
        nm_h[u] = sch.cnt["v_nm"]

    def b_tpose(u):
        pa = u % 4
        j = u // 2
        E = slot_plan[u][2]
        Ep = E + 256 if u % 2 == 0 else E
        sch.wait("sync", "v_nm", nm_h[u])
        if j - 3 in pv_cnt:
            sch.wait("sync", "t_pv", pv_cnt[j - 3])
        tp_h[u] = tpose(T[f"pT{j % 3}"][:, 0:Ep // 128, (u & 1) * P:((u & 1) + 1) * P],
                        T[f"prb{pa}"][:, 0:Ep])

    def emit_pv(j):
        pr_ = j % 8
        nkt = 4 * (pr_ + 1)
        sch.wait("tensor", "y_tp", tp_h[2 * j + 1])
        if j - 2 in pe_cnt:
            sch.wait("tensor", "s_pe", pe_cnt[j - 2])
        pvb = psum[:, PVB[j % 2]:PVB[j % 2] + 256]
        for kt in range(nkt):
            mm(pvb, T["v_sb"][:, kt, :], T[f"pT{j % 3}"][:, kt, :],
               kt == 0, kt == nkt - 1, sem="t_pv" if kt == nkt - 1 else None)
        pv_cnt[j] = sch.cnt["t_pv"]

    def emit_pv_evac(j):
        h_, pr_ = j // 8, j % 8
        sch.wait("vector", "t_pv", pv_cnt[j])
        pvb = psum[:, PVB[j % 2]:PVB[j % 2] + 256]
        vop("tensor_copy", T["attnT"][:, h_, pr_ * 256:(pr_ + 1) * 256], pvb,
            sem="s_pe")
        pe_cnt[j] = sch.cnt["s_pe"]

    for idx in range(128):
        h, pr = idx // 8, idx % 8
        if pr == 0:
            if h >= 2:
                sch.wait("gpsimd", "t_mm", head_mm[h - 2])
            dma("gpsimd", T[f"qth{h % 2}"][:], qt_hbm[h, :, :], "ld_qh")
        if idx >= 2:
            emit_pv(idx - 2)
        if idx >= 3:
            emit_pv_evac(idx - 3)
        for s01 in range(2):
            u = 2 * idx + s01
            slot_plan[u] = plan_slot(u)
            sch.wait("tensor", "ld_qh", 16 * (h + 1))
            b_slot(u, h)
            b_post(u)
            b_tpose(u)
        if pr == 7:
            head_mm[h] = sch.cnt["t_mm"]
    emit_pv(126)
    emit_pv_evac(125)
    emit_pv(127)
    emit_pv_evac(126)
    emit_pv_evac(127)
    phase_barrier()

    pjd_h = []
    for sp in range(8):
        slots = (2 * sp, 2 * sp + 1)
        for sl01, sl in enumerate(slots):
            if sp >= 1:
                sch.wait("gpsimd", "v_pj", 2 * sp)
            dma("gpsimd", T[f"xt{sl01}"][:], xq[sl, :, :], f"ld_x{sl01}")
        for dd in range(DT):
            gdd = sp * DT + dd
            if gdd >= 3:
                sch.wait("gpsimd", "t_pjd", pjd_h[gdd - 3])
            dma("gpsimd", T[f"wpb{dd % 3}"][:], wpbf_hbm[dd, :, :],
                f"ld_wp{dd % 3}")
            sch.wait("tensor", f"ld_wp{dd % 3}", sch.cnt[f"ld_wp{dd % 3}"])
            if sp >= 1 and dd == 0:
                sch.wait("tensor", "v_pj", 2 * sp)
            for sl01 in range(2):
                for jc in range(4):
                    mm(psum[:, sl01 * 2048 + jc * 512: sl01 * 2048 + jc * 512 + 512],
                       T["attnT"][:, dd, slots[sl01] * P:(slots[sl01] + 1) * P],
                       T[f"wpb{dd % 3}"][:, jc * 512:(jc + 1) * 512],
                       dd == 0, dd == DT - 1,
                       sem="t_pjd" if (sl01 == 1 and jc == 3) else None)
            pjd_h.append(sch.cnt["t_pjd"])
        for sl01, sl in enumerate(slots):
            r = RT + NSLOT + sl
            x1t = T[f"x1t{sl % 3}"]
            sch.wait("vector", "t_pjd", DT * (sp + 1))
            sch.wait("vector", f"ld_x{sl01}", sch.cnt[f"ld_x{sl01}"])
            if sl - 3 >= 0:
                sch.wait("vector", "st_x1", 16 * (sl - 2))
                if r - 3 in aff_h:
                    sch.wait("vector", "s_aff", aff_h[r - 3])
            vop("tensor_add", x1t[:], psum[:, sl01 * 2048:sl01 * 2048 + 2048],
                T[f"xt{sl01}"][:])
            vop("tensor_add", x1t[:], x1t[:], bproj_sb[:], sem="v_pj")
            sch.wait("gpsimd", "v_pj", sch.cnt["v_pj"])
            dma("gpsimd", x1_hbm[sl, :, :], x1t[:], "st_x1")
            ln_front(r, x1t, [("v_pj", sch.cnt["v_pj"])])
            if sl >= 2:
                rb = r - 2

                def dst2(lnt, sl2=sl - 2):
                    return tpose(T["xl2T"][:, :, sl2 * P:(sl2 + 1) * P], lnt[:])

                ln_back(rb, T[f"x1t{(sl - 2) % 3}"], dst2)
    for sl2 in (NSLOT - 2, NSLOT - 1):
        rb = RT + NSLOT + sl2

        def dst3(lnt, sl2=sl2):
            return tpose(T["xl2T"][:, :, sl2 * P:(sl2 + 1) * P], lnt[:])

        ln_back(rb, T[f"x1t{sl2 % 3}"], dst3)
    sch.wait("gpsimd", "st_x1", 16 * 16)
    phase_barrier()

    gu_grp = 0
    gutp_h = {}
    wdtp_h = {}
    dq_h = {}

    def d_load(ht):
        pa = ht % 2
        if ht >= 2:
            sch.wait("gpsimd", "s_dq", dq_h.get(ht - 2, 0))
        dma("gpsimd", T[f"wgr{pa}"][:], wg[ht * P:(ht + 1) * P, :], "ld_wg")
        dma("gpsimd", T[f"wur{pa}"][:], wu[ht * P:(ht + 1) * P, :], "ld_wu")

    def d_prep(ht):
        pa = ht % 2
        sch.wait("scalar", "ld_wg", 16 * (ht + 1))
        sch.wait("scalar", "ld_wu", 16 * (ht + 1))
        sch.wait("scalar", "v_stat", 3)
        if ht - 2 in gutp_h:
            sch.wait("scalar", "y_tp", gutp_h[ht - 2])
        act("scalar", T[f"wgd{pa}"][:], T[f"wgr{pa}"][:], AF.Identity, sem="s_dq",
            scale=sg_sb[:, ht:ht + 1], bias=sgb_sb[:, ht:ht + 1])
        act("scalar", T[f"wud{pa}"][:], T[f"wur{pa}"][:], AF.Identity, sem="s_dq",
            scale=su_sb[:, ht:ht + 1], bias=sub_sb[:, ht:ht + 1])
        dq_h[ht] = sch.cnt["s_dq"]
        sch.wait("sync", "s_dq", dq_h[ht])
        if ht >= 2:
            sch.wait("sync", "t_gu", 4 * (ht - 1))
        tpose(T[f"wgT{pa}"][:], T[f"wgd{pa}"][:])
        gutp_h[ht] = tpose(T[f"wuT{pa}"][:], T[f"wud{pa}"][:])

    d_load(0)
    d_prep(0)
    for ht in range(HT):
        pa = ht % 2
        if ht + 1 < HT:
            d_load(ht + 1)
            d_prep(ht + 1)
        for hh in range(2):
            goff = (gu_grp % 2) * 2048
            sch.wait("tensor", "y_tp", gutp_h[ht])
            if gu_grp >= 2:
                sch.wait("tensor", "s_sil", 2 * (gu_grp - 1))
                sch.wait("tensor", "v_act", 2 * (gu_grp - 1))
            for d in range(DT):
                for ci, cc in enumerate((2 * hh, 2 * hh + 1)):
                    mm(psum[:, goff + ci * 512:goff + ci * 512 + 512],
                       T[f"wgT{pa}"][:, d, :], T["xl2T"][:, d, cc * 512:(cc + 1) * 512],
                       d == 0, d == DT - 1)
                for ci, cc in enumerate((2 * hh, 2 * hh + 1)):
                    mm(psum[:, goff + 1024 + ci * 512:goff + 1024 + ci * 512 + 512],
                       T[f"wuT{pa}"][:, d, :], T["xl2T"][:, d, cc * 512:(cc + 1) * 512],
                       d == 0, d == DT - 1,
                       sem="t_gu" if (d == DT - 1 and ci == 1) else None)
            for ci, cc in enumerate((2 * hh, 2 * hh + 1)):
                sch.wait("scalar", "t_gu", sch.cnt["t_gu"])
                if gu_grp >= 2:
                    sch.wait("scalar", "v_act", 2 * (gu_grp - 1))
                act("scalar", T[f"sil{cc}"][:],
                    psum[:, goff + ci * 512:goff + ci * 512 + 512],
                    AF.Silu, sem="s_sil")
                sch.wait("vector", "s_sil", sch.cnt["s_sil"])
                if ht >= 2 and hh == 0 and ci == 0:
                    sch.wait("vector", "st_act", 16 * (ht - 1))
                def f_mul(E, o=T[f"actt{pa}"][:, cc * 512:(cc + 1) * 512],
                          i0=T[f"sil{cc}"][:],
                          i1=psum[:, goff + 1024 + ci * 512:goff + 1024 + ci * 512 + 512]):
                    return E.tensor_mul(o, i0, i1)
                sch.op("vector", f_mul, "v_act", 1)
            gu_grp += 1
        sch.wait("gpsimd", "v_act", sch.cnt["v_act"])
        dma("gpsimd", act_hbm[ht, :, :], T[f"actt{pa}"][:], "st_act")
        if ht >= 4 and ht % 2 == 0 and (ht - 4) // 2 < JT:
            jt = (ht - 4) // 2
            if jt >= 1:
                sch.wait("gpsimd", "s_dqd", jt)
            dma("gpsimd", T["wdr"][:], wd[jt * P:(jt + 1) * P, :], "ld_wd",
                max_dma_last_dim=8192)
            sch.wait("scalar", "ld_wd", 16 * (jt + 1))
            if jt - 1 in wdtp_h:
                sch.wait("scalar", "y_tp", wdtp_h[jt - 1])
            act("scalar", T["wdd"][:], T["wdr"][:], AF.Identity, sem="s_dqd",
                scale=sd_sb[:, jt:jt + 1], bias=sdb_sb[:, jt:jt + 1])
            sch.wait("sync", "s_dqd", sch.cnt["s_dqd"])
            if jt >= 1:
                sch.wait("sync", "st_wdt", 16 * jt)
            wdtp_h[jt] = tpose(T["wdTt"][:], T["wdd"][:])
            sch.wait("gpsimd", "y_tp", wdtp_h[jt])
            dma("gpsimd", wdt_hbm[jt, :, :], T["wdTt"][:], "st_wdt")
    sch.wait("gpsimd", "st_act", 16 * HT)
    sch.wait("gpsimd", "st_wdt", 16 * JT)
    phase_barrier()

    detp_h = {}
    dn_after = {}
    wdtb_cnt = {}
    for hh in range(2):
        for jt0 in range(3):
            g0 = hh * JT + jt0
            if g0 - 3 in dn_after:
                sch.wait("gpsimd", "t_dn", dn_after[g0 - 3])
            dma("gpsimd", T[f"wdTb{g0 % 3}"][:], wdt_hbm[jt0, :, :],
                f"ld_wdt{g0 % 3}")
            wdtb_cnt[g0] = sch.cnt[f"ld_wdt{g0 % 3}"]
        if hh == 1:
            sch.wait("gpsimd", "s_de", sch.cnt["s_de"])
        lact_cnt = {}
        for ht in range(HT):
            dma("gpsimd", T["actH"][:, ht, :],
                act_hbm[ht, :, hh * 1024:(hh + 1) * 1024], f"ld_act{ht % 4}")
            lact_cnt[ht] = sch.cnt[f"ld_act{ht % 4}"]
        if hh == 1:
            sch.wait("gpsimd", "st_out", 16 * 8)
            sch.wait("gpsimd", "v_oadd", sch.cnt["v_oadd"])
        for sl in range(8):
            dma("gpsimd", T["out_sb"][:, sl, :], x1_hbm[hh * 8 + sl, :, :],
                f"ld_out{sl % 2}")
        for jt in range(JT):
            g = hh * JT + jt
            pa = g % 2
            pw = g % 3
            if jt >= 3:
                sch.wait("gpsimd", "t_dn", dn_after[g - 3])
                dma("gpsimd", T[f"wdTb{pw}"][:], wdt_hbm[jt, :, :],
                    f"ld_wdt{pw}")
                wdtb_cnt[g] = sch.cnt[f"ld_wdt{pw}"]
            sch.wait("tensor", f"ld_wdt{pw}", wdtb_cnt[g])
            if jt > 0 or not E_INCR:
                for q4 in range(4):
                    sch.wait("tensor", f"ld_act{q4}", sch.cnt[f"ld_act{q4}"])
            if g >= 2:
                sch.wait("tensor", "s_de", g - 1)
            for ht in range(HT):
                if jt == 0 and E_INCR:
                    sch.wait("tensor", f"ld_act{ht % 4}", lact_cnt[ht])
                for ci in range(2):
                    mm(psum[:, pa * 1024 + ci * 512:pa * 1024 + ci * 512 + 512],
                       T[f"wdTb{pw}"][:, ht, :],
                       T["actH"][:, ht, ci * 512:(ci + 1) * 512],
                       ht == 0, ht == HT - 1,
                       sem="t_dn" if (ht == HT - 1 and ci == 1) else None)
            dn_after[g] = sch.cnt["t_dn"]
            sch.wait("scalar", "t_dn", sch.cnt["t_dn"])
            if g - 2 in detp_h:
                sch.wait("scalar", "y_tp", detp_h[g - 2])
            act("scalar", T[f"dTt{pa}"][:], psum[:, pa * 1024:pa * 1024 + 1024],
                AF.Identity, sem="s_de")
            sch.wait("sync", "s_de", sch.cnt["s_de"])
            if g >= 2:
                sch.wait("sync", "v_oadd", 8 * (g - 1))
            detp_h[g] = tpose(T[f"dnt{pa}"][:], T[f"dTt{pa}"][:])
            sch.wait("vector", "y_tp", detp_h[g])
            sch.wait("vector", "ld_out0", sch.cnt["ld_out0"])
            sch.wait("vector", "ld_out1", sch.cnt["ld_out1"])
            for nt in range(8):
                o = T["out_sb"][:, nt, jt * P:jt * P + P]
                def f_add(E, o=o, i1=T[f"dnt{pa}"][:, nt, :]):
                    return E.tensor_add(o, o, i1)
                sch.op("vector", f_add, "v_oadd", 1)
        sch.wait("gpsimd", "v_oadd", sch.cnt["v_oadd"])
        for sl in range(8):
            dma("gpsimd", out_h[hh * 8 + sl, :, :], T["out_sb"][:, sl, :], "st_out")
    sch.wait("gpsimd", "st_out", 16 * 16)

    global _LAST_SCHED
    _LAST_SCHED = sch
    sems = {}
    for name in sch.sem_names:
        sems[name] = e(nc.semaphore(name))
    block = e(nc.Block())

    def emit(E, stream):
        for item in stream:
            if item[0] == "w":
                E.wait_ge(sems[item[1]], item[2])
            else:
                _, fn, inc_sem, inc_amt = item
                inst = fn(E)
                if inc_sem is not None and inst is not None:
                    inst.then_inc(sems[inc_sem], inc_amt)

    @block.tensor
    def _(t):
        emit(t, sch.streams["tensor"])

    @block.scalar
    def _(sc):
        emit(sc, sch.streams["scalar"])

    @block.vector
    def _(v):
        emit(v, sch.streams["vector"])

    @block.sync
    def _(sy):
        emit(sy, sch.streams["sync"])

    @block.gpsimd
    def _(g):
        emit(g, sch.streams["gpsimd"])

    ctx.close()
    return nc


_LAST_SCHED = None
_NC_CACHE = None


def _get_nc():
    global _NC_CACHE
    if _NC_CACHE is None:
        _NC_CACHE = build_program()
    return _NC_CACHE


def kernel(L0_attn_x, L0_attn_W_qkv, L0_attn_b_qkv, L0_attn_W_proj, L0_attn_b_proj,
           L0_ffn_wq_gate, L0_ffn_s_gate, L0_ffn_wq_up, L0_ffn_s_up,
           L0_ffn_wq_down, L0_ffn_s_down):
    x = np.ascontiguousarray(np.asarray(L0_attn_x, dtype=np.float32))
    wqkv = np.ascontiguousarray(np.asarray(L0_attn_W_qkv, dtype=np.float32))
    bq = np.asarray(L0_attn_b_qkv, dtype=np.float32)
    wpn = np.ascontiguousarray(np.asarray(L0_attn_W_proj, dtype=np.float32))
    bp = np.asarray(L0_attn_b_proj, dtype=np.float32)
    wgn = np.ascontiguousarray(np.asarray(L0_ffn_wq_gate, dtype=np.int32))
    sgn = np.asarray(L0_ffn_s_gate, dtype=np.float32)
    wun = np.ascontiguousarray(np.asarray(L0_ffn_wq_up, dtype=np.int32))
    sun = np.asarray(L0_ffn_s_up, dtype=np.float32)
    wdn = np.ascontiguousarray(np.asarray(L0_ffn_wq_down, dtype=np.int32))
    sdn = np.asarray(L0_ffn_s_down, dtype=np.float32)

    bqkv_t = np.ascontiguousarray(bq.reshape(JQKV, P).T)
    bproj_t = np.ascontiguousarray(np.tile(bp[None, :], (P, 1)))
    sg_t = np.ascontiguousarray(sgn.reshape(HT, P).T)
    su_t = np.ascontiguousarray(sun.reshape(HT, P).T)
    sd_t = np.ascontiguousarray(sdn.reshape(JT, P).T)

    nc = _get_nc()
    in_maps = []
    metas = []
    for c in range(8):
        b = c // 2
        par = c % 2
        gl = [2 * i + par for i in range(NSLOT)]
        xb = x[b]
        xq_c = np.ascontiguousarray(xb.reshape(RT, P, D)[gl])
        tri = (np.arange(128)[None, :] <= np.arange(P)[:, None]).astype(np.float32)
        cm = np.zeros((P, 256), dtype=np.float32)
        if par == 0:
            cm[:, 0:128] = tri
        else:
            cm[:, 0:128] = 1.0
            cm[:, 128:256] = tri
        cmb = cm.astype(ml_dtypes.bfloat16)
        in_maps.append({
            "x_att": xb, "xq": xq_c, "wqkv": wqkv, "bqkv": bqkv_t,
            "wp": wpn, "bproj": bproj_t, "wg": wgn, "wu": wun, "wd": wdn,
            "sg": sg_t, "su": su_t, "sd": sd_t, "cmask": cmb,
        })
        metas.append((b, gl))

    res = run_bass_kernel_spmd(nc, in_maps, core_ids=list(range(8)))

    out = np.empty((B, S, D), dtype=np.float32)
    for c in range(8):
        b, gl = metas[c]
        oc = np.asarray(res.results[c]["out"])
        for i, g in enumerate(gl):
            out[b, g * P:(g + 1) * P, :] = oc[i]
    return out


# version 23
# speedup vs baseline: 1.2279x; 1.0158x over previous
import numpy as np
import ml_dtypes
import concourse.bass as bass
import concourse.mybir as mybir
from concourse.bass_utils import run_bass_kernel_spmd

F32 = mybir.dt.float32
BF16 = mybir.dt.bfloat16
I32 = mybir.dt.int32
AF = mybir.ActivationFunctionType
ALU = mybir.AluOpType
AXX = mybir.AxisListType.X

P = 128
B = 4
S = 4096
D = 2048
DT = D // P
NH = 16
HD = 128
JQKV = 18
HID = 5632
HT = HID // P
JT = D // P
NSLOT = 16
RT = S // P
SCALE = float(HD) ** -0.5
EPS = 1e-5

KE2 = [256 * (i + 1) for i in range(NSLOT)]

ENGS = ("tensor", "scalar", "vector", "sync", "gpsimd")
E_PRELOAD = False
E_INCR = True


class Sched:
    def __init__(self):
        self.streams = {e: [] for e in ENGS}
        self.cnt = {}
        self.sem_names = []

    def sem(self, name):
        if name not in self.cnt:
            self.cnt[name] = 0
            self.sem_names.append(name)

    def wait(self, eng, sem, val):
        val = min(val, self.cnt[sem])
        if val <= 0:
            return
        self.streams[eng].append(("w", sem, val))

    def inc(self, sem, amt):
        self.cnt[sem] += amt
        return self.cnt[sem]

    def op(self, eng, fn, inc_sem=None, inc_amt=1):
        if inc_sem is not None:
            self.inc(inc_sem, inc_amt)
        self.streams[eng].append(("i", fn, inc_sem, inc_amt))


def build_program():
    nc = bass.Bass("TRN2", target_bir_lowering=False)

    x_att = nc.dram_tensor("x_att", [S, D], F32, kind="ExternalInput").ap()
    xq = nc.dram_tensor("xq", [NSLOT, P, D], F32, kind="ExternalInput").ap()
    wqkv = nc.dram_tensor("wqkv", [D, 2304], F32, kind="ExternalInput").ap()
    bqkv = nc.dram_tensor("bqkv", [P, JQKV], F32, kind="ExternalInput").ap()
    wp = nc.dram_tensor("wp", [D, D], F32, kind="ExternalInput").ap()
    bproj = nc.dram_tensor("bproj", [P, D], F32, kind="ExternalInput").ap()
    wg = nc.dram_tensor("wg", [HID, D], I32, kind="ExternalInput").ap()
    wu = nc.dram_tensor("wu", [HID, D], I32, kind="ExternalInput").ap()
    wd = nc.dram_tensor("wd", [D, HID], I32, kind="ExternalInput").ap()
    sg = nc.dram_tensor("sg", [P, HT], F32, kind="ExternalInput").ap()
    su = nc.dram_tensor("su", [P, HT], F32, kind="ExternalInput").ap()
    sd = nc.dram_tensor("sd", [P, JT], F32, kind="ExternalInput").ap()
    cmask_h = nc.dram_tensor("cmask", [P, 256], BF16, kind="ExternalInput").ap()
    out_h = nc.dram_tensor("out", [NSLOT, P, D], F32, kind="ExternalOutput").ap()

    qt_hbm = nc.dram_tensor("qt_hbm", [NH, P, 2048], BF16, kind="Internal").ap()
    wpbf_hbm = nc.dram_tensor("wpbf_hbm", [DT, P, D], BF16, kind="Internal").ap()
    x1_hbm = nc.dram_tensor("x1_hbm", [NSLOT, P, D], F32, kind="Internal").ap()
    act_hbm = nc.dram_tensor("act_hbm", [HT, P, 2048], BF16, kind="Internal").ap()
    wdt_hbm = nc.dram_tensor("wdt_hbm", [JT, P, HT * P], BF16, kind="Internal").ap()

    from contextlib import ExitStack
    ctx = ExitStack()
    e = ctx.enter_context
    bqkv_sb = e(nc.sbuf_tensor("bqkv_sb", [P, JQKV], F32))
    bproj_sb = e(nc.sbuf_tensor("bproj_sb", [P, D], F32))
    sg_sb = e(nc.sbuf_tensor("sg_sb", [P, HT], F32))
    su_sb = e(nc.sbuf_tensor("su_sb", [P, HT], F32))
    sd_sb = e(nc.sbuf_tensor("sd_sb", [P, JT], F32))
    sgb_sb = e(nc.sbuf_tensor("sgb_sb", [P, HT], F32))
    sub_sb = e(nc.sbuf_tensor("sub_sb", [P, HT], F32))
    sdb_sb = e(nc.sbuf_tensor("sdb_sb", [P, JT], F32))
    bnb = [e(nc.sbuf_tensor(f"bnb{i}", [P, 24], F32)) for i in range(4)]
    mv = [e(nc.sbuf_tensor(f"mv{i}", [P, 2], F32)) for i in range(4)]
    vpe = [e(nc.sbuf_tensor(f"vpe{i}", [P, 1], F32)) for i in range(4)]
    lnv = [e(nc.sbuf_tensor(f"lnv{i}", [P, 1], F32)) for i in range(4)]
    rstd = [e(nc.sbuf_tensor(f"rstd{i}", [P, 1], F32)) for i in range(4)]
    nmr = [e(nc.sbuf_tensor(f"nmr{i}", [P, 1], F32)) for i in range(4)]
    smc = [e(nc.sbuf_tensor(f"smc{i}", [P, 8], F32)) for i in range(4)]
    zv = [e(nc.sbuf_tensor(f"zv{i}", [P, 1], F32)) for i in range(4)]
    rcv = [e(nc.sbuf_tensor(f"rcv{i}", [P, 1], F32)) for i in range(4)]

    SEG = {
        "wqkv_sb": ("A", [P, DT, 2304], BF16),
        "xt0": ("AC", [P, D], F32), "xt1": ("AC", [P, D], F32),
        "xt2": ("A", [P, D], F32), "xt3": ("A", [P, D], F32),
        "xt4": ("A", [P, D], F32),
        "lnt0": ("AC", [P, D], BF16), "lnt1": ("AC", [P, D], BF16),
        "lnt2": ("AC", [P, D], BF16),
        "xlk0": ("A", [P, DT, 512], BF16), "xlk1": ("A", [P, DT, 512], BF16),
        "kt_sb": ("AB", [P, S], BF16),
        "vt_sb": ("A", [P, S], BF16),
        "v_sb": ("AB", [P, RT, P], BF16),
        "qev0": ("A", [P, 512], BF16), "qev1": ("A", [P, 512], BF16),
        "qev2": ("A", [P, 512], BF16), "qev3": ("A", [P, 512], BF16),
        "wpc0": ("A", [P, D], BF16), "wpc1": ("A", [P, D], BF16),
        "cmask_sb": ("AB", [P, 256], BF16),
        "qth0": ("B", [P, 2048], BF16), "qth1": ("B", [P, 2048], BF16),
        "prb0": ("B", [P, 4096], BF16), "prb1": ("B", [P, 4096], BF16),
        "prb2": ("B", [P, 4096], BF16), "prb3": ("B", [P, 4096], BF16),
        "pT0": ("B", [P, 32, 256], BF16), "pT1": ("B", [P, 32, 256], BF16),
        "pT2": ("B", [P, 32, 256], BF16), "pT3": ("B", [P, 32, 256], BF16),
        "attnT": ("BC", [P, NH, 2048], BF16),
        "wpb0": ("C", [P, D], BF16), "wpb1": ("C", [P, D], BF16),
        "wpb2": ("C", [P, D], BF16),
        "x1t0": ("C", [P, D], F32), "x1t1": ("C", [P, D], F32),
        "x1t2": ("C", [P, D], F32),
        "xl2T": ("CD", [P, DT, 2048], BF16),
        "wgr0": ("D", [P, D], BF16), "wgr1": ("D", [P, D], BF16),
        "wur0": ("D", [P, D], BF16), "wur1": ("D", [P, D], BF16),
        "wgd0": ("D", [P, D], BF16), "wgd1": ("D", [P, D], BF16),
        "wud0": ("D", [P, D], BF16), "wud1": ("D", [P, D], BF16),
        "wgT0": ("D", [P, DT, P], BF16), "wgT1": ("D", [P, DT, P], BF16),
        "wuT0": ("D", [P, DT, P], BF16), "wuT1": ("D", [P, DT, P], BF16),
        "sil0": ("D", [P, 512], BF16), "sil1": ("D", [P, 512], BF16),
        "sil2": ("D", [P, 512], BF16), "sil3": ("D", [P, 512], BF16),
        "actt0": ("D", [P, 2048], BF16), "actt1": ("D", [P, 2048], BF16),
        "wdr": ("D", [P, HID], BF16),
        "wdd": ("D", [P, HID], BF16),
        "wdTt": ("D", [P, HT, P], BF16),
        "actH": ("E", [P, HT, 1024], BF16),
        "out_sb": ("E", [P, 8, D], F32),
        "wdTb0": ("E", [P, HT, P], BF16), "wdTb1": ("E", [P, HT, P], BF16),
        "wdTb2": ("E", [P, HT, P], BF16),
        "dTt0": ("E", [P, 1024], BF16), "dTt1": ("E", [P, 1024], BF16),
        "dnt0": ("E", [P, 8, P], BF16), "dnt1": ("E", [P, 8, P], BF16),
    }

    def seg_bytes(shape, dtype):
        n = 1
        for s_ in shape[1:]:
            n *= s_
        return n * mybir.dt.size(dtype)

    order = sorted(SEG.items(), key=lambda kv: -seg_bytes(kv[1][1], kv[1][2]))
    placed = []
    offsets = {}
    for name, (phases, shape, dtype) in order:
        sz = (seg_bytes(shape, dtype) + 63) // 64 * 64
        cands = sorted(set([0] + [pe for (pp, po, pe) in placed if set(pp) & set(phases)]))
        for off in cands:
            end = off + sz
            if all((not (set(pp) & set(phases))) or end <= po or off >= pe
                   for (pp, po, pe) in placed):
                placed.append((phases, off, end))
                offsets[name] = off
                break
        else:
            raise RuntimeError("no fit " + name)
    arena_sz = max(pe for (_, _, pe) in placed)
    assert arena_sz <= 202364, f"arena {arena_sz}"
    arena = e(nc.sbuf_tensor("arena", [P, arena_sz], mybir.dt.uint8))
    arena_addr = nc.lookup_mloc(arena).addr

    T = {}
    for name, (phases, shape, dtype) in SEG.items():
        h = nc.virtual_sbuf_tensor(name, shape, dtype)
        nc._tensor(h.name, h.shape, h.dtype, type="SB")
        mloc = nc.lookup_mloc(h)
        mloc.allocated = True
        mloc.addr = arena_addr + offsets[name]
        T[name] = h.ap()

    psum = e(nc.psum_tensor("psum", [P, 4096], F32))

    sch = Sched()
    for s_ in ("ld_cst", "ld_x0", "ld_x1", "ld_qh", "ld_wp0", "ld_wp1", "ld_wp2",
               "ld_wp3", "ld_wg", "ld_wu", "ld_wd", "ld_wdt0", "ld_wdt1",
               "ld_act0", "ld_act1", "ld_act2", "ld_act3", "ld_out0",
               "ld_out1", "ld_wpc", "st_wpc", "ld_wdt2",
               "st_q", "st_x1", "st_act", "st_wdt", "st_out",
               "v_stat", "v_nmr", "v_dg", "v_mx", "v_nm", "v_act", "v_pj", "v_oadd",
               "s_rstd", "s_aff", "s_ev", "s_xp", "s_pe", "s_sil", "s_dq", "s_dqd",
               "s_de", "s_kv", "s_qe",
               "t_mm", "t_qkv", "t_pv", "t_pj", "t_pjd", "t_gu", "t_dn",
               "y_tp", "ph"):
        sch.sem(s_)

    def dma(eng, dst, src, sem, ser=True, **kw):
        if ser and sch.cnt[sem] > 0:
            sch.wait(eng, sem, sch.cnt[sem])
        def fn(E, dst=dst, src=src, kw=kw):
            return E.dma_start(dst, src, **kw)
        sch.op(eng, fn, sem, 16)

    def tpose(dst, src):
        if sch.cnt["y_tp"] > 0:
            sch.wait("sync", "y_tp", sch.cnt["y_tp"])
        def fn(E, dst=dst, src=src):
            return E.dma_start(dst, src, transpose=True)
        sch.op("sync", fn, "y_tp", 16)
        return sch.cnt["y_tp"]

    def act(eng, out, in_, func, sem=None, **kw):
        def fn(E, out=out, in_=in_, func=func, kw=kw):
            return E.activation(out, in_, func, **kw)
        sch.op(eng, fn, sem, 1)

    def mm(out, lhsT, rhs, start, stop, sem=None):
        def fn(E, out=out, lhsT=lhsT, rhs=rhs, start=start, stop=stop):
            return E.matmul(out, lhsT, rhs, start=start, stop=stop)
        sch.op("tensor", fn, sem, 1)

    def vop(meth, *args, sem=None, **kw):
        def fn(E, meth=meth, args=args, kw=kw):
            return getattr(E, meth)(*args, **kw)
        sch.op("vector", fn, sem, 1)

    def phase_barrier():
        for eng in ENGS:
            def fn2(E):
                return E.nop()
            sch.op(eng, fn2, "ph", 1)
        tgt = sch.cnt["ph"]
        for eng in ENGS:
            sch.wait(eng, "ph", tgt)

    dma("gpsimd", bqkv_sb[:], bqkv[:], "ld_cst")
    dma("gpsimd", bproj_sb[:], bproj[:], "ld_cst")
    dma("gpsimd", sg_sb[:], sg[:], "ld_cst")
    dma("gpsimd", su_sb[:], su[:], "ld_cst")
    dma("gpsimd", sd_sb[:], sd[:], "ld_cst")
    dma("gpsimd", T["cmask_sb"][:], cmask_h[:], "ld_cst")
    for t in range(DT):
        dma("gpsimd", T["wqkv_sb"][:, t, :], wqkv[t * P:(t + 1) * P, :], "ld_cst")
    n_cst = sch.cnt["ld_cst"]
    sch.wait("vector", "ld_cst", n_cst)
    vop("tensor_scalar_mul", sgb_sb[:], sg_sb[:], -128.0, sem="v_stat")
    vop("tensor_scalar_mul", sub_sb[:], su_sb[:], -128.0, sem="v_stat")
    vop("tensor_scalar_mul", sdb_sb[:], sd_sb[:], -128.0, sem="v_stat")

    aff_h, rstd_h, nmr_h, stat_h, lntp_h = {}, {}, {}, {}, {}

    def ln_front(r, src_ap, gate):
        pa = r % 4
        for sem_, val_ in gate:
            sch.wait("vector", sem_, val_)
        if r - 4 in aff_h:
            sch.wait("vector", "s_aff", aff_h[r - 4])
        for i in range(4):
            vop("bn_stats", bnb[pa][:, 6 * i:6 * (i + 1)],
                src_ap[:, 512 * i:512 * (i + 1)])
        vop("drain")
        vop("bn_aggr", mv[pa][:], bnb[pa][:])
        vop("drain")
        if r - 4 in rstd_h:
            sch.wait("vector", "s_rstd", rstd_h[r - 4])
        vop("tensor_scalar_add", vpe[pa][:], mv[pa][:, 1:2], EPS, sem="v_stat")
        stat_h[r] = sch.cnt["v_stat"]
        sch.wait("scalar", "v_stat", stat_h[r])
        if r - 4 in nmr_h:
            sch.wait("scalar", "v_nmr", nmr_h[r - 4])
        act("scalar", lnv[pa][:], vpe[pa][:], AF.Ln)
        act("scalar", rstd[pa][:], lnv[pa][:], AF.Exp, sem="s_rstd", scale=-0.5)
        rstd_h[r] = sch.cnt["s_rstd"]
        sch.wait("vector", "s_rstd", rstd_h[r])
        def f_nmr(E, o=nmr[pa][:], i0=mv[pa][:, 0:1], s1=rstd[pa][:]):
            return E.tensor_scalar(o, i0, s1, -1.0, ALU.mult, ALU.mult)
        sch.op("vector", f_nmr, "v_nmr", 1)
        nmr_h[r] = sch.cnt["v_nmr"]

    def ln_back(r, src_ap, dst_fn):
        pa = r % 4
        lnt = T[f"lnt{r % 3}"]
        sch.wait("scalar", "v_nmr", nmr_h[r])
        if r - 3 in lntp_h:
            sch.wait("scalar", "y_tp", lntp_h[r - 3])
        act("scalar", lnt[:], src_ap[:], AF.Identity, sem="s_aff",
            scale=rstd[pa][:], bias=nmr[pa][:])
        aff_h[r] = sch.cnt["s_aff"]
        sch.wait("sync", "s_aff", aff_h[r])
        lntp_h[r] = dst_fn(lnt)

    kv_ev = 0
    q_ev = 0
    qkv_chunk_h = {}

    def emit_chunk(gch):
        nonlocal kv_ev, q_ev
        is_q = gch >= 8
        xlk = T[f"xlk{gch % 2}"]
        sch.wait("tensor", "y_tp", lntp_h[4 * gch + 3])
        sch.wait("tensor", "ld_cst", n_cst)
        if not is_q:
            c = gch
            for ji, jt in enumerate((16, 17)):
                off = (c % 2) * 1024 + ji * 512
                bank = psum[:, off:off + 512]
                if kv_ev + 1 > 4:
                    sch.wait("tensor", "s_kv", kv_ev - 3)
                for d in range(DT):
                    mm(bank, T["wqkv_sb"][:, d, jt * P:(jt + 1) * P], xlk[:, d, :],
                       d == 0, d == DT - 1, sem="t_qkv" if d == DT - 1 else None)
                dst = (T["kt_sb"] if jt == 16 else T["vt_sb"])[:, c * 512:(c + 1) * 512]
                sch.wait("scalar", "t_qkv", sch.cnt["t_qkv"])
                act("scalar", dst, bank, AF.Identity, sem="s_kv",
                    bias=bqkv_sb[:, jt:jt + 1])
                kv_ev += 1
        else:
            qc = gch - 8
            for jt in range(16):
                n = qc * 16 + jt
                off = 2048 + (n % 4) * 512
                bank = psum[:, off:off + 512]
                if n >= 4:
                    sch.wait("tensor", "s_qe", n - 3)
                for d in range(DT):
                    mm(bank, T["wqkv_sb"][:, d, jt * P:(jt + 1) * P], xlk[:, d, :],
                       d == 0, d == DT - 1, sem="t_qkv" if d == DT - 1 else None)
                qb = T[f"qev{n % 4}"]
                sch.wait("scalar", "t_qkv", sch.cnt["t_qkv"])
                if n >= 4:
                    sch.wait("scalar", "st_q", 16 * (n - 3))
                act("scalar", qb[:], bank, AF.Identity, sem="s_qe",
                    bias=bqkv_sb[:, jt:jt + 1])
                q_ev += 1
                sch.wait("gpsimd", "s_qe", q_ev)
                dma("gpsimd", qt_hbm[jt, :, qc * 512:(qc + 1) * 512], qb[:], "st_q")
        qkv_chunk_h[gch] = sch.cnt["t_qkv"]

    def a_back(rb):
        xtb = T[f"xt{rb % 5}"]

        def dst_fn(lnt, cbuf=(rb // 4) % 2, tt=rb % 4, gch=rb // 4):
            if gch - 2 in qkv_chunk_h:
                sch.wait("sync", "t_qkv", qkv_chunk_h[gch - 2])
            return tpose(T[f"xlk{cbuf}"][:, :, tt * P:(tt + 1) * P], lnt[:])

        ln_back(rb, xtb, dst_fn)
        if rb % 4 == 3:
            emit_chunk(rb // 4)

    for r in range(RT + NSLOT):
        is_q = r >= RT
        xt = T[f"xt{r % 5}"]
        if r - 5 in aff_h:
            sch.wait("gpsimd", "s_aff", aff_h[r - 5])
            sch.wait("gpsimd", "v_stat", stat_h[r - 5])
        src = x_att[(r * P):(r + 1) * P, :] if not is_q else xq[r - RT, :, :]
        dma("gpsimd", xt[:], src, f"ld_x{r % 2}")
        if r < DT:
            if r >= 2:
                sch.wait("gpsimd", "st_wpc", 16 * (r - 1))
            dma("gpsimd", T[f"wpc{r % 2}"][:], wp[r * P:(r + 1) * P, :], "ld_wpc")
            if r >= 1:
                sch.wait("gpsimd", "ld_wpc", 16 * r)
                dma("gpsimd", wpbf_hbm[r - 1, :, :], T[f"wpc{(r - 1) % 2}"][:],
                    "st_wpc")
        elif r == DT:
            sch.wait("gpsimd", "ld_wpc", 16 * DT)
            dma("gpsimd", wpbf_hbm[DT - 1, :, :], T[f"wpc{(DT - 1) % 2}"][:],
                "st_wpc")
        ln_front(r, xt, [(f"ld_x{r % 2}", sch.cnt[f"ld_x{r % 2}"])])
        if r >= 2:
            a_back(r - 2)
    a_back(RT + NSLOT - 2)
    a_back(RT + NSLOT - 1)
    sch.wait("sync", "s_kv", kv_ev)
    v_tp_h = tpose(T["v_sb"][:], T["vt_sb"][:])
    sch.wait("gpsimd", "st_q", 16 * 64)
    sch.wait("gpsimd", "st_wpc", 16 * DT)
    phase_barrier()

    NSB = 6
    tenant = [0] * NSB
    nb = 0
    tp_h = {}
    xp_h = {}
    nm_h = {}
    mm_after = {}
    head_mm = {}
    pv_cnt = {}
    pe_cnt = {}
    slot_plan = {}
    PVB = [3072, 3584]

    def plan_slot(u):
        nonlocal nb
        i_loc = 2 * ((u // 2) % 8) + (u & 1)
        E = KE2[i_loc]
        fds = [512] * (E // 512) + ([256] if E % 512 else [])
        segs = []
        koff = 0
        k = 0
        while k < len(fds):
            L = min(4, NSB - nb, len(fds) - k)
            segs.append((nb, fds[k:k + L], koff))
            koff += sum(fds[k:k + L])
            nb = (nb + L) % NSB
            k += L
        sb, sfds, _ = segs[-1]
        dbo = sb * 512 + sum(sfds) - 256
        return [segs, dbo, E, 0]

    def b_slot(u, h):
        segs, dbo, E, _ = slot_plan[u]
        pa = u % 4
        prb = T[f"prb{pa}"]
        i_loc = 2 * ((u // 2) % 8) + (u & 1)
        if u - 4 in tp_h:
            sch.wait("scalar", "y_tp", tp_h[u - 4])
            sch.wait("scalar", "v_nm", nm_h[u - 4])
        nseg = 0
        for si, (b0, sfds, koff) in enumerate(segs):
            last = si == len(segs) - 1
            for ci, fd in enumerate(sfds):
                b = b0 + ci
                if tenant[b] > 0:
                    sch.wait("tensor", "s_xp", tenant[b])
                k0 = koff + sum(sfds[:ci])
                mm(psum[:, b * 512:b * 512 + fd],
                   T[f"qth{h % 2}"][:, i_loc * P:(i_loc + 1) * P],
                   T["kt_sb"][:, k0:k0 + fd], True, True, sem="t_mm")
            sch.wait("scalar", "t_mm", sch.cnt["t_mm"])
            fd = sum(sfds) - (256 if last else 0)
            if fd > 0:
                act("scalar", prb[:, koff:koff + fd],
                    psum[:, b0 * 512:b0 * 512 + fd],
                    AF.Exp, sem="s_xp", scale=SCALE,
                    accum_out=smc[pa][:, nseg:nseg + 1])
                nseg += 1
            if last:
                act("scalar", prb[:, E - 256:E], psum[:, dbo:dbo + 256],
                    AF.Exp, sem="s_xp", scale=SCALE)
            for ci in range(len(sfds)):
                tenant[b0 + ci] = sch.cnt["s_xp"]
        xp_h[u] = sch.cnt["s_xp"]
        mm_after[u] = sch.cnt["t_mm"]
        slot_plan[u][3] = nseg

    def b_post(u):
        segs, dbo, E, nseg = slot_plan[u]
        pa = u % 4
        prb = T[f"prb{pa}"]
        sch.wait("vector", "s_xp", xp_h[u])

        def f_stt(Eg, o=prb[:, E - 256:E], i1=T["cmask_sb"][:],
                  acc=smc[pa][:, nseg:nseg + 1]):
            return Eg.scalar_tensor_tensor(out=o, in0=o, scalar=1.0, in1=i1,
                                           op0=ALU.mult, op1=ALU.mult,
                                           accum_out=acc)
        sch.op("vector", f_stt, "v_dg", 1)
        vop("drain")
        vop("reduce_sum", zv[pa][:], smc[pa][:, 0:nseg + 1], AXX)
        vop("drain")
        vop("reciprocal", rcv[pa][:], zv[pa][:])
        vop("drain")
        if u % 2 == 0:
            vop("tensor_scalar_mul", prb[:, 0:E], prb[:, 0:E], rcv[pa][:])
            vop("memset", prb[:, E:E + 256], 0.0, sem="v_nm")
        else:
            vop("tensor_scalar_mul", prb[:, 0:E], prb[:, 0:E], rcv[pa][:],
                sem="v_nm")
        nm_h[u] = sch.cnt["v_nm"]

    def b_tpose(u):
        pa = u % 4
        j = u // 2
        E = slot_plan[u][2]
        Ep = E + 256 if u % 2 == 0 else E
        sch.wait("sync", "v_nm", nm_h[u])
        if j - 4 in pv_cnt:
            sch.wait("sync", "t_pv", pv_cnt[j - 4])
        tp_h[u] = tpose(T[f"pT{j % 4}"][:, 0:Ep // 128, (u & 1) * P:((u & 1) + 1) * P],
                        T[f"prb{pa}"][:, 0:Ep])

    def emit_pv(j):
        pr_ = j % 8
        nkt = 4 * (pr_ + 1)
        sch.wait("tensor", "y_tp", tp_h[2 * j + 1])
        if j - 2 in pe_cnt:
            sch.wait("tensor", "s_pe", pe_cnt[j - 2])
        pvb = psum[:, PVB[j % 2]:PVB[j % 2] + 256]
        for kt in range(nkt):
            mm(pvb, T["v_sb"][:, kt, :], T[f"pT{j % 4}"][:, kt, :],
               kt == 0, kt == nkt - 1, sem="t_pv" if kt == nkt - 1 else None)
        pv_cnt[j] = sch.cnt["t_pv"]

    def emit_pv_evac(j):
        h_, pr_ = j // 8, j % 8
        sch.wait("vector", "t_pv", pv_cnt[j])
        pvb = psum[:, PVB[j % 2]:PVB[j % 2] + 256]
        vop("tensor_copy", T["attnT"][:, h_, pr_ * 256:(pr_ + 1) * 256], pvb,
            sem="s_pe")
        pe_cnt[j] = sch.cnt["s_pe"]

    for idx in range(128):
        h, pr = idx // 8, idx % 8
        if pr == 0:
            if h >= 2:
                sch.wait("gpsimd", "t_mm", head_mm[h - 2])
            dma("gpsimd", T[f"qth{h % 2}"][:], qt_hbm[h, :, :], "ld_qh")
        if idx >= 3:
            emit_pv(idx - 3)
        if idx >= 4:
            emit_pv_evac(idx - 4)
        for s01 in range(2):
            u = 2 * idx + s01
            slot_plan[u] = plan_slot(u)
            sch.wait("tensor", "ld_qh", 16 * (h + 1))
            b_slot(u, h)
            b_post(u)
            b_tpose(u)
        if pr == 7:
            head_mm[h] = sch.cnt["t_mm"]
    emit_pv(125)
    emit_pv_evac(124)
    emit_pv(126)
    emit_pv_evac(125)
    emit_pv(127)
    emit_pv_evac(126)
    emit_pv_evac(127)
    phase_barrier()

    pjd_h = []
    for sp in range(8):
        slots = (2 * sp, 2 * sp + 1)
        for sl01, sl in enumerate(slots):
            if sp >= 1:
                sch.wait("gpsimd", "v_pj", 2 * sp)
            dma("gpsimd", T[f"xt{sl01}"][:], xq[sl, :, :], f"ld_x{sl01}")
        for dd in range(DT):
            gdd = sp * DT + dd
            if gdd >= 3:
                sch.wait("gpsimd", "t_pjd", pjd_h[gdd - 3])
            dma("gpsimd", T[f"wpb{dd % 3}"][:], wpbf_hbm[dd, :, :],
                f"ld_wp{dd % 3}")
            sch.wait("tensor", f"ld_wp{dd % 3}", sch.cnt[f"ld_wp{dd % 3}"])
            if sp >= 1 and dd == 0:
                sch.wait("tensor", "v_pj", 2 * sp)
            for sl01 in range(2):
                for jc in range(4):
                    mm(psum[:, sl01 * 2048 + jc * 512: sl01 * 2048 + jc * 512 + 512],
                       T["attnT"][:, dd, slots[sl01] * P:(slots[sl01] + 1) * P],
                       T[f"wpb{dd % 3}"][:, jc * 512:(jc + 1) * 512],
                       dd == 0, dd == DT - 1,
                       sem="t_pjd" if (sl01 == 1 and jc == 3) else None)
            pjd_h.append(sch.cnt["t_pjd"])
        for sl01, sl in enumerate(slots):
            r = RT + NSLOT + sl
            x1t = T[f"x1t{sl % 3}"]
            sch.wait("vector", "t_pjd", DT * (sp + 1))
            sch.wait("vector", f"ld_x{sl01}", sch.cnt[f"ld_x{sl01}"])
            if sl - 3 >= 0:
                sch.wait("vector", "st_x1", 16 * (sl - 2))
                if r - 3 in aff_h:
                    sch.wait("vector", "s_aff", aff_h[r - 3])
            vop("tensor_add", x1t[:], psum[:, sl01 * 2048:sl01 * 2048 + 2048],
                T[f"xt{sl01}"][:])
            vop("tensor_add", x1t[:], x1t[:], bproj_sb[:], sem="v_pj")
            sch.wait("gpsimd", "v_pj", sch.cnt["v_pj"])
            dma("gpsimd", x1_hbm[sl, :, :], x1t[:], "st_x1")
            ln_front(r, x1t, [("v_pj", sch.cnt["v_pj"])])
            if sl >= 2:
                rb = r - 2

                def dst2(lnt, sl2=sl - 2):
                    return tpose(T["xl2T"][:, :, sl2 * P:(sl2 + 1) * P], lnt[:])

                ln_back(rb, T[f"x1t{(sl - 2) % 3}"], dst2)
    for sl2 in (NSLOT - 2, NSLOT - 1):
        rb = RT + NSLOT + sl2

        def dst3(lnt, sl2=sl2):
            return tpose(T["xl2T"][:, :, sl2 * P:(sl2 + 1) * P], lnt[:])

        ln_back(rb, T[f"x1t{sl2 % 3}"], dst3)
    sch.wait("gpsimd", "st_x1", 16 * 16)
    phase_barrier()

    gu_grp = 0
    gutp_h = {}
    wdtp_h = {}
    dq_h = {}

    def d_load(ht):
        pa = ht % 2
        if ht >= 2:
            sch.wait("gpsimd", "s_dq", dq_h.get(ht - 2, 0))
        dma("gpsimd", T[f"wgr{pa}"][:], wg[ht * P:(ht + 1) * P, :], "ld_wg")
        dma("gpsimd", T[f"wur{pa}"][:], wu[ht * P:(ht + 1) * P, :], "ld_wu")

    def d_prep(ht):
        pa = ht % 2
        sch.wait("scalar", "ld_wg", 16 * (ht + 1))
        sch.wait("scalar", "ld_wu", 16 * (ht + 1))
        sch.wait("scalar", "v_stat", 3)
        if ht - 2 in gutp_h:
            sch.wait("scalar", "y_tp", gutp_h[ht - 2])
        act("scalar", T[f"wgd{pa}"][:], T[f"wgr{pa}"][:], AF.Identity, sem="s_dq",
            scale=sg_sb[:, ht:ht + 1], bias=sgb_sb[:, ht:ht + 1])
        act("scalar", T[f"wud{pa}"][:], T[f"wur{pa}"][:], AF.Identity, sem="s_dq",
            scale=su_sb[:, ht:ht + 1], bias=sub_sb[:, ht:ht + 1])
        dq_h[ht] = sch.cnt["s_dq"]
        sch.wait("sync", "s_dq", dq_h[ht])
        if ht >= 2:
            sch.wait("sync", "t_gu", 4 * (ht - 1))
        tpose(T[f"wgT{pa}"][:], T[f"wgd{pa}"][:])
        gutp_h[ht] = tpose(T[f"wuT{pa}"][:], T[f"wud{pa}"][:])

    d_load(0)
    d_prep(0)
    for ht in range(HT):
        pa = ht % 2
        if ht + 1 < HT:
            d_load(ht + 1)
            d_prep(ht + 1)
        for hh in range(2):
            goff = (gu_grp % 2) * 2048
            sch.wait("tensor", "y_tp", gutp_h[ht])
            if gu_grp >= 2:
                sch.wait("tensor", "s_sil", 2 * (gu_grp - 1))
                sch.wait("tensor", "v_act", 2 * (gu_grp - 1))
            for d in range(DT):
                for ci, cc in enumerate((2 * hh, 2 * hh + 1)):
                    mm(psum[:, goff + ci * 512:goff + ci * 512 + 512],
                       T[f"wgT{pa}"][:, d, :], T["xl2T"][:, d, cc * 512:(cc + 1) * 512],
                       d == 0, d == DT - 1)
                for ci, cc in enumerate((2 * hh, 2 * hh + 1)):
                    mm(psum[:, goff + 1024 + ci * 512:goff + 1024 + ci * 512 + 512],
                       T[f"wuT{pa}"][:, d, :], T["xl2T"][:, d, cc * 512:(cc + 1) * 512],
                       d == 0, d == DT - 1,
                       sem="t_gu" if (d == DT - 1 and ci == 1) else None)
            for ci, cc in enumerate((2 * hh, 2 * hh + 1)):
                sch.wait("scalar", "t_gu", sch.cnt["t_gu"])
                if gu_grp >= 2:
                    sch.wait("scalar", "v_act", 2 * (gu_grp - 1))
                act("scalar", T[f"sil{cc}"][:],
                    psum[:, goff + ci * 512:goff + ci * 512 + 512],
                    AF.Silu, sem="s_sil")
                sch.wait("vector", "s_sil", sch.cnt["s_sil"])
                if ht >= 2 and hh == 0 and ci == 0:
                    sch.wait("vector", "st_act", 16 * (ht - 1))
                def f_mul(E, o=T[f"actt{pa}"][:, cc * 512:(cc + 1) * 512],
                          i0=T[f"sil{cc}"][:],
                          i1=psum[:, goff + 1024 + ci * 512:goff + 1024 + ci * 512 + 512]):
                    return E.tensor_mul(o, i0, i1)
                sch.op("vector", f_mul, "v_act", 1)
            gu_grp += 1
        sch.wait("gpsimd", "v_act", sch.cnt["v_act"])
        dma("gpsimd", act_hbm[ht, :, :], T[f"actt{pa}"][:], "st_act")
        if ht >= 4 and ht % 2 == 0 and (ht - 4) // 2 < JT:
            jt = (ht - 4) // 2
            if jt >= 1:
                sch.wait("gpsimd", "s_dqd", jt)
            dma("gpsimd", T["wdr"][:], wd[jt * P:(jt + 1) * P, :], "ld_wd",
                max_dma_last_dim=8192)
            sch.wait("scalar", "ld_wd", 16 * (jt + 1))
            if jt - 1 in wdtp_h:
                sch.wait("scalar", "y_tp", wdtp_h[jt - 1])
            act("scalar", T["wdd"][:], T["wdr"][:], AF.Identity, sem="s_dqd",
                scale=sd_sb[:, jt:jt + 1], bias=sdb_sb[:, jt:jt + 1])
            sch.wait("sync", "s_dqd", sch.cnt["s_dqd"])
            if jt >= 1:
                sch.wait("sync", "st_wdt", 16 * jt)
            wdtp_h[jt] = tpose(T["wdTt"][:], T["wdd"][:])
            sch.wait("gpsimd", "y_tp", wdtp_h[jt])
            dma("gpsimd", wdt_hbm[jt, :, :], T["wdTt"][:], "st_wdt")
    sch.wait("gpsimd", "st_act", 16 * HT)
    sch.wait("gpsimd", "st_wdt", 16 * JT)
    phase_barrier()

    detp_h = {}
    dn_after = {}
    wdtb_cnt = {}
    for hh in range(2):
        for jt0 in range(3):
            g0 = hh * JT + jt0
            if g0 - 3 in dn_after:
                sch.wait("gpsimd", "t_dn", dn_after[g0 - 3])
            dma("gpsimd", T[f"wdTb{g0 % 3}"][:], wdt_hbm[jt0, :, :],
                f"ld_wdt{g0 % 3}")
            wdtb_cnt[g0] = sch.cnt[f"ld_wdt{g0 % 3}"]
        if hh == 1:
            sch.wait("gpsimd", "s_de", sch.cnt["s_de"])
        lact_cnt = {}
        for ht in range(HT):
            dma("gpsimd", T["actH"][:, ht, :],
                act_hbm[ht, :, hh * 1024:(hh + 1) * 1024], f"ld_act{ht % 4}")
            lact_cnt[ht] = sch.cnt[f"ld_act{ht % 4}"]
        if hh == 1:
            sch.wait("gpsimd", "st_out", 16 * 8)
            sch.wait("gpsimd", "v_oadd", sch.cnt["v_oadd"])
        for sl in range(8):
            dma("gpsimd", T["out_sb"][:, sl, :], x1_hbm[hh * 8 + sl, :, :],
                f"ld_out{sl % 2}")
        for jt in range(JT):
            g = hh * JT + jt
            pa = g % 2
            pw = g % 3
            if jt >= 3:
                sch.wait("gpsimd", "t_dn", dn_after[g - 3])
                dma("gpsimd", T[f"wdTb{pw}"][:], wdt_hbm[jt, :, :],
                    f"ld_wdt{pw}")
                wdtb_cnt[g] = sch.cnt[f"ld_wdt{pw}"]
            sch.wait("tensor", f"ld_wdt{pw}", wdtb_cnt[g])
            if jt > 0 or not E_INCR:
                for q4 in range(4):
                    sch.wait("tensor", f"ld_act{q4}", sch.cnt[f"ld_act{q4}"])
            if g >= 2:
                sch.wait("tensor", "s_de", g - 1)
            for ht in range(HT):
                if jt == 0 and E_INCR:
                    sch.wait("tensor", f"ld_act{ht % 4}", lact_cnt[ht])
                for ci in range(2):
                    mm(psum[:, pa * 1024 + ci * 512:pa * 1024 + ci * 512 + 512],
                       T[f"wdTb{pw}"][:, ht, :],
                       T["actH"][:, ht, ci * 512:(ci + 1) * 512],
                       ht == 0, ht == HT - 1,
                       sem="t_dn" if (ht == HT - 1 and ci == 1) else None)
            dn_after[g] = sch.cnt["t_dn"]
            sch.wait("scalar", "t_dn", sch.cnt["t_dn"])
            if g - 2 in detp_h:
                sch.wait("scalar", "y_tp", detp_h[g - 2])
            act("scalar", T[f"dTt{pa}"][:], psum[:, pa * 1024:pa * 1024 + 1024],
                AF.Identity, sem="s_de")
            sch.wait("sync", "s_de", sch.cnt["s_de"])
            if g >= 2:
                sch.wait("sync", "v_oadd", 8 * (g - 1))
            detp_h[g] = tpose(T[f"dnt{pa}"][:], T[f"dTt{pa}"][:])
            sch.wait("vector", "y_tp", detp_h[g])
            sch.wait("vector", "ld_out0", sch.cnt["ld_out0"])
            sch.wait("vector", "ld_out1", sch.cnt["ld_out1"])
            for nt in range(8):
                o = T["out_sb"][:, nt, jt * P:jt * P + P]
                def f_add(E, o=o, i1=T[f"dnt{pa}"][:, nt, :]):
                    return E.tensor_add(o, o, i1)
                sch.op("vector", f_add, "v_oadd", 1)
        sch.wait("gpsimd", "v_oadd", sch.cnt["v_oadd"])
        for sl in range(8):
            dma("gpsimd", out_h[hh * 8 + sl, :, :], T["out_sb"][:, sl, :], "st_out")
    sch.wait("gpsimd", "st_out", 16 * 16)

    global _LAST_SCHED
    _LAST_SCHED = sch
    sems = {}
    for name in sch.sem_names:
        sems[name] = e(nc.semaphore(name))
    block = e(nc.Block())

    def emit(E, stream):
        for item in stream:
            if item[0] == "w":
                E.wait_ge(sems[item[1]], item[2])
            else:
                _, fn, inc_sem, inc_amt = item
                inst = fn(E)
                if inc_sem is not None and inst is not None:
                    inst.then_inc(sems[inc_sem], inc_amt)

    @block.tensor
    def _(t):
        emit(t, sch.streams["tensor"])

    @block.scalar
    def _(sc):
        emit(sc, sch.streams["scalar"])

    @block.vector
    def _(v):
        emit(v, sch.streams["vector"])

    @block.sync
    def _(sy):
        emit(sy, sch.streams["sync"])

    @block.gpsimd
    def _(g):
        emit(g, sch.streams["gpsimd"])

    ctx.close()
    return nc


_LAST_SCHED = None
_NC_CACHE = None


def _get_nc():
    global _NC_CACHE
    if _NC_CACHE is None:
        _NC_CACHE = build_program()
    return _NC_CACHE


def kernel(L0_attn_x, L0_attn_W_qkv, L0_attn_b_qkv, L0_attn_W_proj, L0_attn_b_proj,
           L0_ffn_wq_gate, L0_ffn_s_gate, L0_ffn_wq_up, L0_ffn_s_up,
           L0_ffn_wq_down, L0_ffn_s_down):
    x = np.ascontiguousarray(np.asarray(L0_attn_x, dtype=np.float32))
    wqkv = np.ascontiguousarray(np.asarray(L0_attn_W_qkv, dtype=np.float32))
    bq = np.asarray(L0_attn_b_qkv, dtype=np.float32)
    wpn = np.ascontiguousarray(np.asarray(L0_attn_W_proj, dtype=np.float32))
    bp = np.asarray(L0_attn_b_proj, dtype=np.float32)
    wgn = np.ascontiguousarray(np.asarray(L0_ffn_wq_gate, dtype=np.int32))
    sgn = np.asarray(L0_ffn_s_gate, dtype=np.float32)
    wun = np.ascontiguousarray(np.asarray(L0_ffn_wq_up, dtype=np.int32))
    sun = np.asarray(L0_ffn_s_up, dtype=np.float32)
    wdn = np.ascontiguousarray(np.asarray(L0_ffn_wq_down, dtype=np.int32))
    sdn = np.asarray(L0_ffn_s_down, dtype=np.float32)

    bqkv_t = np.ascontiguousarray(bq.reshape(JQKV, P).T)
    bproj_t = np.ascontiguousarray(np.tile(bp[None, :], (P, 1)))
    sg_t = np.ascontiguousarray(sgn.reshape(HT, P).T)
    su_t = np.ascontiguousarray(sun.reshape(HT, P).T)
    sd_t = np.ascontiguousarray(sdn.reshape(JT, P).T)

    nc = _get_nc()
    in_maps = []
    metas = []
    for c in range(8):
        b = c // 2
        par = c % 2
        gl = [2 * i + par for i in range(NSLOT)]
        xb = x[b]
        xq_c = np.ascontiguousarray(xb.reshape(RT, P, D)[gl])
        tri = (np.arange(128)[None, :] <= np.arange(P)[:, None]).astype(np.float32)
        cm = np.zeros((P, 256), dtype=np.float32)
        if par == 0:
            cm[:, 0:128] = tri
        else:
            cm[:, 0:128] = 1.0
            cm[:, 128:256] = tri
        cmb = cm.astype(ml_dtypes.bfloat16)
        in_maps.append({
            "x_att": xb, "xq": xq_c, "wqkv": wqkv, "bqkv": bqkv_t,
            "wp": wpn, "bproj": bproj_t, "wg": wgn, "wu": wun, "wd": wdn,
            "sg": sg_t, "su": su_t, "sd": sd_t, "cmask": cmb,
        })
        metas.append((b, gl))

    res = run_bass_kernel_spmd(nc, in_maps, core_ids=list(range(8)))

    out = np.empty((B, S, D), dtype=np.float32)
    for c in range(8):
        b, gl = metas[c]
        oc = np.asarray(res.results[c]["out"])
        for i, g in enumerate(gl):
            out[b, g * P:(g + 1) * P, :] = oc[i]
    return out
